# version 01
# baseline (speedup 1.0000x reference)
import numpy as np
import concourse.bass as bass
import concourse.mybir as mybir
from concourse.bass_utils import run_bass_kernel_spmd

F32 = mybir.dt.float32
BF16 = mybir.dt.bfloat16
AF = mybir.ActivationFunctionType
ALU = mybir.AluOpType

D = 1024
KT = 8
NB = 2
N = 2048
NP = N + 1
H = 16
HD = 64
DEPTH = 6
DI = 2730
FT1 = 22
WIN = 256
SEG = 512
PMEM = 16

B_ATTA = 4.0
B_LOC = 24.0
B_ATTB = 8.0
LN_EPS = 1e-5
RMS_EPS = 1.1920929e-07

UPTO = 4
FFN_DEPTH = DEPTH
FFN_ATTN_ONLY = False
DEBUG2 = False
DEBUG = False



def _host_tables():
    t = np.arange(N, dtype=np.float64)
    half = np.arange(0, HD, 2, dtype=np.float64)
    inv_freq = 1.0 / 10000.0 ** (half / HD)
    freqs = np.concatenate([t[:, None] * inv_freq[None, :]] * 2, -1)
    base_scale = (half + 0.4 * HD) / (1.4 * HD)
    power = (t - N // 2) / (WIN // 2)
    scale = base_scale[None, :] ** power[:, None]
    scale = np.concatenate([scale, scale], -1)
    cos, sin = np.cos(freqs), np.sin(freqs)
    xpos = np.stack([np.tile(a.T, (2, 1)) for a in
                     (cos * scale, sin * scale, cos / scale, sin / scale)])

    fb = np.repeat(t[:, None] * inv_freq[None, :], 2, axis=-1)
    ropeb = np.stack([np.tile(np.cos(fb).T, (2, 1)),
                      np.tile(np.sin(fb).T, (2, 1))])

    PH = np.zeros((128, 128))
    PI = np.zeros((128, 128))
    for blk in (0, 64):
        for m in range(32):
            PH[blk + m + 32, blk + m] = -1.0
            PH[blk + m, blk + m + 32] = 1.0
        for i in range(32):
            PI[blk + 2 * i + 1, blk + 2 * i] = -1.0
            PI[blk + 2 * i, blk + 2 * i + 1] = 1.0
    SHIFT64 = np.zeros((128, 128))
    for d_ in range(64):
        SHIFT64[d_, 64 + d_] = 1.0

    p_ = np.arange(128)[:, None]
    f_ = np.arange(128)[None, :]
    tri_causal = (p_ <= f_).astype(np.float64)
    tri_anti = (p_ >= f_).astype(np.float64)
    return dict(xpos=xpos, ropeb=ropeb, PH=PH, PI=PI, SHIFT64=SHIFT64,
                tri_causal=tri_causal, tri_anti=tri_anti)


C_PH = 0
C_PI = 128
C_SHIFT = 256
C_TRIC = 384
C_TRIA = 512
C_ONES = 640
C_ONES2 = 768
C_SEL2 = 772
CC_TOT = 900


def _consts_array():
    tb = _host_tables()
    c = np.zeros((128, CC_TOT))
    c[:, C_PH:C_PH + 128] = tb["PH"]
    c[:, C_PI:C_PI + 128] = tb["PI"]
    c[:, C_SHIFT:C_SHIFT + 128] = tb["SHIFT64"]
    c[:, C_TRIC:C_TRIC + 128] = tb["tri_causal"]
    c[:, C_TRIA:C_TRIA + 128] = tb["tri_anti"]
    c[:, C_ONES:C_ONES + 128] = 1.0
    c[0:64, C_ONES2] = 1.0
    c[64:128, C_ONES2 + 1] = 1.0
    c[64, C_SEL2:C_SEL2 + 64] = 1.0
    c[65, C_SEL2 + 64:C_SEL2 + 128] = 1.0
    return c.astype(np.float32), tb


def _pmajor(w, jc):
    ind, outd = w.shape
    return np.ascontiguousarray(
        w.reshape(ind // 128, 128, outd // jc, jc).transpose(2, 1, 0, 3))


def host_prep(inputs, n_cores=8):
    f32 = np.float32
    x = np.asarray(inputs["L0_attn_a_x"], f32)

    WqT = _pmajor(np.asarray(inputs["L0_attn_a_Wq"], f32).T, 128)
    WkT = _pmajor(np.asarray(inputs["L0_attn_a_Wk"], f32).T, 128)
    WvT = _pmajor(np.asarray(inputs["L0_attn_a_Wv"], f32).T, 256)
    WoT = _pmajor(np.asarray(inputs["L0_attn_a_Wo"], f32).T, 128)
    b_a = np.zeros((128, 4 * KT), f32)
    for i, nm in enumerate(["bq", "bk", "bv", "bo"]):
        b_a[:, i * KT:(i + 1) * KT] = (
            np.asarray(inputs[f"L0_attn_a_{nm}"], f32).reshape(KT, 128).T)
    bv_row = np.asarray(inputs["L0_attn_a_bv"], f32).reshape(1, D)

    wqkvT = _pmajor(np.asarray(inputs["L0_attn_b_w_qkv"], f32).T, 512)
    woutT = _pmajor(np.asarray(inputs["L0_attn_b_w_out"], f32).T, 128)
    g_b = np.asarray(inputs["L0_attn_b_g"], f32).reshape(KT, 128).T
    pm = np.asarray(inputs["L0_attn_b_pm"], f32)
    pmk = np.zeros((128, KT, PMEM), f32)
    pmv = np.zeros((16, H, HD + 1), f32)
    for h in range(H):
        pmk[(h % 2) * 64:(h % 2) * 64 + 64, h // 2, :] = pm[0, h].T
        pmv[:, h, 0:HD] = pm[1, h]
    pmv[:, :, HD] = 1.0

    Wqkv = np.stack([_pmajor(np.asarray(inputs["L0_ffn_Wqkv"], f32)[l], 512)
                     for l in range(DEPTH)])
    Wo_f = np.stack([
        _pmajor(np.asarray(inputs["L0_ffn_Wo"], f32)[l], 128)
        .transpose(1, 2, 0, 3).reshape(128, KT, D)
        for l in range(DEPTH)])
    W1 = np.asarray(inputs["L0_ffn_W1"], f32)
    W1p = np.zeros((DEPTH, FT1, 128, 2, KT, 128), f32)
    for l in range(DEPTH):
        for hi in range(2):
            w = np.zeros((D, FT1 * 128), f32)
            w[:, :DI] = W1[l][:, hi * DI:(hi + 1) * DI]
            W1p[l, :, :, hi] = (
                w.reshape(KT, 128, FT1, 128).transpose(2, 1, 0, 3))
    W2 = np.asarray(inputs["L0_ffn_W2"], f32)
    W2p = np.zeros((DEPTH, KT, 128, FT1, 128), f32)
    for l in range(DEPTH):
        w2 = np.zeros((FT1 * 128, D), f32)
        w2[:DI] = W2[l]
        W2p[l] = w2.reshape(FT1, 128, KT, 128).transpose(2, 1, 0, 3)

    small = np.zeros((DEPTH, 128, 70), f32)
    qs = np.asarray(inputs["L0_ffn_q_scale"], f32)
    ks = np.asarray(inputs["L0_ffn_k_scale"], f32)
    b1 = np.asarray(inputs["L0_ffn_b1"], f32)
    for l in range(DEPTH):
        small[l, :, 0] = np.tile(qs[l], 2)
        small[l, :, 1] = np.tile(ks[l], 2)
        small[l, :, 2:10] = np.asarray(
            inputs["L0_ffn_ln_g"], f32)[l].reshape(KT, 128).T
        small[l, :, 10:18] = np.asarray(
            inputs["L0_ffn_ln_b"], f32)[l].reshape(KT, 128).T
        b1a = np.zeros(FT1 * 128, f32)
        b1g = np.zeros(FT1 * 128, f32)
        b1a[:DI] = b1[l][:DI]
        b1g[:DI] = b1[l][DI:]
        small[l, :, 18:40] = b1a.reshape(FT1, 128).T
        small[l, :, 40:62] = b1g.reshape(FT1, 128).T
        small[l, :, 62:70] = np.asarray(
            inputs["L0_ffn_b2"], f32)[l].reshape(KT, 128).T

    consts, tb = _consts_array()
    xpos_t = tb["xpos"].astype(f32)
    ropeb_t = tb["ropeb"].astype(f32)

    Wqkv = Wqkv.reshape(DEPTH * 6, 128, KT * 512)
    W1p = W1p.reshape(DEPTH * FT1, 128, 2 * KT * 128)
    W2p = W2p.reshape(DEPTH * KT, 128, FT1 * 128)
    in_maps = []
    for c in range(n_cores):
        xc = x[NB * c:NB * c + NB]
        xfm = np.ascontiguousarray(
            xc.transpose(2, 0, 1).reshape(KT, 128, NB, N).transpose(1, 0, 2, 3))
        in_maps.append(dict(
            x_in=xfm, WqT=WqT, WkT=WkT, WvT=WvT, WoT=WoT, b_a=b_a,
            bv_row=bv_row, wqkvT=wqkvT, woutT=woutT, g_b=g_b, pmk=pmk,
            pmv=pmv, Wqkv=Wqkv, Wo_f=Wo_f, W1p=W1p, W2p=W2p, small=small,
            consts=consts, xpos=xpos_t, ropeb=ropeb_t))
    return in_maps



class Sem:
    def __init__(self, stack, nc, name):
        self.h = stack.enter_context(nc.semaphore(name))
        self.n = 0

    def inc(self, inst, k=1):
        inst.then_inc(self.h, k)
        self.n += k
        return inst

    def wait(self, eng, val=None):
        eng.wait_ge(self.h, self.n if val is None else val)


class B:
    pass


def build(upto=UPTO):
    import contextlib
    st = contextlib.ExitStack()
    nc = bass.Bass("TRN2", target_bir_lowering=False)
    b = B()
    b.nc = nc
    b.TE, b.VE, b.SE, b.GE, b.SY = (
        nc.tensor, nc.vector, nc.scalar, nc.gpsimd, nc.sync)

    dt = nc.dram_tensor
    dr = {}
    def di(name, shape):
        dr[name] = dt(name, shape, F32, kind="ExternalInput").ap()
    di("x_in", [128, KT, NB, N])
    di("WqT", [KT, 128, KT, 128])
    di("WkT", [KT, 128, KT, 128])
    di("WvT", [4, 128, KT, 256])
    di("WoT", [KT, 128, KT, 128])
    di("b_a", [128, 4 * KT])
    di("bv_row", [1, D])
    di("wqkvT", [6, 128, KT, 512])
    di("woutT", [KT, 128, KT, 128])
    di("g_b", [128, KT])
    di("pmk", [128, KT, PMEM])
    di("pmv", [16, H, HD + 1])
    di("Wqkv", [DEPTH * 6, 128, KT * 512])
    di("Wo_f", [DEPTH, 128, KT, D])
    di("W1p", [DEPTH * FT1, 128, 2 * KT * 128])
    di("W2p", [DEPTH * KT, 128, FT1 * 128])
    di("small", [DEPTH, 128, 70])
    di("consts", [128, CC_TOT])
    di("xpos", [4, 128, N])
    di("ropeb", [2, 128, N])
    dr["x_out"] = dt("x_out", [128, KT, NB, N], F32, kind="ExternalOutput").ap()
    if DEBUG2:
        for nm, shp in (("e_y", [128, KT * 1024]), ("e_q", [128, KT * 1024]),
                        ("e_k", [128, KT * 1280]), ("e_v", [128, 10 * H * (HD + 1)]),
                        ("e_o", [128, KT * 1024])):
            dr[nm] = dt(nm, shp, BF16, kind="ExternalOutput").ap()
    if DEBUG:
        for nm, shp in (("d_u", [128, KT, N]), ("d_q", [128, KT, N]),
                        ("d_k", [128, KT, N]), ("d_v", [128, 16, H, HD + 1]),
                        ("d_o", [128, KT, N])):
            dr[nm] = dt(nm, shp, BF16, kind="ExternalOutput").ap()
    dr["x_save"] = dt("x_save", [NB, 128, KT, N], F32, kind="Internal").ap()
    dr["h_sc"] = nc.dram_tensor("h_sc", [128, FT1, N], BF16, kind="Internal").ap()
    b.dr = dr

    xs = st.enter_context(nc.sbuf_tensor("xs", [128, NB, KT, NP], F32))
    WB = 74240
    b.WB = WB
    arena = st.enter_context(nc.sbuf_tensor("arena", [128, WB], mybir.dt.uint8))
    onesf = st.enter_context(nc.sbuf_tensor("onesf", [128, 128], F32))
    cst_b = st.enter_context(nc.sbuf_tensor("cst_b", [128, CC_TOT], BF16))
    smalls = st.enter_context(nc.sbuf_tensor("smalls", [128, 70], F32))
    ba_sb = st.enter_context(nc.sbuf_tensor("ba_sb", [128, 4 * KT], F32))
    gb_sb = st.enter_context(nc.sbuf_tensor("gb_sb", [128, KT], F32))
    bvb = st.enter_context(nc.sbuf_tensor("bvb", [1, D], BF16))
    pmk_b = st.enter_context(nc.sbuf_tensor("pmk_b", [128, KT, PMEM], BF16))
    pmv_b = st.enter_context(nc.sbuf_tensor("pmv_b", [16, H, HD + 1], BF16))
    cbias = st.enter_context(nc.sbuf_tensor("cbias", [128, 8], F32))
    ps = st.enter_context(nc.psum_tensor("ps", [128, 8, 512], F32))
    b.xs, b.arena, b.onesf, b.cst_b, b.smalls, b.ps = (
        xs, arena, onesf, cst_b, smalls, ps)
    b.ba_sb, b.gb_sb, b.bvb, b.cbias = ba_sb, gb_sb, bvb, cbias
    b.pmk_b, b.pmv_b = pmk_b, pmv_b

    def view(off, dtype, *shape, p0=0, np_=128):
        esz = 4 if dtype == F32 else 2
        nby = int(np.prod(shape)) * esz
        assert off + nby <= WB, (off, nby, WB)
        ap = arena[p0:p0 + np_, off:off + nby].bitcast(dtype)
        if len(shape) > 1:
            names = " ".join(f"s{i}" for i in range(len(shape)))
            ap = ap.rearrange(f"p ({names}) -> p {names}",
                              **{f"s{i}": v for i, v in enumerate(shape)})
        return ap
    b.view = view

    b.dW = Sem(st, nc, "dW")
    b.dS = Sem(st, nc, "dS")
    b.mm = Sem(st, nc, "mm")
    b.ev = Sem(st, nc, "ev")
    b.es = Sem(st, nc, "es")
    b.ep = Sem(st, nc, "ep")
    b.bar = Sem(st, nc, "bar")

    b.c_PH = cst_b[:, C_PH:C_PH + 128]
    b.c_PI = cst_b[:, C_PI:C_PI + 128]
    b.c_SHIFT = cst_b[0:64, C_SHIFT:C_SHIFT + 128]
    b.c_tric = cst_b[:, C_TRIC:C_TRIC + 128]
    b.c_tria = cst_b[:, C_TRIA:C_TRIA + 128]
    b.c_ones = lambda p0, np_, w: cst_b[p0:p0 + np_, C_ONES:C_ONES + w]
    b.c_ones2 = cst_b[:, C_ONES2:C_ONES2 + 2]
    b.c_sel2 = cst_b[64:66, C_SEL2:C_SEL2 + 128]
    b.c_onesf = lambda p0, np_, w: onesf[p0:p0 + np_, 0:w]

    b.bg = {k: [] for k in range(8)}

    emit_prologue(b)
    if upto >= 1:
        emit_attn_a(b)
    if upto >= 2:
        emit_ffn6(b, 0)
    if upto >= 3:
        emit_attn_b(b)
    if upto >= 4:
        emit_ffn6(b, 1)
    emit_epilogue(b)
    st.close()
    return nc


def join_all(b, engines):
    for eng in engines:
        for sem in (b.mm, b.ev, b.es, b.ep, b.dW, b.dS):
            if sem.n:
                sem.wait(eng)


def bank_wait(b, bank):
    for sem, val in b.bg[bank]:
        sem.wait(b.TE, val)
    b.bg[bank] = []


def emit_prologue(b):
    SY, VE, GE = b.SY, b.VE, b.GE
    b.dW.inc(GE.dma_start(out=b.cst_b[:, :], in_=b.dr["consts"]), 16)
    b.dW.inc(GE.dma_start(out=b.bvb[:, :], in_=b.dr["bv_row"]), 16)
    b.dS.inc(SY.dma_start(out=b.ba_sb[:, :], in_=b.dr["b_a"]), 16)
    b.dS.inc(SY.dma_start(out=b.gb_sb[:, :], in_=b.dr["g_b"]), 16)
    for bb in range(NB):
        b.dS.inc(SY.dma_start(out=b.xs[:, bb, :, 1:NP],
                              in_=b.dr["x_in"][:, :, bb, :]), 16)
    b.dS.wait(VE)
    b.dW.wait(VE)
    VE.memset(b.onesf[:, :], 1.0)
    for i, val in enumerate([LN_EPS, RMS_EPS, -B_ATTA, -B_LOC, -B_ATTB,
                             1e-24, 0.0, 0.0]):
        VE.memset(b.cbias[:, i:i + 1], float(val))
    b.ev.inc(VE.memset(b.xs[:, :, :, 0:1], 0.0))


def emit_epilogue(b):
    SY = b.SY
    b.ev.wait(SY)
    b.es.wait(SY)
    b.mm.wait(SY)
    b.dW.wait(SY)
    b.ep.wait(SY)
    for bb in range(NB):
        b.dS.inc(SY.dma_start(out=b.dr["x_out"][:, :, bb, :],
                              in_=b.xs[:, bb, :, 1:NP]), 16)
    b.dS.wait(SY)



def ln_chunk(b, srcs, dsts, width, eps_col, lnb, gamma=None, beta=None,
             want_mean=True):
    TE, VE, SE = b.TE, b.VE, b.SE
    b.mm.wait(SE)
    for k in range(KT):
        i = SE.activation(lnb["sq"][:, k, 0:width], srcs[k], AF.Square)
    b.es.inc(i)
    bank_wait(b, 7)
    bank_wait(b, 6)
    if want_mean:
        for k in range(KT):
            TE.matmul(b.ps[0:1, 7, 0:width], b.c_onesf(0, 128, 1), srcs[k],
                      start=(k == 0), stop=(k == KT - 1))
    b.es.wait(TE)
    for k in range(KT):
        i = TE.matmul(b.ps[0:1, 6, 0:width], b.c_ones(0, 128, 1),
                      lnb["sq"][:, k, 0:width],
                      start=(k == 0), stop=(k == KT - 1))
    b.mm.inc(i)
    b.mm.wait(VE)
    ra, rb, rc, rsd, rrec = (lnb[n][0:1, 0:width]
                             for n in ("ra", "rb", "rc", "rsd", "rrec"))
    rstd, mrs = lnb["rstd"][0:1, 0:width], lnb["mrs"][0:1, 0:width]
    if want_mean:
        VE.tensor_scalar(ra, b.ps[0:1, 7, 0:width], 1.0 / D, None, ALU.mult)
        VE.tensor_tensor(rb, ra, ra, ALU.mult)
        VE.tensor_scalar(rc, b.ps[0:1, 6, 0:width], 1.0 / D, None, ALU.mult)
        i = VE.tensor_tensor(rc, rc, rb, ALU.subtract)
    else:
        i = VE.tensor_scalar(rc, b.ps[0:1, 6, 0:width], 1.0 / D, None,
                             ALU.mult)
    b.ev.inc(i)
    b.ev.wait(SE)
    b.es.inc(SE.activation(rsd, rc, AF.Sqrt,
                           bias=b.cbias[0:1, eps_col:eps_col + 1]))
    b.es.wait(VE)
    i = VE.reciprocal(rrec, rsd)
    if want_mean:
        i = VE.tensor_tensor(rb, ra, rrec, ALU.mult)
    b.ev.inc(i)
    b.bg[7] = [(b.ev, b.ev.n)]
    b.bg[6] = [(b.ev, b.ev.n)]
    bank_wait(b, 5)
    bank_wait(b, 4)
    b.ev.wait(TE)
    i = TE.matmul(b.ps[:, 5, 0:width], b.c_onesf(0, 1, 128), rrec,
                  start=True, stop=True)
    if want_mean:
        i = TE.matmul(b.ps[:, 4, 0:width], b.c_onesf(0, 1, 128), rb,
                      start=True, stop=True)
    b.mm.inc(i)
    b.mm.wait(VE)
    for k in range(KT):
        t1 = lnb["t1"][:, 0:width]
        VE.tensor_tensor(t1, srcs[k], b.ps[:, 5, 0:width], ALU.mult)
        if want_mean:
            if gamma is None:
                i = VE.tensor_tensor(dsts[k], t1, b.ps[:, 4, 0:width],
                                     ALU.subtract)
            else:
                t2 = lnb["t2"][:, 0:width]
                VE.tensor_tensor(t2, t1, b.ps[:, 4, 0:width], ALU.subtract)
                i = VE.tensor_scalar(dsts[k], t2, gamma[:, k:k + 1],
                                     beta[:, k:k + 1], ALU.mult, ALU.add)
        else:
            if gamma is None:
                i = VE.tensor_copy(dsts[k], t1)
            else:
                i = VE.tensor_scalar(dsts[k], t1, gamma[:, k:k + 1], None,
                                     ALU.mult)
    b.ev.inc(i)
    b.bg[5] = [(b.ev, b.ev.n)]
    b.bg[4] = [(b.ev, b.ev.n)]


def mk_lnb(b, sq_off, rows_off, t_off, width=512, viewfn=None):
    v = viewfn if viewfn is not None else b.view
    return dict(
        sq=v(sq_off, BF16, KT, width),
        ra=v(rows_off, F32, width, p0=0, np_=1),
        rb=v(rows_off + 2048, F32, width, p0=0, np_=1),
        rc=v(rows_off + 4096, F32, width, p0=0, np_=1),
        rsd=v(rows_off + 6144, F32, width, p0=0, np_=1),
        rrec=v(rows_off + 8192, F32, width, p0=0, np_=1),
        rstd=v(rows_off + 10240, BF16, width, p0=0, np_=1),
        mrs=v(rows_off + 10240 + width * 2, BF16, width, p0=0, np_=1),
        t1=v(t_off, F32, width),
        t2=v(t_off + width * 4, F32, width),
    )



def emit_attn_a(b):
    TE, VE, SE, GE, SY = b.TE, b.VE, b.SE, b.GE, b.SY
    ps, v = b.ps, b.view
    dr = b.dr

    u = v(0, BF16, KT, N)
    ocat = v(0, BF16, KT, N)
    v65 = v(32768, BF16, 16, H, HD + 1)
    lnb = mk_lnb(b, 32768, 45056, 40960)
    qkpan = v(66048, BF16, 2, KT, 128)
    vpan = v(70144, BF16, KT, 256)
    abuf = v(66048, BF16, 2, 1024)
    otmp_e = v(70144, BF16, 512)
    otmp_o = v(70144 + 1024, BF16, 512)
    zinv_e = v(72192, BF16, 512, p0=64, np_=1)
    zinv_o = v(72192 + 1024, BF16, 512, p0=64, np_=1)
    wopan = v(66048, BF16, 2, KT, 128)

    for eng in (TE, VE, SE, GE):
        b.dS.wait(eng)
        b.dW.wait(eng)
        if eng is not VE:
            b.ev.wait(eng)

    for bb in range(NB):
        join_all(b, (TE, VE, SE, GE, SY))
        xb_bf = b.xs[:, bb].rearrange("p k t -> p (k t)").bitcast(
            BF16).rearrange("p (k t) -> p k t", k=KT, t=2 * NP)
        QC0, KC0 = 0, 2050

        b.dS.inc(SY.dma_start(out=dr["x_save"][bb],
                              in_=b.xs[:, bb, :, 1:NP]), 16)
        spill_done = b.dS.n

        b.ev.wait(SE)
        for tc in range(4):
            srcs = [b.xs[:, bb, k, 1 + tc * 512:1 + (tc + 1) * 512]
                    for k in range(KT)]
            dsts = [u[:, k, tc * 512:(tc + 1) * 512] for k in range(KT)]
            ln_chunk(b, srcs, dsts, 512, 0, lnb)
        u_done = b.ev.n

        b.mm.wait(VE)
        b.ev.inc(VE.memset(v65[:, :, :, HD:HD + 1], 1.0))
        ones_done = b.ev.n

        b.ev.wait(SE, u_done)
        pan_free = [(None, 0), (None, 0)]
        for w01, wnm in ((0, "WqT"), (1, "WkT")):
            for jt in range(KT):
                slot = (w01 * KT + jt) % 2
                g = pan_free[slot]
                if g[0] is not None:
                    g[0].wait(GE, g[1])
                b.dW.inc(GE.dma_start(out=qkpan[:, slot], in_=dr[wnm][jt]), 16)
                for tc in range(4):
                    bank = tc % 4
                    bank_wait(b, bank)
                    if tc == 0:
                        b.dW.wait(TE)
                        b.ev.wait(TE, u_done)
                        if w01 == 0 and jt == 0:
                            b.dS.wait(TE, spill_done)
                    for k in range(KT):
                        i = TE.matmul(ps[:, bank, :], qkpan[:, slot, k, :],
                                      u[:, k, tc * 512:(tc + 1) * 512],
                                      start=(k == 0), stop=(k == KT - 1))
                    b.mm.inc(i)
                pan_free[slot] = (b.mm, b.mm.n)
                b.mm.wait(SE)
                b.dS.wait(SE, spill_done)
                base = QC0 if w01 == 0 else KC0
                for tc in range(4):
                    i = SE.activation(
                        xb_bf[:, jt, base + tc * 512:base + (tc + 1) * 512],
                        ps[:, tc % 4, :], AF.Identity,
                        bias=b.ba_sb[:, w01 * KT + jt:w01 * KT + jt + 1])
                    b.es.inc(i)
                for bank in range(4):
                    b.bg[bank] = [(b.es, b.es.n)]
        vp_guard = (None, 0)
        for jc in range(4):
            if vp_guard[0] is not None:
                vp_guard[0].wait(GE, vp_guard[1])
            b.dW.inc(GE.dma_start(out=vpan[:, :], in_=dr["WvT"][jc]), 16)
            b.dW.wait(TE)
            for tt in range(16):
                bank = tt % 4
                bank_wait(b, bank)
                for k in range(KT):
                    TE.matmul(ps[:, bank, 0:256], u[:, k, tt * 128:(tt + 1) * 128],
                              vpan[:, k, :], start=(k == 0), stop=False)
                i = TE.matmul(ps[:, bank, 0:256], b.c_ones(0, 1, 128),
                              b.bvb[0:1, jc * 256:(jc + 1) * 256],
                              start=False, stop=True)
                b.mm.inc(i)
                b.mm.wait(SE)
                i = SE.copy(v65[:, tt, 4 * jc:4 * jc + 4, 0:HD],
                            ps[:, bank, 0:256])
                b.es.inc(i)
                b.bg[bank] = [(b.es, b.es.n)]
            vp_guard = (b.mm, b.mm.n)

        if DEBUG and bb == 0:
            b.ev.wait(SY, u_done)
            b.es.wait(SY)
            b.dS.inc(SY.dma_start(out=dr["d_u"], in_=u[:, :, :]), 16)
            b.dS.inc(SY.dma_start(
                out=dr["d_q"],
                in_=xb_bf[:, :, QC0:QC0 + N]), 16)
            b.dS.inc(SY.dma_start(
                out=dr["d_k"],
                in_=xb_bf[:, :, KC0:KC0 + N]), 16)
            b.dS.inc(SY.dma_start(out=dr["d_v"], in_=v65[:, :, :, :]), 16)
            dump_done = b.dS.n
            b.dS.wait(TE, dump_done)

        b.ev.wait(TE, ones_done)
        for hp in range(KT):
            for ncx in range(4):
                ncs = slice(ncx * 512, (ncx + 1) * 512)
                aslot_guard = [(None, 0), (None, 0)]
                for mt in range(16):
                    for bk in (0, 1):
                        bank_wait(b, bk)
                    ms = slice(mt * 128, (mt + 1) * 128)
                    i0 = TE.matmul(ps[:, 0, :],
                                   xb_bf[0:64, hp, QC0 + mt * 128:QC0 + (mt + 1) * 128],
                                   xb_bf[0:64, hp, KC0 + ncx * 512:KC0 + (ncx + 1) * 512],
                                   start=True, stop=True)
                    i1 = TE.matmul(ps[:, 1, :],
                                   xb_bf[64:128, hp, QC0 + mt * 128:QC0 + (mt + 1) * 128],
                                   xb_bf[64:128, hp, KC0 + ncx * 512:KC0 + (ncx + 1) * 512],
                                   start=True, stop=True)
                    b.mm.inc(i1)
                    b.mm.wait(SE)
                    g = aslot_guard[mt % 2]
                    if g[0] is not None:
                        g[0].wait(SE, g[1])
                    i = SE.activation(abuf[:, mt % 2], ps[:, 0:2, :], AF.Exp,
                                      scale=1.0 / 32.0,
                                      bias=b.cbias[:, 2:3])
                    b.es.inc(i)
                    b.bg[0] = [(b.es, b.es.n)]
                    b.bg[1] = [(b.es, b.es.n)]
                    if mt == 0:
                        bank_wait(b, 2)
                        bank_wait(b, 3)
                    b.es.wait(TE)
                    TE.matmul(ps[0:65, 2, :], v65[:, mt, 2 * hp, :],
                              abuf[:, mt % 2, 0:512],
                              start=(mt == 0), stop=(mt == 15))
                    i = TE.matmul(ps[0:65, 3, :], v65[:, mt, 2 * hp + 1, :],
                                  abuf[:, mt % 2, 512:1024],
                                  start=(mt == 0), stop=(mt == 15))
                    b.mm.inc(i)
                    aslot_guard[mt % 2] = (b.mm, b.mm.n)
                b.mm.wait(VE)
                with b.nc.allow_low_precision(
                        reason="zinv feeds a bf16 matmul operand anyway"):
                    VE.reciprocal(zinv_e[0:1, :], ps[64:65, 2, :])
                    b.ev.inc(VE.reciprocal(zinv_o[0:1, :], ps[64:65, 3, :]))
                bank_wait(b, 4)
                bank_wait(b, 5)
                b.ev.wait(TE)
                TE.matmul(ps[0:64, 4, :], b.c_ones(64, 1, 64), zinv_e[0:1, :],
                          start=True, stop=True)
                i = TE.matmul(ps[0:64, 5, :], b.c_ones(64, 1, 64),
                              zinv_o[0:1, :], start=True, stop=True)
                b.mm.inc(i)
                b.mm.wait(SE)
                SE.copy(otmp_e[0:64, :], ps[0:64, 2, :])
                i = SE.copy(otmp_o[0:64, :], ps[0:64, 3, :])
                b.es.inc(i)
                b.bg[2] = [(b.es, b.es.n)]
                b.bg[3] = [(b.es, b.es.n)]
                b.es.wait(VE)
                VE.tensor_tensor(ocat[0:64, hp, ncs], otmp_e[0:64, :],
                                 ps[0:64, 4, :], ALU.mult)
                i = VE.tensor_tensor(abuf[0:64, 0, 0:512], otmp_o[0:64, :],
                                     ps[0:64, 5, :], ALU.mult)
                b.ev.inc(i)
                b.bg[4] = [(b.ev, b.ev.n)]
                b.bg[5] = [(b.ev, b.ev.n)]
                bank_wait(b, 6)
                b.ev.wait(TE)
                i = TE.matmul(ps[:, 6, :], b.c_SHIFT, abuf[0:64, 0, 0:512],
                              start=True, stop=True)
                b.mm.inc(i)
                b.mm.wait(SE)
                i = SE.copy(ocat[64:128, hp, ncs], ps[64:128, 6, :])
                b.es.inc(i)
                b.bg[6] = [(b.es, b.es.n)]
                aslot_guard[0] = (b.mm, b.mm.n)

        if DEBUG and bb == 0:
            b.es.wait(SY)
            b.dS.inc(SY.dma_start(out=dr["d_o"], in_=ocat[:, :, :]), 16)
            b.dS.wait(TE)

        ocat_done = b.es.n
        b.es.wait(GE)
        b.ev.wait(GE)
        wo_guard = [(None, 0), (None, 0)]
        for ft in range(KT):
            slot = ft % 2
            g = wo_guard[slot]
            if g[0] is not None:
                g[0].wait(GE, g[1])
            b.dW.inc(GE.dma_start(out=wopan[:, slot], in_=dr["WoT"][ft]), 16)
            b.dW.wait(TE)
            if ft == 0:
                b.es.wait(TE, ocat_done)
            for tc in range(4):
                bank = tc % 4
                bank_wait(b, bank)
                for jt in range(KT):
                    i = TE.matmul(ps[:, bank, :], wopan[:, slot, jt, :],
                                  ocat[:, jt, tc * 512:(tc + 1) * 512],
                                  start=(jt == 0), stop=(jt == KT - 1))
                b.mm.inc(i)
            wo_guard[slot] = (b.mm, b.mm.n)
            b.mm.wait(SE)
            for tc in range(4):
                i = SE.activation(
                    b.xs[:, bb, ft, 1 + tc * 512:1 + (tc + 1) * 512],
                    ps[:, tc % 4, :], AF.Identity,
                    bias=b.ba_sb[:, 3 * KT + ft:3 * KT + ft + 1])
                b.es.inc(i)
                b.bg[tc % 4] = [(b.es, b.es.n)]
        b.es.wait(GE)
        b.dW.inc(GE.dma_start(out=b.xs[:, bb, :, 1:NP], in_=dr["x_save"][bb],
                              accum_op=ALU.add), 16)
        b.dW.wait(GE)


def emit_ffn6(b, inv):
    TE, VE, SE, GE, SY = b.TE, b.VE, b.SE, b.GE, b.SY
    ps, v = b.ps, b.view
    dr = b.dr
    sm = b.smalls

    join_all(b, (TE, VE, SE, GE, SY))
    wq_fl, w1_fl, w2_fl = dr["Wqkv"], dr["W1p"], dr["W2p"]
    for bb in range(NB):
        b.dS.inc(SY.dma_start(out=dr["x_save"][bb],
                              in_=b.xs[:, bb, :, 1:NP]), 16)

    for bb in range(NB):
        join_all(b, (TE, VE, SE, GE, SY))

        def rv(off, dtype, *shape, p0=0, np_=128):
            esz = 4 if dtype == F32 else 2
            nby = int(np.prod(shape)) * esz
            assert off + nby <= KT * NP * 4, (off, nby)
            ap = b.xs[p0:p0 + np_, 1 - bb].rearrange(
                "p k t -> p (k t)").bitcast(mybir.dt.uint8)
            ap = ap[:, off:off + nby].bitcast(dtype)
            if len(shape) > 1:
                names = " ".join(f"s{i}" for i in range(len(shape)))
                ap = ap.rearrange(f"p ({names}) -> p {names}",
                                  **{f"s{i}": s for i, s in enumerate(shape)})
            return ap

        zc = lambda k, lo, n: b.xs[:, bb, k, lo:lo + n]

        if bb == 1:
            b.dS.inc(SY.dma_start(out=b.xs[:, 1, :, 1:NP],
                                  in_=dr["x_save"][1]), 16)
        b.dS.wait(SE)
        b.dS.wait(TE)
        b.dS.wait(VE)
        VE.memset(b.xs[:, bb, :, 0:1], 0.0)
        lnb0 = mk_lnb(b, 0, 12288, 8192)
        for tc in range(4):
            srcs = [zc(k, 1 + tc * 512, 512) for k in range(KT)]
            ln_chunk(b, srcs, srcs, 512, 0, lnb0)

        y_bf = v(0, BF16, KT, 1024)
        ocat = v(0, BF16, KT, 1024)
        qf = v(16384, BF16, KT, 1024)
        kf = v(32768, BF16, KT, 1280)
        abuf = [v(53248, BF16, 2, 384), v(54784, BF16, 2, 384)]
        r_f32 = v(57344, F32, 512, p0=64, np_=2)
        r_bf = v(59392, BF16, 512, p0=64, np_=2)
        zinv2 = v(60416, BF16, 2, 256, p0=64, np_=1)
        m1t = v(61440, BF16, 512)
        m2t = v(62464, BF16, 512)
        rsb = v(63488, BF16, 512)
        v65 = rv(0, BF16, 10, H, HD + 1)
        qkvpan = rv(20800, BF16, 2, KT, 512)
        wo_sb = rv(37184, BF16, KT, 1024)
        tabs = rv(53568, BF16, 2, 4, 512)
        xn = v(0, BF16, KT, 2048)
        w1pan = v(32768, BF16, 2, 2, KT, 128)
        hout = v(40960, BF16, 2, 2048)
        ff_tmp = [v(49152 + i * 2048, F32, 512) for i in range(4)]
        lnb1 = mk_lnb(b, 45056, 53248, 57344, viewfn=rv)
        lnb1["t1"] = v(57344, F32, 512)
        lnb1["t2"] = v(59392, F32, 512)
        at2 = v(64512, BF16, 512)
        hin = [v(0, BF16, FT1, 512), v(22528, BF16, FT1, 512)]
        w2all = rv(0, BF16, KT, FT1, 128)

        b.g1tmp_guard = [(None, 0), (None, 0)]
        b.fftmp_guard = [(None, 0), (None, 0)]
        for lyr in range(FFN_DEPTH):
            b.es.wait(SY)
            b.ev.wait(SY)
            b.dS.inc(SY.dma_start(out=sm[:, :], in_=dr["small"][lyr]), 16)
            smalls_ld = b.dS.n

            for c in range(2):
                T0 = c * 1024
                b.ev.wait(GE)
                b.mm.wait(GE)
                for k in range(KT):
                    src = zc(k, 1 + T0, 1024) if k < 4 else zc(k, T0, 1024)
                    i = GE.tensor_copy(y_bf[:, k, :], src)
                b.ep.inc(i)
                for jc in range(6):
                    slot = jc % 2
                    if jc >= 2:
                        b.mm.wait(GE)
                    b.dW.inc(GE.dma_start(out=qkvpan[:, slot],
                                          in_=wq_fl[lyr * 6 + jc],
                                          max_dma_last_dim=8192), 16)
                    b.dW.wait(TE)
                    b.ep.wait(TE)
                    if jc < 4:
                        isq = jc < 2
                        for jsub in range(4):
                            jt = (jc % 2) * 4 + jsub
                            for hc in range(2):
                                bank = (jsub * 2 + hc) % 2
                                bank_wait(b, bank)
                                for k in range(KT):
                                    i = TE.matmul(
                                        ps[:, bank, :],
                                        qkvpan[:, slot, k, jsub * 128:(jsub + 1) * 128],
                                        y_bf[:, k, hc * 512:(hc + 1) * 512],
                                        start=(k == 0), stop=(k == KT - 1))
                                b.mm.inc(i)
                                rope_pipe(b, bb, bank, jt, isq, c, hc,
                                          qf, kf, tabs, m1t, m2t, rsb,
                                          r_f32, r_bf, smalls_ld)
                    else:
                        for tt in range(8):
                            bank = tt % 2
                            bank_wait(b, bank)
                            for k in range(KT):
                                i = TE.matmul(
                                    ps[:, bank, :],
                                    y_bf[:, k, tt * 128:(tt + 1) * 128],
                                    qkvpan[:, slot, k, :],
                                    start=(k == 0), stop=(k == KT - 1))
                            b.mm.inc(i)
                            b.mm.wait(SE)
                            i = SE.copy(
                                v65[:, 2 + tt, (jc - 4) * 8:(jc - 4) * 8 + 8, 0:HD],
                                ps[:, bank, :])
                            b.es.inc(i)
                            b.bg[bank] = [(b.es, b.es.n)]
                b.es.wait(VE)
                b.ev.inc(VE.memset(v65[:, :, :, HD:HD + 1], 1.0))
                ones_done = b.ev.n

                b.ev.wait(TE, ones_done)
                for hp in range(KT):
                    for w in range(4):
                        first = (c == 0 and w == 0)
                        mts = (2, 3) if first else (2, 0, 1, 3)
                        local_attn_unit(b, bb, hp, w, T0, mts, qf, kf, v65,
                                        abuf, zinv2, m1t, m2t, rsb, ocat)
                if DEBUG2 and bb == 1 and lyr == 0 and c == 0:
                    b.es.wait(SY)
                    b.ev.wait(SY)
                    dmp = (("e_y", v(0, BF16, KT * 1024)),
                           ("e_q", v(16384, BF16, KT * 1024)),
                           ("e_k", v(32768, BF16, KT * 1280)),
                           ("e_v", rv(0, BF16, 10 * H * (HD + 1))),
                           ("e_o", v(0, BF16, KT * 1024)))
                    for nm, ap in dmp:
                        b.dS.inc(SY.dma_start(out=dr[nm], in_=ap), 16)
                    b.dS.wait(TE)
                    b.dS.wait(GE)
                if c == 0:
                    b.dW.inc(GE.dma_start(out=wo_sb[:, :],
                                          in_=dr["Wo_f"][lyr],
                                          max_dma_last_dim=8192), 16)
                b.dW.wait(TE)
                for hc in range(2):
                    for ft in range(KT):
                        bank = (hc * KT + ft) % 4
                        bank_wait(b, bank)
                        for jt in range(KT):
                            i = TE.matmul(
                                ps[:, bank, :],
                                wo_sb[:, jt, ft * 128:(ft + 1) * 128],
                                ocat[:, jt, hc * 512:(hc + 1) * 512],
                                start=(jt == 0), stop=(jt == KT - 1))
                        b.mm.inc(i)
                        b.mm.wait(VE)
                        i = VE.tensor_tensor(
                            zc(ft, 1 + T0 + hc * 512, 512),
                            zc(ft, 1 + T0 + hc * 512, 512),
                            ps[:, bank, :], ALU.add)
                        b.ev.inc(i)
                        b.bg[bank] = [(b.ev, b.ev.n)]
                b.ev.wait(GE)
                b.mm.wait(GE)
                GE.tensor_copy(kf[:, :, 0:256], kf[:, :, 1024:1280])
                i = GE.tensor_copy(v65[:, 0:2, :, :], v65[:, 8:10, :, :])
                b.ep.inc(i)
                b.ep.wait(TE)
                b.ep.wait(SE)

            if FFN_ATTN_ONLY:
                continue
            b.mm.wait(VE)
            b.mm.wait(SE)
            for tc in range(4):
                srcs = [zc(k, 1 + tc * 512, 512) if k < 4
                        else zc(k, tc * 512, 512) for k in range(KT)]
                dsts = [xn[:, k, tc * 512:(tc + 1) * 512] for k in range(KT)]
                ln_chunk(b, srcs, dsts, 512, 0, lnb1,
                         gamma=sm[:, 2:10], beta=sm[:, 10:18])
            xn_done = b.ev.n
            b.mm.wait(GE)
            b.ev.wait(GE)
            w1_guard = [(None, 0), (None, 0)]
            ho_guard = [(None, 0), (None, 0)]
            for fp in range(FT1):
                slot = fp % 2
                g = w1_guard[slot]
                if g[0] is not None:
                    g[0].wait(GE, g[1])
                b.dW.inc(GE.dma_start(out=w1pan[:, slot],
                                      in_=w1_fl[lyr * FT1 + fp]), 16)
                b.dW.wait(TE)
                if fp == 0:
                    b.ev.wait(TE, xn_done)
                g = ho_guard[slot]
                if g[0] is not None:
                    g[0].wait(VE, g[1])
                for ts in range(4):
                    ba = (ts % 2) * 2
                    bank_wait(b, ba)
                    bank_wait(b, ba + 1)
                    for k in range(KT):
                        TE.matmul(ps[:, ba, :], w1pan[:, slot, 0, k, :],
                                  xn[:, k, ts * 512:(ts + 1) * 512],
                                  start=(k == 0), stop=(k == KT - 1))
                    for k in range(KT):
                        i = TE.matmul(ps[:, ba + 1, :], w1pan[:, slot, 1, k, :],
                                      xn[:, k, ts * 512:(ts + 1) * 512],
                                      start=(k == 0), stop=(k == KT - 1))
                    b.mm.inc(i)
                    b.mm.wait(SE)
                    gt = m1t if ts % 2 == 0 else m2t
                    at = rsb if ts % 2 == 0 else at2
                    g = b.g1tmp_guard[ts % 2]
                    if g[0] is not None:
                        g[0].wait(SE, g[1])
                    SE.activation(gt, ps[:, ba + 1, :], AF.Gelu,
                                  bias=sm[:, 40 + fp:41 + fp])
                    i = SE.activation(at, ps[:, ba, :], AF.Identity,
                                      bias=sm[:, 18 + fp:19 + fp])
                    b.es.inc(i)
                    b.es.wait(VE)
                    i = VE.tensor_tensor(
                        hout[:, slot, ts * 512:(ts + 1) * 512], at, gt,
                        ALU.mult)
                    b.ev.inc(i)
                    b.g1tmp_guard[ts % 2] = (b.ev, b.ev.n)
                    b.bg[ba] = [(b.ev, b.ev.n)]
                    b.bg[ba + 1] = [(b.ev, b.ev.n)]
                w1_guard[slot] = (b.mm, b.mm.n)
                b.ev.wait(SY)
                b.dS.inc(SY.dma_start(out=dr["h_sc"][:, fp, :],
                                      in_=hout[:, slot, :]), 16)
                ho_guard[slot] = (b.dS, b.dS.n)
            b.mm.wait(GE)
            b.ev.wait(GE)
            for ft in range(KT):
                b.dW.inc(GE.dma_start(out=w2all[:, ft], in_=w2_fl[lyr * KT + ft],
                                      max_dma_last_dim=8192), 16)
            w2_ld = b.dW.n
            hin_guard = [(None, 0), (None, 0)]
            for ts in range(4):
                slot = ts % 2
                g = hin_guard[slot]
                if g[0] is not None:
                    g[0].wait(SY, g[1])
                b.dS.inc(SY.dma_start(out=hin[slot],
                                      in_=dr["h_sc"][:, :, ts * 512:(ts + 1) * 512]), 16)
                b.dS.wait(TE)
                b.dW.wait(TE, w2_ld)
                for ft in range(KT):
                    bank = 4 + (ft % 2)
                    bank_wait(b, bank)
                    for kt in range(FT1):
                        i = TE.matmul(ps[:, bank, :], w2all[:, ft, kt, :],
                                      hin[slot][:, kt, :],
                                      start=(kt == 0), stop=(kt == FT1 - 1))
                    b.mm.inc(i)
                    b.mm.wait(SE)
                    stt = ff_tmp[ft % 2]
                    g = b.fftmp_guard[ft % 2]
                    if g[0] is not None:
                        g[0].wait(SE, g[1])
                    i = SE.activation(stt, ps[:, bank, :], AF.Identity,
                                      bias=sm[:, 62 + ft:63 + ft])
                    b.es.inc(i)
                    b.es.wait(VE)
                    i = VE.tensor_tensor(zc(ft, 1 + ts * 512, 512),
                                         zc(ft, 1 + ts * 512, 512), stt,
                                         ALU.add)
                    b.ev.inc(i)
                    b.fftmp_guard[ft % 2] = (b.ev, b.ev.n)
                    b.bg[bank] = [(b.ev, b.ev.n)]
                hin_guard[slot] = (b.mm, b.mm.n)

        join_all(b, (GE, SY))
        if bb == 0:
            b.dW.inc(GE.dma_start(out=dr["x_save"][0],
                                  in_=b.xs[:, 0, :, 1:NP],
                                  accum_op=ALU.add), 16)
        else:
            b.dW.inc(GE.dma_start(out=b.xs[:, 1, :, 1:NP],
                                  in_=dr["x_save"][1], accum_op=ALU.add), 16)
            b.dW.wait(GE)
    join_all(b, (SY,))
    b.dS.inc(SY.dma_start(out=b.xs[:, 0, :, 1:NP], in_=dr["x_save"][0]), 16)
    b.dS.wait(SY)


def rope_pipe(b, bb, bank, jt, isq, c, hc, qf, kf, tabs, m1t, m2t, rsb,
              r_f32, r_bf, smalls_ld):
    TE, VE, SE, GE, SY = b.TE, b.VE, b.SE, b.GE, b.SY
    ps = b.ps
    T0 = c * 1024
    slot = hc
    if isq and jt == 0:
        b.ev.wait(GE)
        b.dW.inc(GE.dma_start(
            out=tabs[:, slot],
            in_=b.dr["xpos"][:, :, T0 + hc * 512:T0 + (hc + 1) * 512]
            .rearrange("a p t -> p a t")), 16)
    dst = (qf[:, jt, hc * 512:(hc + 1) * 512] if isq
           else kf[:, jt, 256 + hc * 512:256 + (hc + 1) * 512])
    ct = tabs[:, slot, 0 if isq else 2, :]
    st = tabs[:, slot, 1 if isq else 3, :]
    b.mm.wait(SE)
    b.ev.wait(SE)
    i = SE.activation(m1t[:, :], ps[:, bank, :], AF.Square)
    b.es.inc(i)
    bank_wait(b, 6)
    b.es.wait(TE)
    i = TE.matmul(ps[64:66, 6, :], b.c_ones2, m1t[:, :],
                  start=True, stop=True)
    b.mm.inc(i)
    b.mm.wait(SE)
    i = SE.activation(r_f32[0:2, :], ps[64:66, 6, :], AF.Sqrt,
                      bias=b.cbias[64:66, 5:6])
    b.es.inc(i)
    b.bg[6] = [(b.es, b.es.n)]
    b.es.wait(VE)
    with b.nc.allow_low_precision(reason="rnorm feeds bf16 matmul"):
        i = VE.reciprocal(r_bf[0:2, :], r_f32[0:2, :])
    b.ev.inc(i)
    bank_wait(b, 7)
    b.ev.wait(TE)
    i = TE.matmul(ps[:, 7, :], b.c_sel2, r_bf[0:2, :], start=True, stop=True)
    b.mm.inc(i)
    b.mm.wait(VE)
    b.dS.wait(VE, smalls_ld)
    col = 0 if isq else 1
    VE.tensor_scalar(rsb[:, :], ps[:, 7, :], b.smalls[:, col:col + 1], None,
                     ALU.mult)
    i = VE.tensor_tensor(dst, ps[:, bank, :], rsb[:, :], ALU.mult)
    b.ev.inc(i)
    b.bg[7] = [(b.ev, b.ev.n)]
    b.bg[bank] = [(b.ev, b.ev.n)]
    bank_wait(b, 5)
    b.ev.wait(TE)
    i = TE.matmul(ps[:, 5, :], b.c_PH, dst, start=True, stop=True)
    b.mm.inc(i)
    b.dW.wait(VE)
    VE.tensor_tensor(m1t[:, :], dst, ct, ALU.mult)
    b.mm.wait(VE)
    VE.tensor_tensor(m2t[:, :], ps[:, 5, :], st, ALU.mult)
    i = VE.tensor_tensor(dst, m1t[:, :], m2t[:, :], ALU.add)
    b.ev.inc(i)
    b.bg[5] = [(b.ev, b.ev.n)]


def local_attn_unit(b, bb, hp, w, T0, mts, qf, kf, v65, abuf, zinv2,
                    m1t, m2t, rsb, ocat):
    TE, VE, SE = b.TE, b.VE, b.SE
    ps = b.ps
    def sc_slot(mt):
        if mt == 0:
            return 0, 0, 128, 0, 128
        if mt == 1:
            return 0, 128, 256, 0, 256
        if mt == 2:
            return 1, 0, 256, 0, 256
        return 1, 256, 128, 128, 128
    for h01 in (0, 1):
        for bk in (2 * h01, 2 * h01 + 1):
            bank_wait(b, bk)
    p0 = lambda h01: h01 * 64
    qs = lambda h01, n0, nw: qf[p0(h01):p0(h01) + 64, hp,
                                w * 256 + n0:w * 256 + n0 + nw]
    ks = lambda h01, mt: kf[p0(h01):p0(h01) + 64, hp,
                            w * 256 + mt * 128:w * 256 + (mt + 1) * 128]
    for mt in mts:
        sb, c0, cw, n0, nw = sc_slot(mt)
        i = TE.matmul(ps[:, 2 * 0 + sb, c0:c0 + cw], ks(0, mt), qs(0, n0, nw),
                      start=True, stop=True)
        i = TE.matmul(ps[:, 2 * 1 + sb, c0:c0 + cw], ks(1, mt), qs(1, n0, nw),
                      start=True, stop=True)
    b.mm.inc(i)
    b.mm.wait(SE)
    b.ev.wait(SE)
    for h01 in (0, 1):
        sbs = (0, 1) if 0 in mts else (1,)
        for sb in sbs:
            i = SE.activation(abuf[h01][:, sb, 0:384],
                              ps[:, 2 * h01 + sb, 0:384], AF.Exp, scale=8.0,
                              bias=b.cbias[:, 3:4])
    b.es.inc(i)
    for h01 in (0, 1):
        b.bg[2 * h01] = [(b.es, b.es.n)]
        b.bg[2 * h01 + 1] = [(b.es, b.es.n)]
    b.es.wait(VE)
    for h01 in (0, 1):
        a = abuf[h01]
        if 0 in mts:
            VE.tensor_tensor(a[:, 0, 0:128], a[:, 0, 0:128], b.c_tria,
                             ALU.mult)
            VE.tensor_tensor(a[:, 0, 256:384], a[:, 0, 256:384], b.c_tria,
                             ALU.mult)
        VE.tensor_tensor(a[:, 1, 0:128], a[:, 1, 0:128], b.c_tric, ALU.mult)
        i = VE.tensor_tensor(a[:, 1, 256:384], a[:, 1, 256:384], b.c_tric,
                             ALU.mult)
    b.ev.inc(i)
    bank_wait(b, 4)
    bank_wait(b, 5)
    b.ev.wait(TE)
    amap = {0: (0, 0, 128, 0, 128), 1: (0, 128, 384, 0, 256),
            2: (1, 0, 256, 0, 256), 3: (1, 256, 384, 128, 128)}
    for h01 in (0, 1):
        for n_mt, mt in enumerate(mts):
            sb, c0, c1, n0, nw = amap[mt]
            i = TE.matmul(ps[0:65, 4 + h01, n0:n0 + nw],
                          v65[:, 2 * w + mt, 2 * hp + h01, :],
                          abuf[h01][:, sb, c0:c1],
                          start=(n_mt == 0), stop=(n_mt == len(mts) - 1),
                          skip_group_check=True)
    b.mm.inc(i)
    b.mm.wait(VE)
    with b.nc.allow_low_precision(reason="zinv feeds bf16 path"):
        VE.reciprocal(zinv2[0:1, 0, :], ps[64:65, 4, 0:256])
        i = VE.reciprocal(zinv2[0:1, 1, :], ps[64:65, 5, 0:256])
    b.ev.inc(i)
    bank_wait(b, 6)
    b.ev.wait(TE)
    TE.matmul(ps[0:64, 6, 0:256], b.c_ones(64, 1, 64), zinv2[0:1, 0, :],
              start=True, stop=True)
    i = TE.matmul(ps[0:64, 6, 256:512], b.c_ones(64, 1, 64), zinv2[0:1, 1, :],
                  start=True, stop=True, skip_group_check=True)
    b.mm.inc(i)
    b.mm.wait(SE)
    SE.copy(m1t[0:64, 0:256], ps[0:64, 4, 0:256])
    i = SE.copy(m2t[0:64, 0:256], ps[0:64, 5, 0:256])
    b.es.inc(i)
    b.bg[4] = [(b.es, b.es.n)]
    b.bg[5] = [(b.es, b.es.n)]
    b.es.wait(VE)
    VE.tensor_tensor(ocat[0:64, hp, w * 256:(w + 1) * 256], m1t[0:64, 0:256],
                     ps[0:64, 6, 0:256], ALU.mult)
    i = VE.tensor_tensor(rsb[0:64, 0:256], m2t[0:64, 0:256],
                         ps[0:64, 6, 256:512], ALU.mult)
    b.ev.inc(i)
    b.bg[6] = [(b.ev, b.ev.n)]
    bank_wait(b, 7)
    b.ev.wait(TE)
    i = TE.matmul(ps[:, 7, 0:256], b.c_SHIFT, rsb[0:64, 0:256],
                  start=True, stop=True)
    b.mm.inc(i)
    b.mm.wait(SE)
    i = SE.copy(ocat[64:128, hp, w * 256:(w + 1) * 256], ps[64:128, 7, 0:256])
    b.es.inc(i)
    b.bg[7] = [(b.es, b.es.n)]


def emit_attn_b(b):
    TE, VE, SE, GE, SY = b.TE, b.VE, b.SE, b.GE, b.SY
    ps, v = b.ps, b.view
    dr = b.dr

    join_all(b, (TE, VE, SE, GE, SY))

    b.dW.inc(GE.dma_start(out=b.pmk_b[:, :, :], in_=dr["pmk"]), 16)
    b.dW.inc(GE.dma_start(out=b.pmv_b[:, :, :], in_=dr["pmv"]), 16)

    u = v(0, BF16, KT, 512)
    qseg = v(8192, BF16, KT, 512)
    kseg = v(16384, BF16, KT, 512)
    v65b = v(24576, BF16, 4, H, HD + 1)
    ocat = v(32896, BF16, KT, 512)
    abuf = [v(41088, BF16, 512), v(42112, BF16, 512)]
    apm_e = v(43136, BF16, 512, p0=0, np_=16)
    apm_o = v(44160, BF16, 512, p0=0, np_=16)
    otmp_e = v(45184, BF16, 512)
    otmp_o = v(46208, BF16, 512)
    zinv_e = v(47232, BF16, 512, p0=64, np_=1)
    zinv_o = v(48256, BF16, 512, p0=64, np_=1)
    tabs_b = v(49280, BF16, 2, 512)
    m1t = v(51328, BF16, 512)
    m2t = v(52352, BF16, 512)
    wpan = v(53376, BF16, 2, KT, 512)
    lnb = mk_lnb(b, 24576, 28800, 41088)

    for bb in range(NB):
        for s in range(4):
            S0 = s * 512
            join_all(b, (GE, SY))
            srcs = [b.xs[:, bb, k, 1 + S0:1 + S0 + 512] for k in range(KT)]
            dsts = [u[:, k, :] for k in range(KT)]
            ln_chunk(b, srcs, dsts, 512, 1, lnb, gamma=b.gb_sb,
                     want_mean=False)
            u_done = b.ev.n
            b.ev.wait(GE)
            b.dW.inc(GE.dma_start(
                out=tabs_b,
                in_=dr["ropeb"][:, :, S0:S0 + 512].rearrange(
                    "a p t -> p a t")), 16)
            b.ev.wait(SE, u_done)
            for jc in range(6):
                slot = jc % 2
                if jc >= 2:
                    b.mm.wait(GE)
                b.dW.inc(GE.dma_start(out=wpan[:, slot],
                                      in_=dr["wqkvT"][jc],
                                      max_dma_last_dim=8192), 16)
                b.dW.wait(TE)
                b.ev.wait(TE, u_done)
                if jc < 4:
                    isq = jc < 2
                    dstt = qseg if isq else kseg
                    for jsub in range(4):
                        jt = (jc % 2) * 4 + jsub
                        bank = jsub % 2
                        bank_wait(b, bank)
                        for k in range(KT):
                            i = TE.matmul(
                                ps[:, bank, :],
                                wpan[:, slot, k, jsub * 128:(jsub + 1) * 128],
                                u[:, k, :], start=(k == 0), stop=(k == KT - 1))
                        b.mm.inc(i)
                        dsl = dstt[:, jt, :]
                        b.mm.wait(SE)
                        b.ev.wait(SE)
                        i = SE.copy(dsl, ps[:, bank, :])
                        b.es.inc(i)
                        b.bg[bank] = [(b.es, b.es.n)]
                        bank_wait(b, 7)
                        b.es.wait(TE)
                        i = TE.matmul(ps[:, 7, :], b.c_PI, dsl,
                                      start=True, stop=True)
                        b.mm.inc(i)
                        b.dW.wait(VE)
                        b.es.wait(VE)
                        VE.tensor_tensor(m1t, dsl, tabs_b[:, 0, :], ALU.mult)
                        b.mm.wait(VE)
                        VE.tensor_tensor(m2t, ps[:, 7, :], tabs_b[:, 1, :],
                                         ALU.mult)
                        i = VE.tensor_tensor(dsl, m1t, m2t, ALU.add)
                        b.ev.inc(i)
                        b.bg[7] = [(b.ev, b.ev.n)]
                else:
                    for tt in range(4):
                        bank = tt % 2
                        bank_wait(b, bank)
                        for k in range(KT):
                            i = TE.matmul(
                                ps[:, bank, :],
                                u[:, k, tt * 128:(tt + 1) * 128],
                                wpan[:, slot, k, :],
                                start=(k == 0), stop=(k == KT - 1))
                        b.mm.inc(i)
                        b.mm.wait(SE)
                        i = SE.copy(
                            v65b[:, tt, (jc - 4) * 8:(jc - 4) * 8 + 8, 0:HD],
                            ps[:, bank, :])
                        b.es.inc(i)
                        b.bg[bank] = [(b.es, b.es.n)]
            b.es.wait(VE)
            b.ev.inc(VE.memset(v65b[:, :, :, HD:HD + 1], 1.0))
            ones_done = b.ev.n
            b.ev.wait(TE, ones_done)
            b.dW.wait(TE)

            for hp in range(KT):
                for bk in (0, 1, 2, 3):
                    bank_wait(b, bk)
                qsl = lambda h01, c0, cw: qseg[h01 * 64:h01 * 64 + 64, hp,
                                               c0:c0 + cw]
                i = TE.matmul(ps[0:16, 0, :], b.pmk_b[0:64, hp, :],
                              qsl(0, 0, 512), start=True, stop=True)
                i = TE.matmul(ps[0:16, 1, :], b.pmk_b[64:128, hp, :],
                              qsl(1, 0, 512), start=True, stop=True)
                b.mm.inc(i)
                b.mm.wait(SE)
                b.ev.wait(SE)
                i = SE.activation(apm_e[0:16, :], ps[0:16, 0, :], AF.Exp,
                                  scale=0.125, bias=b.cbias[0:16, 4:5])
                i = SE.activation(apm_o[0:16, :], ps[0:16, 1, :], AF.Exp,
                                  scale=0.125, bias=b.cbias[0:16, 4:5])
                b.es.inc(i)
                b.bg[0] = [(b.es, b.es.n)]
                b.bg[1] = [(b.es, b.es.n)]
                b.es.wait(TE)
                TE.matmul(ps[0:65, 2, :], b.pmv_b[:, 2 * hp, :],
                          apm_e[0:16, :], start=True, stop=False,
                          skip_group_check=True)
                i = TE.matmul(ps[0:65, 3, :], b.pmv_b[:, 2 * hp + 1, :],
                              apm_o[0:16, :], start=True, stop=False,
                              skip_group_check=True)
                b.mm.inc(i)
                for kt in range(4):
                    cw = 512 - kt * 128
                    bank_wait(b, 0)
                    bank_wait(b, 1)
                    ksl = lambda h01: kseg[h01 * 64:h01 * 64 + 64, hp,
                                           kt * 128:(kt + 1) * 128]
                    i = TE.matmul(ps[:, 0, kt * 128:512], ksl(0),
                                  qsl(0, kt * 128, cw), start=True, stop=True)
                    i = TE.matmul(ps[:, 1, kt * 128:512], ksl(1),
                                  qsl(1, kt * 128, cw), start=True, stop=True)
                    b.mm.inc(i)
                    b.mm.wait(SE)
                    b.ev.wait(SE)
                    i = SE.activation(abuf[0][:, kt * 128:512],
                                      ps[:, 0, kt * 128:512], AF.Exp,
                                      scale=0.125, bias=b.cbias[:, 4:5])
                    i = SE.activation(abuf[1][:, kt * 128:512],
                                      ps[:, 1, kt * 128:512], AF.Exp,
                                      scale=0.125, bias=b.cbias[:, 4:5])
                    b.es.inc(i)
                    b.bg[0] = [(b.es, b.es.n)]
                    b.bg[1] = [(b.es, b.es.n)]
                    b.es.wait(VE)
                    VE.tensor_tensor(abuf[0][:, kt * 128:(kt + 1) * 128],
                                     abuf[0][:, kt * 128:(kt + 1) * 128],
                                     b.c_tric, ALU.mult)
                    i = VE.tensor_tensor(abuf[1][:, kt * 128:(kt + 1) * 128],
                                         abuf[1][:, kt * 128:(kt + 1) * 128],
                                         b.c_tric, ALU.mult)
                    b.ev.inc(i)
                    b.ev.wait(TE)
                    TE.matmul(ps[0:65, 2, kt * 128:512],
                              v65b[:, kt, 2 * hp, :], abuf[0][:, kt * 128:512],
                              start=False, stop=(kt == 3),
                              skip_group_check=True)
                    i = TE.matmul(ps[0:65, 3, kt * 128:512],
                                  v65b[:, kt, 2 * hp + 1, :],
                                  abuf[1][:, kt * 128:512],
                                  start=False, stop=(kt == 3),
                                  skip_group_check=True)
                    b.mm.inc(i)
                b.mm.wait(VE)
                with b.nc.allow_low_precision(reason="zinv bf16"):
                    VE.reciprocal(zinv_e[0:1, :], ps[64:65, 2, :])
                    i = VE.reciprocal(zinv_o[0:1, :], ps[64:65, 3, :])
                b.ev.inc(i)
                bank_wait(b, 4)
                bank_wait(b, 5)
                b.ev.wait(TE)
                TE.matmul(ps[0:64, 4, :], b.c_ones(64, 1, 64), zinv_e[0:1, :],
                          start=True, stop=True)
                i = TE.matmul(ps[0:64, 5, :], b.c_ones(64, 1, 64),
                              zinv_o[0:1, :], start=True, stop=True)
                b.mm.inc(i)
                b.mm.wait(SE)
                SE.copy(otmp_e[0:64, :], ps[0:64, 2, :])
                i = SE.copy(otmp_o[0:64, :], ps[0:64, 3, :])
                b.es.inc(i)
                b.bg[2] = [(b.es, b.es.n)]
                b.bg[3] = [(b.es, b.es.n)]
                b.es.wait(VE)
                VE.tensor_tensor(ocat[0:64, hp, :], otmp_e[0:64, :],
                                 ps[0:64, 4, :], ALU.mult)
                i = VE.tensor_tensor(m1t[0:64, :], otmp_o[0:64, :],
                                     ps[0:64, 5, :], ALU.mult)
                b.ev.inc(i)
                b.bg[4] = [(b.ev, b.ev.n)]
                b.bg[5] = [(b.ev, b.ev.n)]
                bank_wait(b, 6)
                b.ev.wait(TE)
                i = TE.matmul(ps[:, 6, :], b.c_SHIFT, m1t[0:64, :],
                              start=True, stop=True)
                b.mm.inc(i)
                b.mm.wait(SE)
                i = SE.copy(ocat[64:128, hp, :], ps[64:128, 6, :])
                b.es.inc(i)
                b.bg[6] = [(b.es, b.es.n)]

            ocat_done = b.es.n
            b.es.wait(GE)
            b.ev.wait(GE)
            wo_guard = [(None, 0), (None, 0)]
            for ft in range(KT):
                slot = ft % 2
                g = wo_guard[slot]
                if g[0] is not None:
                    g[0].wait(GE, g[1])
                b.dW.inc(GE.dma_start(out=wpan[:, slot, :, 0:128],
                                      in_=dr["woutT"][ft]), 16)
                b.dW.wait(TE)
                if ft == 0:
                    b.es.wait(TE, ocat_done)
                bank = ft % 4
                bank_wait(b, bank)
                for jt in range(KT):
                    i = TE.matmul(ps[:, bank, :], wpan[:, slot, jt, 0:128],
                                  ocat[:, jt, :], start=(jt == 0),
                                  stop=(jt == KT - 1))
                b.mm.inc(i)
                wo_guard[slot] = (b.mm, b.mm.n)
                b.mm.wait(VE)
                i = VE.tensor_tensor(b.xs[:, bb, ft, 1 + S0:1 + S0 + 512],
                                     b.xs[:, bb, ft, 1 + S0:1 + S0 + 512],
                                     ps[:, bank, :], ALU.add)
                b.ev.inc(i)
                b.bg[bank] = [(b.ev, b.ev.n)]


def kernel(**inputs):
    in_maps = host_prep(inputs)
    nc = build(UPTO)
    res = run_bass_kernel_spmd(nc, in_maps, core_ids=list(range(8)))
    out = np.zeros((16, N, D), np.float32)
    for c in range(8):
        xo = res.results[c]["x_out"]
        out[NB * c:NB * c + NB] = (
            xo.transpose(1, 0, 2, 3).reshape(D, NB, N).transpose(1, 2, 0))
    return out


# version 02
# speedup vs baseline: 1.4032x; 1.4032x over previous
import numpy as np
import concourse.bass as bass
import concourse.mybir as mybir
from concourse.bass_utils import run_bass_kernel_spmd

F32 = mybir.dt.float32
BF16 = mybir.dt.bfloat16
AF = mybir.ActivationFunctionType
ALU = mybir.AluOpType

D = 1024
KT = 8
NB = 2
N = 2048
NP = N + 1
H = 16
HD = 64
DEPTH = 6
DI = 2730
FT1 = 22
WIN = 256
SEG = 512
PMEM = 16

B_ATTA = 4.0
B_LOC = 24.0
B_ATTB = 8.0
LN_EPS = 1e-5
RMS_EPS = 1.1920929e-07

UPTO = 4
FFN_DEPTH = DEPTH
FFN_ATTN_ONLY = False
DEBUG2 = False
DEBUG = False



def _host_tables():
    t = np.arange(N, dtype=np.float64)
    half = np.arange(0, HD, 2, dtype=np.float64)
    inv_freq = 1.0 / 10000.0 ** (half / HD)
    freqs = np.concatenate([t[:, None] * inv_freq[None, :]] * 2, -1)
    base_scale = (half + 0.4 * HD) / (1.4 * HD)
    power = (t - N // 2) / (WIN // 2)
    scale = base_scale[None, :] ** power[:, None]
    scale = np.concatenate([scale, scale], -1)
    cos, sin = np.cos(freqs), np.sin(freqs)
    xpos = np.stack([np.tile(a.T, (2, 1)) for a in
                     (cos * scale, sin * scale, cos / scale, sin / scale)])

    fb = np.repeat(t[:, None] * inv_freq[None, :], 2, axis=-1)
    ropeb = np.stack([np.tile(np.cos(fb).T, (2, 1)),
                      np.tile(np.sin(fb).T, (2, 1))])

    PH = np.zeros((128, 128))
    PI = np.zeros((128, 128))
    for blk in (0, 64):
        for m in range(32):
            PH[blk + m + 32, blk + m] = -1.0
            PH[blk + m, blk + m + 32] = 1.0
        for i in range(32):
            PI[blk + 2 * i + 1, blk + 2 * i] = -1.0
            PI[blk + 2 * i, blk + 2 * i + 1] = 1.0
    SHIFT64 = np.zeros((128, 128))
    for d_ in range(64):
        SHIFT64[d_, 64 + d_] = 1.0

    p_ = np.arange(128)[:, None]
    f_ = np.arange(128)[None, :]
    tri_causal = (p_ <= f_).astype(np.float64)
    tri_anti = (p_ >= f_).astype(np.float64)
    return dict(xpos=xpos, ropeb=ropeb, PH=PH, PI=PI, SHIFT64=SHIFT64,
                tri_causal=tri_causal, tri_anti=tri_anti)


C_PH = 0
C_PI = 128
C_SHIFT = 256
C_TRIC = 384
C_TRIA = 512
C_ONES = 640
C_ONES2 = 768
C_SEL2 = 772
CC_TOT = 900


def _consts_array():
    tb = _host_tables()
    c = np.zeros((128, CC_TOT))
    c[:, C_PH:C_PH + 128] = tb["PH"]
    c[:, C_PI:C_PI + 128] = tb["PI"]
    c[:, C_SHIFT:C_SHIFT + 128] = tb["SHIFT64"]
    c[:, C_TRIC:C_TRIC + 128] = tb["tri_causal"]
    c[:, C_TRIA:C_TRIA + 128] = tb["tri_anti"]
    c[:, C_ONES:C_ONES + 128] = 1.0
    c[0:64, C_ONES2] = 1.0
    c[64:128, C_ONES2 + 1] = 1.0
    c[64, C_SEL2:C_SEL2 + 64] = 1.0
    c[65, C_SEL2 + 64:C_SEL2 + 128] = 1.0
    return c.astype(np.float32), tb


def _pmajor(w, jc):
    ind, outd = w.shape
    return np.ascontiguousarray(
        w.reshape(ind // 128, 128, outd // jc, jc).transpose(2, 1, 0, 3))


def host_prep(inputs, n_cores=8):
    f32 = np.float32
    x = np.asarray(inputs["L0_attn_a_x"], f32)

    WqT = _pmajor(np.asarray(inputs["L0_attn_a_Wq"], f32).T, 128)
    WkT = _pmajor(np.asarray(inputs["L0_attn_a_Wk"], f32).T, 128)
    WvT = _pmajor(np.asarray(inputs["L0_attn_a_Wv"], f32).T, 256)
    WoT = _pmajor(np.asarray(inputs["L0_attn_a_Wo"], f32).T, 128)
    b_a = np.zeros((128, 4 * KT), f32)
    for i, nm in enumerate(["bq", "bk", "bv", "bo"]):
        b_a[:, i * KT:(i + 1) * KT] = (
            np.asarray(inputs[f"L0_attn_a_{nm}"], f32).reshape(KT, 128).T)
    bv_row = np.asarray(inputs["L0_attn_a_bv"], f32).reshape(1, D)

    wqkvT = _pmajor(np.asarray(inputs["L0_attn_b_w_qkv"], f32).T, 512)
    woutT = _pmajor(np.asarray(inputs["L0_attn_b_w_out"], f32).T, 128)
    g_b = np.asarray(inputs["L0_attn_b_g"], f32).reshape(KT, 128).T
    pm = np.asarray(inputs["L0_attn_b_pm"], f32)
    pmk = np.zeros((128, KT, PMEM), f32)
    pmv = np.zeros((16, H, HD + 1), f32)
    for h in range(H):
        pmk[(h % 2) * 64:(h % 2) * 64 + 64, h // 2, :] = pm[0, h].T
        pmv[:, h, 0:HD] = pm[1, h]
    pmv[:, :, HD] = 1.0

    Wqkv = np.stack([_pmajor(np.asarray(inputs["L0_ffn_Wqkv"], f32)[l], 512)
                     for l in range(DEPTH)])
    Wo_f = np.stack([
        _pmajor(np.asarray(inputs["L0_ffn_Wo"], f32)[l], 128)
        .transpose(1, 2, 0, 3).reshape(128, KT, D)
        for l in range(DEPTH)])
    W1 = np.asarray(inputs["L0_ffn_W1"], f32)
    W1p = np.zeros((DEPTH, FT1, 128, 2, KT, 128), f32)
    for l in range(DEPTH):
        for hi in range(2):
            w = np.zeros((D, FT1 * 128), f32)
            w[:, :DI] = W1[l][:, hi * DI:(hi + 1) * DI]
            W1p[l, :, :, hi] = (
                w.reshape(KT, 128, FT1, 128).transpose(2, 1, 0, 3))
    W2 = np.asarray(inputs["L0_ffn_W2"], f32)
    W2p = np.zeros((DEPTH, KT, 128, FT1, 128), f32)
    for l in range(DEPTH):
        w2 = np.zeros((FT1 * 128, D), f32)
        w2[:DI] = W2[l]
        W2p[l] = w2.reshape(FT1, 128, KT, 128).transpose(2, 1, 0, 3)

    small = np.zeros((DEPTH, 128, 70), f32)
    qs = np.asarray(inputs["L0_ffn_q_scale"], f32)
    ks = np.asarray(inputs["L0_ffn_k_scale"], f32)
    b1 = np.asarray(inputs["L0_ffn_b1"], f32)
    for l in range(DEPTH):
        small[l, :, 0] = np.tile(qs[l], 2)
        small[l, :, 1] = np.tile(ks[l], 2)
        small[l, :, 2:10] = np.asarray(
            inputs["L0_ffn_ln_g"], f32)[l].reshape(KT, 128).T
        small[l, :, 10:18] = np.asarray(
            inputs["L0_ffn_ln_b"], f32)[l].reshape(KT, 128).T
        b1a = np.zeros(FT1 * 128, f32)
        b1g = np.zeros(FT1 * 128, f32)
        b1a[:DI] = b1[l][:DI]
        b1g[:DI] = b1[l][DI:]
        small[l, :, 18:40] = b1a.reshape(FT1, 128).T
        small[l, :, 40:62] = b1g.reshape(FT1, 128).T
        small[l, :, 62:70] = np.asarray(
            inputs["L0_ffn_b2"], f32)[l].reshape(KT, 128).T

    consts, tb = _consts_array()
    xpos_t = tb["xpos"].astype(f32)
    ropeb_t = tb["ropeb"].astype(f32)

    Wqkv = Wqkv.reshape(DEPTH * 6, 128, KT * 512)
    W1p = W1p.reshape(DEPTH * FT1, 128, 2 * KT * 128)
    W2p = W2p.reshape(DEPTH * KT, 128, FT1 * 128)
    in_maps = []
    for c in range(n_cores):
        xc = x[NB * c:NB * c + NB]
        xfm = np.ascontiguousarray(
            xc.transpose(2, 0, 1).reshape(KT, 128, NB, N).transpose(1, 0, 2, 3))
        in_maps.append(dict(
            x_in=xfm, WqT=WqT, WkT=WkT, WvT=WvT, WoT=WoT, b_a=b_a,
            bv_row=bv_row, wqkvT=wqkvT, woutT=woutT, g_b=g_b, pmk=pmk,
            pmv=pmv, Wqkv=Wqkv, Wo_f=Wo_f, W1p=W1p, W2p=W2p, small=small,
            consts=consts, xpos=xpos_t, ropeb=ropeb_t))
    return in_maps



class Sem:
    def __init__(self, stack, nc, name):
        self.h = stack.enter_context(nc.semaphore(name))
        self.n = 0

    def inc(self, inst, k=1):
        inst.then_inc(self.h, k)
        self.n += k
        return inst

    def wait(self, eng, val=None):
        eng.wait_ge(self.h, self.n if val is None else val)


class B:
    pass


def build(upto=UPTO):
    import contextlib
    st = contextlib.ExitStack()
    nc = bass.Bass("TRN2", target_bir_lowering=False)
    b = B()
    b.nc = nc
    b.TE, b.VE, b.SE, b.GE, b.SY = (
        nc.tensor, nc.vector, nc.scalar, nc.gpsimd, nc.sync)

    dt = nc.dram_tensor
    dr = {}
    def di(name, shape):
        dr[name] = dt(name, shape, F32, kind="ExternalInput").ap()
    di("x_in", [128, KT, NB, N])
    di("WqT", [KT, 128, KT, 128])
    di("WkT", [KT, 128, KT, 128])
    di("WvT", [4, 128, KT, 256])
    di("WoT", [KT, 128, KT, 128])
    di("b_a", [128, 4 * KT])
    di("bv_row", [1, D])
    di("wqkvT", [6, 128, KT, 512])
    di("woutT", [KT, 128, KT, 128])
    di("g_b", [128, KT])
    di("pmk", [128, KT, PMEM])
    di("pmv", [16, H, HD + 1])
    di("Wqkv", [DEPTH * 6, 128, KT * 512])
    di("Wo_f", [DEPTH, 128, KT, D])
    di("W1p", [DEPTH * FT1, 128, 2 * KT * 128])
    di("W2p", [DEPTH * KT, 128, FT1 * 128])
    di("small", [DEPTH, 128, 70])
    di("consts", [128, CC_TOT])
    di("xpos", [4, 128, N])
    di("ropeb", [2, 128, N])
    dr["x_out"] = dt("x_out", [128, KT, NB, N], F32, kind="ExternalOutput").ap()
    if DEBUG2:
        for nm, shp in (("e_y", [128, KT * 1024]), ("e_q", [128, KT * 1024]),
                        ("e_k", [128, KT * 1280]), ("e_v", [128, 10 * H * (HD + 1)]),
                        ("e_o", [128, KT * 1024])):
            dr[nm] = dt(nm, shp, BF16, kind="ExternalOutput").ap()
    if DEBUG:
        for nm, shp in (("d_u", [128, KT, N]), ("d_q", [128, KT, N]),
                        ("d_k", [128, KT, N]), ("d_v", [128, 16, H, HD + 1]),
                        ("d_o", [128, KT, N])):
            dr[nm] = dt(nm, shp, BF16, kind="ExternalOutput").ap()
    dr["x_save"] = dt("x_save", [NB, 128, KT, N], F32, kind="Internal").ap()
    dr["h_sc"] = nc.dram_tensor("h_sc", [128, FT1, N], BF16, kind="Internal").ap()
    b.dr = dr

    xs = st.enter_context(nc.sbuf_tensor("xs", [128, NB, KT, NP], F32))
    WB = 74240
    b.WB = WB
    arena = st.enter_context(nc.sbuf_tensor("arena", [128, WB], mybir.dt.uint8))
    onesf = st.enter_context(nc.sbuf_tensor("onesf", [128, 128], F32))
    cst_b = st.enter_context(nc.sbuf_tensor("cst_b", [128, CC_TOT], BF16))
    smalls = st.enter_context(nc.sbuf_tensor("smalls", [128, 70], F32))
    ba_sb = st.enter_context(nc.sbuf_tensor("ba_sb", [128, 4 * KT], F32))
    gb_sb = st.enter_context(nc.sbuf_tensor("gb_sb", [128, KT], F32))
    bvb = st.enter_context(nc.sbuf_tensor("bvb", [1, D], BF16))
    pmk_b = st.enter_context(nc.sbuf_tensor("pmk_b", [128, KT, PMEM], BF16))
    pmv_b = st.enter_context(nc.sbuf_tensor("pmv_b", [16, H, HD + 1], BF16))
    cbias = st.enter_context(nc.sbuf_tensor("cbias", [128, 8], F32))
    ps = st.enter_context(nc.psum_tensor("ps", [128, 8, 512], F32))
    b.xs, b.arena, b.onesf, b.cst_b, b.smalls, b.ps = (
        xs, arena, onesf, cst_b, smalls, ps)
    b.ba_sb, b.gb_sb, b.bvb, b.cbias = ba_sb, gb_sb, bvb, cbias
    b.pmk_b, b.pmv_b = pmk_b, pmv_b

    def view(off, dtype, *shape, p0=0, np_=128):
        esz = 4 if dtype == F32 else 2
        nby = int(np.prod(shape)) * esz
        assert off + nby <= WB, (off, nby, WB)
        ap = arena[p0:p0 + np_, off:off + nby].bitcast(dtype)
        if len(shape) > 1:
            names = " ".join(f"s{i}" for i in range(len(shape)))
            ap = ap.rearrange(f"p ({names}) -> p {names}",
                              **{f"s{i}": v for i, v in enumerate(shape)})
        return ap
    b.view = view

    b.dW = Sem(st, nc, "dW")
    b.dS = Sem(st, nc, "dS")
    b.mm = Sem(st, nc, "mm")
    b.ev = Sem(st, nc, "ev")
    b.es = Sem(st, nc, "es")
    b.ep = Sem(st, nc, "ep")
    b.bar = Sem(st, nc, "bar")

    b.c_PH = cst_b[:, C_PH:C_PH + 128]
    b.c_PI = cst_b[:, C_PI:C_PI + 128]
    b.c_SHIFT = cst_b[0:64, C_SHIFT:C_SHIFT + 128]
    b.c_tric = cst_b[:, C_TRIC:C_TRIC + 128]
    b.c_tria = cst_b[:, C_TRIA:C_TRIA + 128]
    b.c_ones = lambda p0, np_, w: cst_b[p0:p0 + np_, C_ONES:C_ONES + w]
    b.c_ones2 = cst_b[:, C_ONES2:C_ONES2 + 2]
    b.c_sel2 = cst_b[64:66, C_SEL2:C_SEL2 + 128]
    b.c_onesf = lambda p0, np_, w: onesf[p0:p0 + np_, 0:w]

    b.bg = {k: [] for k in range(8)}

    emit_prologue(b)
    if upto >= 1:
        emit_attn_a(b)
    if upto >= 2:
        emit_ffn6(b, 0)
    if upto >= 3:
        emit_attn_b(b)
    if upto >= 4:
        emit_ffn6(b, 1)
    emit_epilogue(b)
    st.close()
    return nc


def join_all(b, engines):
    for eng in engines:
        for sem in (b.mm, b.ev, b.es, b.ep, b.dW, b.dS):
            if sem.n:
                sem.wait(eng)


def bank_wait(b, bank):
    for sem, val in b.bg[bank]:
        sem.wait(b.TE, val)
    b.bg[bank] = []


def emit_prologue(b):
    SY, VE, GE = b.SY, b.VE, b.GE
    b.dW.inc(GE.dma_start(out=b.cst_b[:, :], in_=b.dr["consts"]), 16)
    b.dW.inc(GE.dma_start(out=b.bvb[:, :], in_=b.dr["bv_row"]), 16)
    b.dS.inc(SY.dma_start(out=b.ba_sb[:, :], in_=b.dr["b_a"]), 16)
    b.dS.inc(SY.dma_start(out=b.gb_sb[:, :], in_=b.dr["g_b"]), 16)
    for bb in range(NB):
        b.dS.inc(SY.dma_start(out=b.xs[:, bb, :, 1:NP],
                              in_=b.dr["x_in"][:, :, bb, :]), 16)
    b.dS.wait(VE)
    b.dW.wait(VE)
    VE.memset(b.onesf[:, :], 1.0)
    for i, val in enumerate([LN_EPS, RMS_EPS, -B_ATTA, -B_LOC, -B_ATTB,
                             1e-24, 0.0, 0.0]):
        VE.memset(b.cbias[:, i:i + 1], float(val))
    b.ev.inc(VE.memset(b.xs[:, :, :, 0:1], 0.0))


def emit_epilogue(b):
    SY = b.SY
    b.ev.wait(SY)
    b.es.wait(SY)
    b.mm.wait(SY)
    b.dW.wait(SY)
    b.ep.wait(SY)
    for bb in range(NB):
        b.dS.inc(SY.dma_start(out=b.dr["x_out"][:, :, bb, :],
                              in_=b.xs[:, bb, :, 1:NP]), 16)
    b.dS.wait(SY)



def ln_chunk(b, srcs, dsts, width, eps_col, lnb, gamma=None, beta=None,
             want_mean=True):
    TE, VE, SE = b.TE, b.VE, b.SE
    b.mm.wait(SE)
    for k in range(KT):
        i = SE.activation(lnb["sq"][:, k, 0:width], srcs[k], AF.Square)
    b.es.inc(i)
    bank_wait(b, 7)
    bank_wait(b, 6)
    if want_mean:
        for k in range(KT):
            TE.matmul(b.ps[0:1, 7, 0:width], b.c_onesf(0, 128, 1), srcs[k],
                      start=(k == 0), stop=(k == KT - 1))
    b.es.wait(TE)
    for k in range(KT):
        i = TE.matmul(b.ps[0:1, 6, 0:width], b.c_ones(0, 128, 1),
                      lnb["sq"][:, k, 0:width],
                      start=(k == 0), stop=(k == KT - 1))
    b.mm.inc(i)
    b.mm.wait(VE)
    ra, rb, rc, rsd, rrec = (lnb[n][0:1, 0:width]
                             for n in ("ra", "rb", "rc", "rsd", "rrec"))
    rstd, mrs = lnb["rstd"][0:1, 0:width], lnb["mrs"][0:1, 0:width]
    if want_mean:
        VE.tensor_scalar(ra, b.ps[0:1, 7, 0:width], 1.0 / D, None, ALU.mult)
        VE.tensor_tensor(rb, ra, ra, ALU.mult)
        VE.tensor_scalar(rc, b.ps[0:1, 6, 0:width], 1.0 / D, None, ALU.mult)
        i = VE.tensor_tensor(rc, rc, rb, ALU.subtract)
    else:
        i = VE.tensor_scalar(rc, b.ps[0:1, 6, 0:width], 1.0 / D, None,
                             ALU.mult)
    b.ev.inc(i)
    b.ev.wait(SE)
    b.es.inc(SE.activation(rsd, rc, AF.Sqrt,
                           bias=b.cbias[0:1, eps_col:eps_col + 1]))
    b.es.wait(VE)
    i = VE.reciprocal(rrec, rsd)
    if want_mean:
        i = VE.tensor_tensor(rb, ra, rrec, ALU.mult)
    b.ev.inc(i)
    b.bg[7] = [(b.ev, b.ev.n)]
    b.bg[6] = [(b.ev, b.ev.n)]
    bank_wait(b, 5)
    bank_wait(b, 4)
    b.ev.wait(TE)
    i = TE.matmul(b.ps[:, 5, 0:width], b.c_onesf(0, 1, 128), rrec,
                  start=True, stop=True)
    if want_mean:
        i = TE.matmul(b.ps[:, 4, 0:width], b.c_onesf(0, 1, 128), rb,
                      start=True, stop=True)
    b.mm.inc(i)
    b.mm.wait(VE)
    for k in range(KT):
        t1 = lnb["t1"][:, 0:width]
        VE.tensor_tensor(t1, srcs[k], b.ps[:, 5, 0:width], ALU.mult)
        if want_mean:
            if gamma is None:
                i = VE.tensor_tensor(dsts[k], t1, b.ps[:, 4, 0:width],
                                     ALU.subtract)
            else:
                t2 = lnb["t2"][:, 0:width]
                VE.tensor_tensor(t2, t1, b.ps[:, 4, 0:width], ALU.subtract)
                i = VE.tensor_scalar(dsts[k], t2, gamma[:, k:k + 1],
                                     beta[:, k:k + 1], ALU.mult, ALU.add)
        else:
            if gamma is None:
                i = VE.tensor_copy(dsts[k], t1)
            else:
                i = VE.tensor_scalar(dsts[k], t1, gamma[:, k:k + 1], None,
                                     ALU.mult)
    b.ev.inc(i)
    b.bg[5] = [(b.ev, b.ev.n)]
    b.bg[4] = [(b.ev, b.ev.n)]


def mk_lnb(b, sq_off, rows_off, t_off, width=512, viewfn=None):
    v = viewfn if viewfn is not None else b.view
    return dict(
        sq=v(sq_off, BF16, KT, width),
        ra=v(rows_off, F32, width, p0=0, np_=1),
        rb=v(rows_off + 2048, F32, width, p0=0, np_=1),
        rc=v(rows_off + 4096, F32, width, p0=0, np_=1),
        rsd=v(rows_off + 6144, F32, width, p0=0, np_=1),
        rrec=v(rows_off + 8192, F32, width, p0=0, np_=1),
        rstd=v(rows_off + 10240, BF16, width, p0=0, np_=1),
        mrs=v(rows_off + 10240 + width * 2, BF16, width, p0=0, np_=1),
        t1=v(t_off, F32, width),
        t2=v(t_off + width * 4, F32, width),
    )



def emit_attn_a(b):
    TE, VE, SE, GE, SY = b.TE, b.VE, b.SE, b.GE, b.SY
    ps, v = b.ps, b.view
    dr = b.dr

    u = v(0, BF16, KT, N)
    ocat = v(0, BF16, KT, N)
    v65 = v(32768, BF16, 16, H, HD + 1)
    lnb = mk_lnb(b, 32768, 45056, 40960)
    qkpan = v(66048, BF16, 2, KT, 128)
    vpan = v(70144, BF16, KT, 256)
    abuf = v(66048, BF16, 2, 1024)
    otmp_e = v(70144, BF16, 512)
    otmp_o = v(70144 + 1024, BF16, 512)
    zinv_e = v(72192, BF16, 512, p0=64, np_=1)
    zinv_o = v(72192 + 1024, BF16, 512, p0=64, np_=1)
    wopan = v(66048, BF16, 2, KT, 128)

    for eng in (TE, VE, SE, GE):
        b.dS.wait(eng)
        b.dW.wait(eng)
        if eng is not VE:
            b.ev.wait(eng)

    for bb in range(NB):
        join_all(b, (TE, VE, SE, GE, SY))
        xb_bf = b.xs[:, bb].rearrange("p k t -> p (k t)").bitcast(
            BF16).rearrange("p (k t) -> p k t", k=KT, t=2 * NP)
        QC0, KC0 = 0, 2050

        b.dS.inc(SY.dma_start(out=dr["x_save"][bb],
                              in_=b.xs[:, bb, :, 1:NP]), 16)
        spill_done = b.dS.n

        b.ev.wait(SE)
        for tc in range(4):
            srcs = [b.xs[:, bb, k, 1 + tc * 512:1 + (tc + 1) * 512]
                    for k in range(KT)]
            dsts = [u[:, k, tc * 512:(tc + 1) * 512] for k in range(KT)]
            ln_chunk(b, srcs, dsts, 512, 0, lnb)
        u_done = b.ev.n

        b.mm.wait(VE)
        b.ev.inc(VE.memset(v65[:, :, :, HD:HD + 1], 1.0))
        ones_done = b.ev.n

        b.ev.wait(SE, u_done)
        pan_free = [(None, 0), (None, 0)]
        for w01, wnm in ((0, "WqT"), (1, "WkT")):
            for jt in range(KT):
                slot = (w01 * KT + jt) % 2
                g = pan_free[slot]
                if g[0] is not None:
                    g[0].wait(GE, g[1])
                b.dW.inc(GE.dma_start(out=qkpan[:, slot], in_=dr[wnm][jt]), 16)
                for tc in range(4):
                    bank = tc % 4
                    bank_wait(b, bank)
                    if tc == 0:
                        b.dW.wait(TE)
                        b.ev.wait(TE, u_done)
                        if w01 == 0 and jt == 0:
                            b.dS.wait(TE, spill_done)
                    for k in range(KT):
                        i = TE.matmul(ps[:, bank, :], qkpan[:, slot, k, :],
                                      u[:, k, tc * 512:(tc + 1) * 512],
                                      start=(k == 0), stop=(k == KT - 1))
                    b.mm.inc(i)
                pan_free[slot] = (b.mm, b.mm.n)
                b.mm.wait(SE)
                b.dS.wait(SE, spill_done)
                base = QC0 if w01 == 0 else KC0
                for tc in range(4):
                    i = SE.activation(
                        xb_bf[:, jt, base + tc * 512:base + (tc + 1) * 512],
                        ps[:, tc % 4, :], AF.Identity,
                        bias=b.ba_sb[:, w01 * KT + jt:w01 * KT + jt + 1])
                    b.es.inc(i)
                for bank in range(4):
                    b.bg[bank] = [(b.es, b.es.n)]
        vp_guard = (None, 0)
        for jc in range(4):
            if vp_guard[0] is not None:
                vp_guard[0].wait(GE, vp_guard[1])
            b.dW.inc(GE.dma_start(out=vpan[:, :], in_=dr["WvT"][jc]), 16)
            b.dW.wait(TE)
            for tt in range(16):
                bank = tt % 4
                bank_wait(b, bank)
                for k in range(KT):
                    TE.matmul(ps[:, bank, 0:256], u[:, k, tt * 128:(tt + 1) * 128],
                              vpan[:, k, :], start=(k == 0), stop=False)
                i = TE.matmul(ps[:, bank, 0:256], b.c_ones(0, 1, 128),
                              b.bvb[0:1, jc * 256:(jc + 1) * 256],
                              start=False, stop=True)
                b.mm.inc(i)
                b.mm.wait(SE)
                i = SE.copy(v65[:, tt, 4 * jc:4 * jc + 4, 0:HD],
                            ps[:, bank, 0:256])
                b.es.inc(i)
                b.bg[bank] = [(b.es, b.es.n)]
            vp_guard = (b.mm, b.mm.n)

        if DEBUG and bb == 0:
            b.ev.wait(SY, u_done)
            b.es.wait(SY)
            b.dS.inc(SY.dma_start(out=dr["d_u"], in_=u[:, :, :]), 16)
            b.dS.inc(SY.dma_start(
                out=dr["d_q"],
                in_=xb_bf[:, :, QC0:QC0 + N]), 16)
            b.dS.inc(SY.dma_start(
                out=dr["d_k"],
                in_=xb_bf[:, :, KC0:KC0 + N]), 16)
            b.dS.inc(SY.dma_start(out=dr["d_v"], in_=v65[:, :, :, :]), 16)
            dump_done = b.dS.n
            b.dS.wait(TE, dump_done)

        b.ev.wait(TE, ones_done)
        for hp in range(KT):
            for ncx in range(4):
                ncs = slice(ncx * 512, (ncx + 1) * 512)
                expc = {}
                def scores(mt):
                    pb = 2 * (mt % 2)
                    bank_wait(b, pb)
                    bank_wait(b, pb + 1)
                    TE.matmul(ps[:, pb, :],
                              xb_bf[0:64, hp, QC0 + mt * 128:QC0 + (mt + 1) * 128],
                              xb_bf[0:64, hp, KC0 + ncx * 512:KC0 + (ncx + 1) * 512],
                              start=True, stop=True)
                    i = TE.matmul(ps[:, pb + 1, :],
                                  xb_bf[64:128, hp, QC0 + mt * 128:QC0 + (mt + 1) * 128],
                                  xb_bf[64:128, hp, KC0 + ncx * 512:KC0 + (ncx + 1) * 512],
                                  start=True, stop=True)
                    b.mm.inc(i)
                    b.mm.wait(SE)
                    g = aslot_guard[mt % 2]
                    if g[0] is not None:
                        g[0].wait(SE, g[1])
                    i = SE.activation(abuf[:, mt % 2], ps[:, pb:pb + 2, :],
                                      AF.Exp, scale=1.0 / 32.0,
                                      bias=b.cbias[:, 2:3])
                    b.es.inc(i)
                    expc[mt] = b.es.n
                    b.bg[pb] = [(b.es, b.es.n)]
                    b.bg[pb + 1] = [(b.es, b.es.n)]

                def av(mt):
                    if mt == 0:
                        bank_wait(b, 4)
                        bank_wait(b, 5)
                    b.es.wait(TE, expc[mt])
                    TE.matmul(ps[0:65, 4, :], v65[:, mt, 2 * hp, :],
                              abuf[:, mt % 2, 0:512],
                              start=(mt == 0), stop=(mt == 15))
                    i = TE.matmul(ps[0:65, 5, :], v65[:, mt, 2 * hp + 1, :],
                                  abuf[:, mt % 2, 512:1024],
                                  start=(mt == 0), stop=(mt == 15))
                    b.mm.inc(i)
                    aslot_guard[mt % 2] = (b.mm, b.mm.n)

                aslot_guard = [(None, 0), (None, 0)]
                scores(0)
                for mt in range(1, 16):
                    scores(mt)
                    av(mt - 1)
                av(15)
                b.mm.wait(VE)
                with b.nc.allow_low_precision(
                        reason="zinv feeds a bf16 matmul operand anyway"):
                    VE.reciprocal(zinv_e[0:1, :], ps[64:65, 4, :])
                    b.ev.inc(VE.reciprocal(zinv_o[0:1, :], ps[64:65, 5, :]))
                bank_wait(b, 6)
                bank_wait(b, 7)
                b.ev.wait(TE)
                TE.matmul(ps[0:64, 6, :], b.c_ones(64, 1, 64), zinv_e[0:1, :],
                          start=True, stop=True)
                i = TE.matmul(ps[0:64, 7, :], b.c_ones(64, 1, 64),
                              zinv_o[0:1, :], start=True, stop=True)
                b.mm.inc(i)
                b.mm.wait(SE)
                SE.copy(otmp_e[0:64, :], ps[0:64, 4, :])
                i = SE.copy(otmp_o[0:64, :], ps[0:64, 5, :])
                b.es.inc(i)
                b.bg[5] = [(b.es, b.es.n)]
                b.es.wait(VE)
                VE.tensor_tensor(ocat[0:64, hp, ncs], otmp_e[0:64, :],
                                 ps[0:64, 6, :], ALU.mult)
                i = VE.tensor_tensor(abuf[0:64, 0, 0:512], otmp_o[0:64, :],
                                     ps[0:64, 7, :], ALU.mult)
                b.ev.inc(i)
                b.bg[6] = [(b.ev, b.ev.n)]
                b.bg[7] = [(b.ev, b.ev.n)]
                bank_wait(b, 4)
                b.ev.wait(TE)
                i = TE.matmul(ps[:, 4, :], b.c_SHIFT, abuf[0:64, 0, 0:512],
                              start=True, stop=True)
                b.mm.inc(i)
                b.mm.wait(SE)
                i = SE.copy(ocat[64:128, hp, ncs], ps[64:128, 4, :])
                b.es.inc(i)
                b.bg[4] = [(b.es, b.es.n)]

        ocat_done = b.es.n
        b.es.wait(GE)
        b.ev.wait(GE)
        wo_guard = [(None, 0), (None, 0)]
        for ft in range(KT):
            slot = ft % 2
            g = wo_guard[slot]
            if g[0] is not None:
                g[0].wait(GE, g[1])
            b.dW.inc(GE.dma_start(out=wopan[:, slot], in_=dr["WoT"][ft]), 16)
            b.dW.wait(TE)
            if ft == 0:
                b.es.wait(TE, ocat_done)
            for tc in range(4):
                bank = tc % 4
                bank_wait(b, bank)
                for jt in range(KT):
                    i = TE.matmul(ps[:, bank, :], wopan[:, slot, jt, :],
                                  ocat[:, jt, tc * 512:(tc + 1) * 512],
                                  start=(jt == 0), stop=(jt == KT - 1))
                b.mm.inc(i)
            wo_guard[slot] = (b.mm, b.mm.n)
            b.mm.wait(SE)
            for tc in range(4):
                i = SE.activation(
                    b.xs[:, bb, ft, 1 + tc * 512:1 + (tc + 1) * 512],
                    ps[:, tc % 4, :], AF.Identity,
                    bias=b.ba_sb[:, 3 * KT + ft:3 * KT + ft + 1])
                b.es.inc(i)
                b.bg[tc % 4] = [(b.es, b.es.n)]
        b.es.wait(GE)
        b.dW.inc(GE.dma_start(out=b.xs[:, bb, :, 1:NP], in_=dr["x_save"][bb],
                              accum_op=ALU.add), 16)
        b.dW.wait(GE)


def emit_ffn6(b, inv):
    TE, VE, SE, GE, SY = b.TE, b.VE, b.SE, b.GE, b.SY
    ps, v = b.ps, b.view
    dr = b.dr
    sm = b.smalls

    join_all(b, (TE, VE, SE, GE, SY))
    wq_fl, w1_fl, w2_fl = dr["Wqkv"], dr["W1p"], dr["W2p"]
    for bb in range(NB):
        b.dS.inc(SY.dma_start(out=dr["x_save"][bb],
                              in_=b.xs[:, bb, :, 1:NP]), 16)

    for bb in range(NB):
        join_all(b, (TE, VE, SE, GE, SY))

        def rv(off, dtype, *shape, p0=0, np_=128):
            esz = 4 if dtype == F32 else 2
            nby = int(np.prod(shape)) * esz
            assert off + nby <= KT * NP * 4, (off, nby)
            ap = b.xs[p0:p0 + np_, 1 - bb].rearrange(
                "p k t -> p (k t)").bitcast(mybir.dt.uint8)
            ap = ap[:, off:off + nby].bitcast(dtype)
            if len(shape) > 1:
                names = " ".join(f"s{i}" for i in range(len(shape)))
                ap = ap.rearrange(f"p ({names}) -> p {names}",
                                  **{f"s{i}": s for i, s in enumerate(shape)})
            return ap

        zc = lambda k, lo, n: b.xs[:, bb, k, lo:lo + n]

        if bb == 1:
            b.dS.inc(SY.dma_start(out=b.xs[:, 1, :, 1:NP],
                                  in_=dr["x_save"][1]), 16)
        b.dS.wait(SE)
        b.dS.wait(TE)
        b.dS.wait(VE)
        VE.memset(b.xs[:, bb, :, 0:1], 0.0)
        lnb0 = mk_lnb(b, 0, 12288, 8192)
        for tc in range(4):
            srcs = [zc(k, 1 + tc * 512, 512) for k in range(KT)]
            ln_chunk(b, srcs, srcs, 512, 0, lnb0)

        y_bf = v(0, BF16, KT, 1024)
        ocat = v(0, BF16, KT, 1024)
        qf = v(16384, BF16, KT, 1024)
        kf = v(32768, BF16, KT, 1280)
        abufs = [[v(53248, BF16, 2, 384), v(54784, BF16, 2, 384)],
                 [v(56320, BF16, 2, 384), v(57856, BF16, 2, 384)]]
        otmps = [v(59392, BF16, 512), v(60416, BF16, 512)]
        zinv2s = [v(61440, BF16, 2, 256, p0=64, np_=1),
                  v(62464, BF16, 2, 256, p0=64, np_=1)]
        oddts = [v(63488, BF16, 512), v(64512, BF16, 512)]
        rope_bufs = (
            [v(65536, BF16, 512), v(66560, BF16, 512)],
            [v(67584, F32, 512, p0=64, np_=2),
             v(69632, F32, 512, p0=64, np_=2)],
            [v(71680, BF16, 512, p0=64, np_=2),
             v(72704, BF16, 512, p0=64, np_=2)],
            [v(53248, BF16, 512), v(54272, BF16, 512)],
            v(55296, BF16, 512),
            v(56320, BF16, 512),
        )
        m1t = v(55296, BF16, 512)
        m2t = v(56320, BF16, 512)
        at1 = v(53248, BF16, 512)
        v65 = rv(0, BF16, 10, H, HD + 1)
        qkvpan = rv(20800, BF16, 2, KT, 512)
        wo_sb = rv(37184, BF16, KT, 1024)
        tabs = rv(53568, BF16, 2, 4, 512)
        xn = v(0, BF16, KT, 2048)
        w1pan = v(32768, BF16, 2, 2, KT, 128)
        hout = v(40960, BF16, 2, 2048)
        ff_tmp = [v(49152 + i * 2048, F32, 512) for i in range(4)]
        lnb1 = mk_lnb(b, 45056, 53248, 57344, viewfn=rv)
        lnb1["t1"] = v(57344, F32, 512)
        lnb1["t2"] = v(59392, F32, 512)
        at2 = v(64512, BF16, 512)
        hin = [v(0, BF16, FT1, 512), v(22528, BF16, FT1, 512)]
        w2all = rv(0, BF16, KT, FT1, 128)

        b.g1tmp_guard = [(None, 0), (None, 0)]
        b.fftmp_guard = [(None, 0), (None, 0)]
        for lyr in range(FFN_DEPTH):
            b.es.wait(SY)
            b.ev.wait(SY)
            b.dS.inc(SY.dma_start(out=sm[:, :], in_=dr["small"][lyr]), 16)
            smalls_ld = b.dS.n

            for c in range(2):
                T0 = c * 1024
                b.ev.wait(GE)
                b.mm.wait(GE)
                for k in range(KT):
                    srcz = zc(k, 1 + T0, 1024) if k < 4 else zc(k, T0, 1024)
                    i = GE.tensor_copy(y_bf[:, k, :], srcz)
                b.ep.inc(i)
                for hc in range(2):
                    b.dW.inc(GE.dma_start(
                        out=tabs[:, hc],
                        in_=dr["xpos"][:, :, T0 + hc * 512:T0 + (hc + 1) * 512]
                        .rearrange("a p t -> p a t")), 16)
                rp = RopePipe(b, qf, kf, tabs, rope_bufs, qkvpan,
                              [wq_fl[lyr * 6 + jc] for jc in range(4)],
                              smalls_ld)
                rp.run(y_bf)
                for jc in (4, 5):
                    slot = jc % 2
                    b.mm.wait(GE)
                    b.dW.inc(GE.dma_start(out=qkvpan[:, slot],
                                          in_=wq_fl[lyr * 6 + jc],
                                          max_dma_last_dim=8192), 16)
                    b.dW.wait(TE)
                    for tt in range(8):
                        bank = tt % 2
                        bank_wait(b, bank)
                        for k in range(KT):
                            i = TE.matmul(
                                ps[:, bank, :],
                                y_bf[:, k, tt * 128:(tt + 1) * 128],
                                qkvpan[:, slot, k, :],
                                start=(k == 0), stop=(k == KT - 1))
                        b.mm.inc(i)
                        b.mm.wait(SE)
                        i = SE.copy(
                            v65[:, 2 + tt, (jc - 4) * 8:(jc - 4) * 8 + 8, 0:HD],
                            ps[:, bank, :])
                        b.es.inc(i)
                        b.bg[bank] = [(b.es, b.es.n)]
                b.es.wait(VE)
                b.ev.inc(VE.memset(v65[:, :, :, HD:HD + 1], 1.0))
                ones_done = b.ev.n

                b.ev.wait(TE, ones_done)
                units = [(hp, w) for hp in range(KT) for w in range(4)]
                prev = None
                for idx, (hp, w) in enumerate(units):
                    first = (c == 0 and w == 0)
                    mts = (2, 3) if first else (2, 0, 1, 3)
                    par = idx % 2
                    g = la_stage1(b, hp, w, mts, qf, kf, abufs[par])
                    if prev is not None:
                        la_stage2(b, *prev)
                    prev = (units[idx][0], units[idx][1], mts, par,
                            abufs[par], g, v65, zinv2s[par], otmps[par],
                            oddts[par], ocat)
                la_stage2(b, *prev)
                if c == 0:
                    b.dW.inc(GE.dma_start(out=wo_sb[:, :],
                                          in_=dr["Wo_f"][lyr],
                                          max_dma_last_dim=8192), 16)
                b.dW.wait(TE)
                for hc in range(2):
                    for ft in range(KT):
                        bank = (hc * KT + ft) % 4
                        bank_wait(b, bank)
                        for jt in range(KT):
                            i = TE.matmul(
                                ps[:, bank, :],
                                wo_sb[:, jt, ft * 128:(ft + 1) * 128],
                                ocat[:, jt, hc * 512:(hc + 1) * 512],
                                start=(jt == 0), stop=(jt == KT - 1))
                        b.mm.inc(i)
                        b.mm.wait(VE)
                        i = VE.tensor_tensor(
                            zc(ft, 1 + T0 + hc * 512, 512),
                            zc(ft, 1 + T0 + hc * 512, 512),
                            ps[:, bank, :], ALU.add)
                        b.ev.inc(i)
                        b.bg[bank] = [(b.ev, b.ev.n)]
                b.ev.wait(GE)
                b.mm.wait(GE)
                GE.tensor_copy(kf[:, :, 0:256], kf[:, :, 1024:1280])
                i = GE.tensor_copy(v65[:, 0:2, :, :], v65[:, 8:10, :, :])
                b.ep.inc(i)
                b.ep.wait(TE)
                b.ep.wait(SE)

            if FFN_ATTN_ONLY:
                continue
            b.mm.wait(VE)
            b.mm.wait(SE)
            for tc in range(4):
                srcs = [zc(k, 1 + tc * 512, 512) if k < 4
                        else zc(k, tc * 512, 512) for k in range(KT)]
                dsts = [xn[:, k, tc * 512:(tc + 1) * 512] for k in range(KT)]
                ln_chunk(b, srcs, dsts, 512, 0, lnb1,
                         gamma=sm[:, 2:10], beta=sm[:, 10:18])
            xn_done = b.ev.n
            b.mm.wait(GE)
            b.ev.wait(GE)
            w1_guard = [(None, 0), (None, 0)]
            ho_guard = [(None, 0), (None, 0)]
            for fp in range(FT1):
                slot = fp % 2
                g = w1_guard[slot]
                if g[0] is not None:
                    g[0].wait(GE, g[1])
                b.dW.inc(GE.dma_start(out=w1pan[:, slot],
                                      in_=w1_fl[lyr * FT1 + fp]), 16)
                b.dW.wait(TE)
                if fp == 0:
                    b.ev.wait(TE, xn_done)
                g = ho_guard[slot]
                if g[0] is not None:
                    g[0].wait(VE, g[1])
                for ts in range(4):
                    ba = (ts % 2) * 2
                    bank_wait(b, ba)
                    bank_wait(b, ba + 1)
                    for k in range(KT):
                        TE.matmul(ps[:, ba, :], w1pan[:, slot, 0, k, :],
                                  xn[:, k, ts * 512:(ts + 1) * 512],
                                  start=(k == 0), stop=(k == KT - 1))
                    for k in range(KT):
                        i = TE.matmul(ps[:, ba + 1, :], w1pan[:, slot, 1, k, :],
                                      xn[:, k, ts * 512:(ts + 1) * 512],
                                      start=(k == 0), stop=(k == KT - 1))
                    b.mm.inc(i)
                    b.mm.wait(SE)
                    gt = m1t if ts % 2 == 0 else m2t
                    at = at1 if ts % 2 == 0 else at2
                    g = b.g1tmp_guard[ts % 2]
                    if g[0] is not None:
                        g[0].wait(SE, g[1])
                    SE.activation(gt, ps[:, ba + 1, :], AF.Gelu,
                                  bias=sm[:, 40 + fp:41 + fp])
                    i = SE.activation(at, ps[:, ba, :], AF.Identity,
                                      bias=sm[:, 18 + fp:19 + fp])
                    b.es.inc(i)
                    b.es.wait(VE)
                    i = VE.tensor_tensor(
                        hout[:, slot, ts * 512:(ts + 1) * 512], at, gt,
                        ALU.mult)
                    b.ev.inc(i)
                    b.g1tmp_guard[ts % 2] = (b.ev, b.ev.n)
                    b.bg[ba] = [(b.ev, b.ev.n)]
                    b.bg[ba + 1] = [(b.ev, b.ev.n)]
                w1_guard[slot] = (b.mm, b.mm.n)
                b.ev.wait(SY)
                b.dS.inc(SY.dma_start(out=dr["h_sc"][:, fp, :],
                                      in_=hout[:, slot, :]), 16)
                ho_guard[slot] = (b.dS, b.dS.n)
            b.mm.wait(GE)
            b.ev.wait(GE)
            for ft in range(KT):
                b.dW.inc(GE.dma_start(out=w2all[:, ft], in_=w2_fl[lyr * KT + ft],
                                      max_dma_last_dim=8192), 16)
            w2_ld = b.dW.n
            hin_guard = [(None, 0), (None, 0)]
            for ts in range(4):
                slot = ts % 2
                g = hin_guard[slot]
                if g[0] is not None:
                    g[0].wait(SY, g[1])
                b.dS.inc(SY.dma_start(out=hin[slot],
                                      in_=dr["h_sc"][:, :, ts * 512:(ts + 1) * 512]), 16)
                b.dS.wait(TE)
                b.dW.wait(TE, w2_ld)
                for ft in range(KT):
                    bank = 4 + (ft % 2)
                    bank_wait(b, bank)
                    for kt in range(FT1):
                        i = TE.matmul(ps[:, bank, :], w2all[:, ft, kt, :],
                                      hin[slot][:, kt, :],
                                      start=(kt == 0), stop=(kt == FT1 - 1))
                    b.mm.inc(i)
                    b.mm.wait(SE)
                    stt = ff_tmp[ft % 2]
                    g = b.fftmp_guard[ft % 2]
                    if g[0] is not None:
                        g[0].wait(SE, g[1])
                    i = SE.activation(stt, ps[:, bank, :], AF.Identity,
                                      bias=sm[:, 62 + ft:63 + ft])
                    b.es.inc(i)
                    b.es.wait(VE)
                    i = VE.tensor_tensor(zc(ft, 1 + ts * 512, 512),
                                         zc(ft, 1 + ts * 512, 512), stt,
                                         ALU.add)
                    b.ev.inc(i)
                    b.fftmp_guard[ft % 2] = (b.ev, b.ev.n)
                    b.bg[bank] = [(b.ev, b.ev.n)]
                hin_guard[slot] = (b.mm, b.mm.n)

        join_all(b, (GE, SY))
        if bb == 0:
            b.dW.inc(GE.dma_start(out=dr["x_save"][0],
                                  in_=b.xs[:, 0, :, 1:NP],
                                  accum_op=ALU.add), 16)
        else:
            b.dW.inc(GE.dma_start(out=b.xs[:, 1, :, 1:NP],
                                  in_=dr["x_save"][1], accum_op=ALU.add), 16)
            b.dW.wait(GE)
    join_all(b, (SY,))
    b.dS.inc(SY.dma_start(out=b.xs[:, 0, :, 1:NP], in_=dr["x_save"][0]), 16)
    b.dS.wait(SY)


class RopePipe:

    def __init__(self, b, qf, kf, tabs, bufs, qkvpan, wq_src, smalls_ld):
        self.b = b
        self.qf, self.kf, self.tabs = qf, kf, tabs
        self.sqt, self.r_f32, self.r_bf, self.rsb, self.m1, self.m2 = bufs
        self.qkvpan = qkvpan
        self.wq_src = wq_src
        self.smalls_ld = smalls_ld
        self.cnt = {}
        self.pan_guard = {}

    def unit(self, u):
        jc, rest = divmod(u, 8)
        jsub, hc = divmod(rest, 2)
        isq = jc < 2
        jt = (jc % 2) * 4 + jsub
        return jc, jsub, hc, isq, jt

    def dst(self, u):
        jc, jsub, hc, isq, jt = self.unit(u)
        if isq:
            return self.qf[:, jt, hc * 512:(hc + 1) * 512]
        return self.kf[:, jt, 256 + hc * 512:256 + (hc + 1) * 512]

    def s0(self, u):
        b = self.b
        TE, GE = b.TE, b.GE
        jc, jsub, hc, isq, jt = self.unit(u)
        if jsub == 0 and hc == 0:
            slot = jc % 2
            g = self.pan_guard.get(slot)
            if g is not None:
                g[0].wait(GE, g[1])
            b.dW.inc(GE.dma_start(out=self.qkvpan[:, slot],
                                  in_=self.wq_src[jc],
                                  max_dma_last_dim=8192), 16)
            self.cnt[("pan", jc)] = b.dW.n
        bank = u % 4
        bank_wait(b, bank)
        b.dW.wait(TE, self.cnt[("pan", jc)])
        b.ep.wait(TE)
        slot = jc % 2
        for k in range(KT):
            i = TE.matmul(b.ps[:, bank, :],
                          self.qkvpan[:, slot, k, jsub * 128:(jsub + 1) * 128],
                          self.ybf[:, k, hc * 512:(hc + 1) * 512],
                          start=(k == 0), stop=(k == KT - 1))
        b.mm.inc(i)
        self.cnt[("s0", u)] = b.mm.n
        if jsub == 3 and hc == 1:
            self.pan_guard[(jc + 1) % 2] = (b.mm, b.mm.n)
        SE = b.SE
        b.mm.wait(SE)
        b.ev.wait(SE)
        i = SE.activation(self.sqt[u % 2], b.ps[:, bank, :], AF.Square)
        b.es.inc(i)
        self.cnt[("sq", u)] = b.es.n

    def ss(self, u):
        b = self.b
        TE, SE, VE = b.TE, b.SE, b.VE
        bank_wait(b, 6)
        b.es.wait(TE, self.cnt[("sq", u)])
        i = TE.matmul(b.ps[64:66, 6, :], b.c_ones2, self.sqt[u % 2],
                      start=True, stop=True)
        b.mm.inc(i)
        b.mm.wait(SE)
        b.ev.wait(SE)
        i = SE.activation(self.r_f32[u % 2][0:2, :], b.ps[64:66, 6, :],
                          AF.Sqrt, bias=b.cbias[64:66, 5:6])
        b.es.inc(i)
        b.bg[6] = [(b.es, b.es.n)]
        b.es.wait(VE)
        with b.nc.allow_low_precision(reason="rnorm feeds bf16 matmul"):
            i = VE.reciprocal(self.r_bf[u % 2][0:2, :],
                              self.r_f32[u % 2][0:2, :])
        b.ev.inc(i)
        self.cnt[("rc", u)] = b.ev.n

    def bc(self, u):
        b = self.b
        TE, VE = b.TE, b.VE
        bank_wait(b, 7)
        b.ev.wait(TE, self.cnt[("rc", u)])
        i = TE.matmul(b.ps[:, 7, :], b.c_sel2, self.r_bf[u % 2][0:2, :],
                      start=True, stop=True)
        b.mm.inc(i)
        b.mm.wait(VE)
        b.dS.wait(VE, self.smalls_ld)
        jc, jsub, hc, isq, jt = self.unit(u)
        col = 0 if isq else 1
        VE.tensor_scalar(self.rsb[u % 2], b.ps[:, 7, :],
                         b.smalls[:, col:col + 1], None, ALU.mult)
        i = VE.tensor_tensor(self.dst(u), b.ps[:, u % 4, :], self.rsb[u % 2],
                             ALU.mult)
        b.ev.inc(i)
        b.bg[7] = [(b.ev, b.ev.n)]
        b.bg[u % 4] = [(b.ev, b.ev.n)]
        self.cnt[("ev", u)] = b.ev.n

    def pm(self, u):
        b = self.b
        TE, VE = b.TE, b.VE
        bank_wait(b, 5)
        b.ev.wait(TE, self.cnt[("ev", u)])
        d = self.dst(u)
        i = TE.matmul(b.ps[:, 5, :], b.c_PH, d, start=True, stop=True)
        b.mm.inc(i)
        jc, jsub, hc, isq, jt = self.unit(u)
        ct = self.tabs[:, hc, 0 if isq else 2, :]
        st = self.tabs[:, hc, 1 if isq else 3, :]
        b.dW.wait(VE)
        VE.tensor_tensor(self.m1, d, ct, ALU.mult)
        b.mm.wait(VE)
        VE.tensor_tensor(self.m2, b.ps[:, 5, :], st, ALU.mult)
        i = VE.tensor_tensor(d, self.m1, self.m2, ALU.add)
        b.ev.inc(i)
        b.bg[5] = [(b.ev, b.ev.n)]

    def run(self, ybf):
        self.ybf = ybf
        NU = 32
        for u in range(NU):
            self.s0(u)
            if u >= 1:
                self.ss(u - 1)
            if u >= 2:
                self.bc(u - 2)
            if u >= 3:
                self.pm(u - 3)
        self.ss(NU - 1)
        self.bc(NU - 2)
        self.pm(NU - 3)
        self.bc(NU - 1)
        self.pm(NU - 2)
        self.pm(NU - 1)


def la_stage1(b, hp, w, mts, qf, kf, abuf):
    TE, VE, SE = b.TE, b.VE, b.SE
    ps = b.ps
    def sc_slot(mt):
        if mt == 0:
            return 0, 0, 128, 0, 128
        if mt == 1:
            return 0, 128, 384, 0, 256
        if mt == 2:
            return 1, 0, 256, 0, 256
        return 1, 256, 384, 128, 128
    for bk in range(4):
        bank_wait(b, bk)
    p0 = lambda h01: h01 * 64
    qs = lambda h01, n0, nw: qf[p0(h01):p0(h01) + 64, hp,
                                w * 256 + n0:w * 256 + n0 + nw]
    ks = lambda h01, mt: kf[p0(h01):p0(h01) + 64, hp,
                            w * 256 + mt * 128:w * 256 + (mt + 1) * 128]
    for mt in mts:
        sb, c0, c1, n0, nw = sc_slot(mt)
        TE.matmul(ps[:, sb, c0:c1], ks(0, mt), qs(0, n0, nw),
                  start=True, stop=True)
        i = TE.matmul(ps[:, 2 + sb, c0:c1], ks(1, mt), qs(1, n0, nw),
                      start=True, stop=True)
    b.mm.inc(i)
    b.mm.wait(SE)
    b.ev.wait(SE)
    sbs = (0, 1) if 0 in mts else (1,)
    for h01 in (0, 1):
        for sb in sbs:
            i = SE.activation(abuf[h01][:, sb, 0:384],
                              ps[:, 2 * h01 + sb, 0:384], AF.Exp, scale=8.0,
                              bias=b.cbias[:, 3:4])
    b.es.inc(i)
    for bk in range(4):
        b.bg[bk] = [(b.es, b.es.n)]
    b.es.wait(VE)
    for h01 in (0, 1):
        a = abuf[h01]
        if 0 in mts:
            VE.tensor_tensor(a[:, 0, 0:128], a[:, 0, 0:128], b.c_tria,
                             ALU.mult)
            VE.tensor_tensor(a[:, 0, 256:384], a[:, 0, 256:384], b.c_tria,
                             ALU.mult)
        VE.tensor_tensor(a[:, 1, 0:128], a[:, 1, 0:128], b.c_tric, ALU.mult)
        i = VE.tensor_tensor(a[:, 1, 256:384], a[:, 1, 256:384], b.c_tric,
                             ALU.mult)
    b.ev.inc(i)
    return (b.es.n, b.ev.n)


def la_stage2(b, hp, w, mts, par, abuf, guards, v65, zinv2, otmp, oddt, ocat):
    TE, VE, SE = b.TE, b.VE, b.SE
    ps = b.ps
    es_a, ev_a = guards
    amap = {0: (0, 0, 128, 0, 128), 1: (0, 128, 384, 0, 256),
            2: (1, 0, 256, 0, 256), 3: (1, 256, 384, 128, 128)}
    bank_wait(b, 4)
    bank_wait(b, 5)
    b.es.wait(TE, es_a)
    b.ev.wait(TE, ev_a)
    for h01 in (0, 1):
        for n_mt, mt in enumerate(mts):
            sb, c0, c1, n0, nw = amap[mt]
            i = TE.matmul(ps[0:65, 4 + h01, n0:n0 + nw],
                          v65[:, 2 * w + mt, 2 * hp + h01, :],
                          abuf[h01][:, sb, c0:c1],
                          start=(n_mt == 0), stop=(n_mt == len(mts) - 1),
                          skip_group_check=True)
    b.mm.inc(i)
    b.mm.wait(VE)
    with b.nc.allow_low_precision(reason="zinv bf16"):
        VE.reciprocal(zinv2[0:1, 0, :], ps[64:65, 4, 0:256])
        i = VE.reciprocal(zinv2[0:1, 1, :], ps[64:65, 5, 0:256])
    b.ev.inc(i)
    bank_wait(b, 6)
    b.ev.wait(TE)
    TE.matmul(ps[0:64, 6, 0:256], b.c_ones(64, 1, 64), zinv2[0:1, 0, :],
              start=True, stop=True)
    i = TE.matmul(ps[0:64, 6, 256:512], b.c_ones(64, 1, 64), zinv2[0:1, 1, :],
                  start=True, stop=True, skip_group_check=True)
    b.mm.inc(i)
    b.mm.wait(SE)
    b.ev.wait(SE)
    SE.copy(otmp[0:64, 0:512][:, 0:256], ps[0:64, 4, 0:256])
    i = SE.copy(otmp[0:64, 256:512], ps[0:64, 5, 0:256])
    b.es.inc(i)
    b.bg[4] = [(b.es, b.es.n)]
    b.bg[5] = [(b.es, b.es.n)]
    b.es.wait(VE)
    VE.tensor_tensor(ocat[0:64, hp, w * 256:(w + 1) * 256], otmp[0:64, 0:256],
                     ps[0:64, 6, 0:256], ALU.mult)
    i = VE.tensor_tensor(oddt[0:64, 0:256], otmp[0:64, 256:512],
                         ps[0:64, 6, 256:512], ALU.mult)
    b.ev.inc(i)
    b.bg[6] = [(b.ev, b.ev.n)]
    bank_wait(b, 7)
    b.ev.wait(TE)
    i = TE.matmul(ps[:, 7, 0:256], b.c_SHIFT, oddt[0:64, 0:256],
                  start=True, stop=True)
    b.mm.inc(i)
    b.mm.wait(SE)
    i = SE.copy(ocat[64:128, hp, w * 256:(w + 1) * 256], ps[64:128, 7, 0:256])
    b.es.inc(i)
    b.bg[7] = [(b.es, b.es.n)]


def emit_attn_b(b):
    TE, VE, SE, GE, SY = b.TE, b.VE, b.SE, b.GE, b.SY
    ps, v = b.ps, b.view
    dr = b.dr

    join_all(b, (TE, VE, SE, GE, SY))

    b.dW.inc(GE.dma_start(out=b.pmk_b[:, :, :], in_=dr["pmk"]), 16)
    b.dW.inc(GE.dma_start(out=b.pmv_b[:, :, :], in_=dr["pmv"]), 16)

    u = v(0, BF16, KT, 512)
    qseg = v(8192, BF16, KT, 512)
    kseg = v(16384, BF16, KT, 512)
    v65b = v(24576, BF16, 4, H, HD + 1)
    ocat = v(32896, BF16, KT, 512)
    abuf = [v(41088, BF16, 512), v(42112, BF16, 512)]
    apm_e = v(43136, BF16, 512, p0=0, np_=16)
    apm_o = v(44160, BF16, 512, p0=0, np_=16)
    otmp_e = v(45184, BF16, 512)
    otmp_o = v(46208, BF16, 512)
    zinv_e = v(47232, BF16, 512, p0=64, np_=1)
    zinv_o = v(48256, BF16, 512, p0=64, np_=1)
    tabs_b = v(49280, BF16, 2, 512)
    m1t = v(51328, BF16, 512)
    m2t = v(52352, BF16, 512)
    wpan = v(53376, BF16, 2, KT, 512)
    lnb = mk_lnb(b, 24576, 28800, 41088)

    for bb in range(NB):
        for s in range(4):
            S0 = s * 512
            join_all(b, (GE, SY))
            srcs = [b.xs[:, bb, k, 1 + S0:1 + S0 + 512] for k in range(KT)]
            dsts = [u[:, k, :] for k in range(KT)]
            ln_chunk(b, srcs, dsts, 512, 1, lnb, gamma=b.gb_sb,
                     want_mean=False)
            u_done = b.ev.n
            b.ev.wait(GE)
            b.dW.inc(GE.dma_start(
                out=tabs_b,
                in_=dr["ropeb"][:, :, S0:S0 + 512].rearrange(
                    "a p t -> p a t")), 16)
            b.ev.wait(SE, u_done)
            for jc in range(6):
                slot = jc % 2
                if jc >= 2:
                    b.mm.wait(GE)
                b.dW.inc(GE.dma_start(out=wpan[:, slot],
                                      in_=dr["wqkvT"][jc],
                                      max_dma_last_dim=8192), 16)
                b.dW.wait(TE)
                b.ev.wait(TE, u_done)
                if jc < 4:
                    isq = jc < 2
                    dstt = qseg if isq else kseg
                    for jsub in range(4):
                        jt = (jc % 2) * 4 + jsub
                        bank = jsub % 2
                        bank_wait(b, bank)
                        for k in range(KT):
                            i = TE.matmul(
                                ps[:, bank, :],
                                wpan[:, slot, k, jsub * 128:(jsub + 1) * 128],
                                u[:, k, :], start=(k == 0), stop=(k == KT - 1))
                        b.mm.inc(i)
                        dsl = dstt[:, jt, :]
                        b.mm.wait(SE)
                        b.ev.wait(SE)
                        i = SE.copy(dsl, ps[:, bank, :])
                        b.es.inc(i)
                        b.bg[bank] = [(b.es, b.es.n)]
                        bank_wait(b, 7)
                        b.es.wait(TE)
                        i = TE.matmul(ps[:, 7, :], b.c_PI, dsl,
                                      start=True, stop=True)
                        b.mm.inc(i)
                        b.dW.wait(VE)
                        b.es.wait(VE)
                        VE.tensor_tensor(m1t, dsl, tabs_b[:, 0, :], ALU.mult)
                        b.mm.wait(VE)
                        VE.tensor_tensor(m2t, ps[:, 7, :], tabs_b[:, 1, :],
                                         ALU.mult)
                        i = VE.tensor_tensor(dsl, m1t, m2t, ALU.add)
                        b.ev.inc(i)
                        b.bg[7] = [(b.ev, b.ev.n)]
                else:
                    for tt in range(4):
                        bank = tt % 2
                        bank_wait(b, bank)
                        for k in range(KT):
                            i = TE.matmul(
                                ps[:, bank, :],
                                u[:, k, tt * 128:(tt + 1) * 128],
                                wpan[:, slot, k, :],
                                start=(k == 0), stop=(k == KT - 1))
                        b.mm.inc(i)
                        b.mm.wait(SE)
                        i = SE.copy(
                            v65b[:, tt, (jc - 4) * 8:(jc - 4) * 8 + 8, 0:HD],
                            ps[:, bank, :])
                        b.es.inc(i)
                        b.bg[bank] = [(b.es, b.es.n)]
            b.es.wait(VE)
            b.ev.inc(VE.memset(v65b[:, :, :, HD:HD + 1], 1.0))
            ones_done = b.ev.n
            b.ev.wait(TE, ones_done)
            b.dW.wait(TE)

            for hp in range(KT):
                for bk in (0, 1, 2, 3):
                    bank_wait(b, bk)
                qsl = lambda h01, c0, cw: qseg[h01 * 64:h01 * 64 + 64, hp,
                                               c0:c0 + cw]
                i = TE.matmul(ps[0:16, 0, :], b.pmk_b[0:64, hp, :],
                              qsl(0, 0, 512), start=True, stop=True)
                i = TE.matmul(ps[0:16, 1, :], b.pmk_b[64:128, hp, :],
                              qsl(1, 0, 512), start=True, stop=True)
                b.mm.inc(i)
                b.mm.wait(SE)
                b.ev.wait(SE)
                i = SE.activation(apm_e[0:16, :], ps[0:16, 0, :], AF.Exp,
                                  scale=0.125, bias=b.cbias[0:16, 4:5])
                i = SE.activation(apm_o[0:16, :], ps[0:16, 1, :], AF.Exp,
                                  scale=0.125, bias=b.cbias[0:16, 4:5])
                b.es.inc(i)
                b.bg[0] = [(b.es, b.es.n)]
                b.bg[1] = [(b.es, b.es.n)]
                b.es.wait(TE)
                TE.matmul(ps[0:65, 2, :], b.pmv_b[:, 2 * hp, :],
                          apm_e[0:16, :], start=True, stop=False,
                          skip_group_check=True)
                i = TE.matmul(ps[0:65, 3, :], b.pmv_b[:, 2 * hp + 1, :],
                              apm_o[0:16, :], start=True, stop=False,
                              skip_group_check=True)
                b.mm.inc(i)
                for kt in range(4):
                    cw = 512 - kt * 128
                    bank_wait(b, 0)
                    bank_wait(b, 1)
                    ksl = lambda h01: kseg[h01 * 64:h01 * 64 + 64, hp,
                                           kt * 128:(kt + 1) * 128]
                    i = TE.matmul(ps[:, 0, kt * 128:512], ksl(0),
                                  qsl(0, kt * 128, cw), start=True, stop=True)
                    i = TE.matmul(ps[:, 1, kt * 128:512], ksl(1),
                                  qsl(1, kt * 128, cw), start=True, stop=True)
                    b.mm.inc(i)
                    b.mm.wait(SE)
                    b.ev.wait(SE)
                    i = SE.activation(abuf[0][:, kt * 128:512],
                                      ps[:, 0, kt * 128:512], AF.Exp,
                                      scale=0.125, bias=b.cbias[:, 4:5])
                    i = SE.activation(abuf[1][:, kt * 128:512],
                                      ps[:, 1, kt * 128:512], AF.Exp,
                                      scale=0.125, bias=b.cbias[:, 4:5])
                    b.es.inc(i)
                    b.bg[0] = [(b.es, b.es.n)]
                    b.bg[1] = [(b.es, b.es.n)]
                    b.es.wait(VE)
                    VE.tensor_tensor(abuf[0][:, kt * 128:(kt + 1) * 128],
                                     abuf[0][:, kt * 128:(kt + 1) * 128],
                                     b.c_tric, ALU.mult)
                    i = VE.tensor_tensor(abuf[1][:, kt * 128:(kt + 1) * 128],
                                         abuf[1][:, kt * 128:(kt + 1) * 128],
                                         b.c_tric, ALU.mult)
                    b.ev.inc(i)
                    b.ev.wait(TE)
                    TE.matmul(ps[0:65, 2, kt * 128:512],
                              v65b[:, kt, 2 * hp, :], abuf[0][:, kt * 128:512],
                              start=False, stop=(kt == 3),
                              skip_group_check=True)
                    i = TE.matmul(ps[0:65, 3, kt * 128:512],
                                  v65b[:, kt, 2 * hp + 1, :],
                                  abuf[1][:, kt * 128:512],
                                  start=False, stop=(kt == 3),
                                  skip_group_check=True)
                    b.mm.inc(i)
                b.mm.wait(VE)
                with b.nc.allow_low_precision(reason="zinv bf16"):
                    VE.reciprocal(zinv_e[0:1, :], ps[64:65, 2, :])
                    i = VE.reciprocal(zinv_o[0:1, :], ps[64:65, 3, :])
                b.ev.inc(i)
                bank_wait(b, 4)
                bank_wait(b, 5)
                b.ev.wait(TE)
                TE.matmul(ps[0:64, 4, :], b.c_ones(64, 1, 64), zinv_e[0:1, :],
                          start=True, stop=True)
                i = TE.matmul(ps[0:64, 5, :], b.c_ones(64, 1, 64),
                              zinv_o[0:1, :], start=True, stop=True)
                b.mm.inc(i)
                b.mm.wait(SE)
                SE.copy(otmp_e[0:64, :], ps[0:64, 2, :])
                i = SE.copy(otmp_o[0:64, :], ps[0:64, 3, :])
                b.es.inc(i)
                b.bg[2] = [(b.es, b.es.n)]
                b.bg[3] = [(b.es, b.es.n)]
                b.es.wait(VE)
                VE.tensor_tensor(ocat[0:64, hp, :], otmp_e[0:64, :],
                                 ps[0:64, 4, :], ALU.mult)
                i = VE.tensor_tensor(m1t[0:64, :], otmp_o[0:64, :],
                                     ps[0:64, 5, :], ALU.mult)
                b.ev.inc(i)
                b.bg[4] = [(b.ev, b.ev.n)]
                b.bg[5] = [(b.ev, b.ev.n)]
                bank_wait(b, 6)
                b.ev.wait(TE)
                i = TE.matmul(ps[:, 6, :], b.c_SHIFT, m1t[0:64, :],
                              start=True, stop=True)
                b.mm.inc(i)
                b.mm.wait(SE)
                i = SE.copy(ocat[64:128, hp, :], ps[64:128, 6, :])
                b.es.inc(i)
                b.bg[6] = [(b.es, b.es.n)]

            ocat_done = b.es.n
            b.es.wait(GE)
            b.ev.wait(GE)
            wo_guard = [(None, 0), (None, 0)]
            for ft in range(KT):
                slot = ft % 2
                g = wo_guard[slot]
                if g[0] is not None:
                    g[0].wait(GE, g[1])
                b.dW.inc(GE.dma_start(out=wpan[:, slot, :, 0:128],
                                      in_=dr["woutT"][ft]), 16)
                b.dW.wait(TE)
                if ft == 0:
                    b.es.wait(TE, ocat_done)
                bank = ft % 4
                bank_wait(b, bank)
                for jt in range(KT):
                    i = TE.matmul(ps[:, bank, :], wpan[:, slot, jt, 0:128],
                                  ocat[:, jt, :], start=(jt == 0),
                                  stop=(jt == KT - 1))
                b.mm.inc(i)
                wo_guard[slot] = (b.mm, b.mm.n)
                b.mm.wait(VE)
                i = VE.tensor_tensor(b.xs[:, bb, ft, 1 + S0:1 + S0 + 512],
                                     b.xs[:, bb, ft, 1 + S0:1 + S0 + 512],
                                     ps[:, bank, :], ALU.add)
                b.ev.inc(i)
                b.bg[bank] = [(b.ev, b.ev.n)]


def kernel(**inputs):
    in_maps = host_prep(inputs)
    nc = build(UPTO)
    res = run_bass_kernel_spmd(nc, in_maps, core_ids=list(range(8)))
    out = np.zeros((16, N, D), np.float32)
    for c in range(8):
        xo = res.results[c]["x_out"]
        out[NB * c:NB * c + NB] = (
            xo.transpose(1, 0, 2, 3).reshape(D, NB, N).transpose(1, 2, 0))
    return out


# version 03
# speedup vs baseline: 1.7358x; 1.2370x over previous
import numpy as np
import concourse.bass as bass
import concourse.mybir as mybir
from concourse.bass_utils import run_bass_kernel_spmd

F32 = mybir.dt.float32
BF16 = mybir.dt.bfloat16
AF = mybir.ActivationFunctionType
ALU = mybir.AluOpType

D = 1024
KT = 8
NB = 2
N = 2048
NP = N + 1
H = 16
HD = 64
DEPTH = 6
DI = 2730
FT1 = 22
WIN = 256
SEG = 512
PMEM = 16

B_ATTA = 4.0
B_LOC = 24.0
B_ATTB = 8.0
LN_EPS = 1e-5
RMS_EPS = 1.1920929e-07

UPTO = 4
FFN_DEPTH = DEPTH
FFN_ATTN_ONLY = False
DEBUG2 = False
DEBUG = False



def _host_tables():
    t = np.arange(N, dtype=np.float64)
    half = np.arange(0, HD, 2, dtype=np.float64)
    inv_freq = 1.0 / 10000.0 ** (half / HD)
    freqs = np.concatenate([t[:, None] * inv_freq[None, :]] * 2, -1)
    base_scale = (half + 0.4 * HD) / (1.4 * HD)
    power = (t - N // 2) / (WIN // 2)
    scale = base_scale[None, :] ** power[:, None]
    scale = np.concatenate([scale, scale], -1)
    cos, sin = np.cos(freqs), np.sin(freqs)
    xpos = np.stack([np.tile(a.T, (2, 1)) for a in
                     (cos * scale, sin * scale, cos / scale, sin / scale)])

    fb = np.repeat(t[:, None] * inv_freq[None, :], 2, axis=-1)
    ropeb = np.stack([np.tile(np.cos(fb).T, (2, 1)),
                      np.tile(np.sin(fb).T, (2, 1))])

    PH = np.zeros((128, 128))
    PI = np.zeros((128, 128))
    for blk in (0, 64):
        for m in range(32):
            PH[blk + m + 32, blk + m] = -1.0
            PH[blk + m, blk + m + 32] = 1.0
        for i in range(32):
            PI[blk + 2 * i + 1, blk + 2 * i] = -1.0
            PI[blk + 2 * i, blk + 2 * i + 1] = 1.0
    SHIFT64 = np.zeros((128, 128))
    for d_ in range(64):
        SHIFT64[d_, 64 + d_] = 1.0

    p_ = np.arange(128)[:, None]
    f_ = np.arange(128)[None, :]
    tri_causal = (p_ <= f_).astype(np.float64)
    tri_anti = (p_ >= f_).astype(np.float64)
    return dict(xpos=xpos, ropeb=ropeb, PH=PH, PI=PI, SHIFT64=SHIFT64,
                tri_causal=tri_causal, tri_anti=tri_anti)


C_PH = 0
C_PI = 128
C_SHIFT = 256
C_TRIC = 384
C_TRIA = 512
C_ONES = 640
C_ONES2 = 768
C_SEL2 = 772
CC_TOT = 900


def _consts_array():
    tb = _host_tables()
    c = np.zeros((128, CC_TOT))
    c[:, C_PH:C_PH + 128] = tb["PH"]
    c[:, C_PI:C_PI + 128] = tb["PI"]
    c[:, C_SHIFT:C_SHIFT + 128] = tb["SHIFT64"]
    c[:, C_TRIC:C_TRIC + 128] = tb["tri_causal"]
    c[:, C_TRIA:C_TRIA + 128] = tb["tri_anti"]
    c[:, C_ONES:C_ONES + 128] = 1.0
    c[0:64, C_ONES2] = 1.0
    c[64:128, C_ONES2 + 1] = 1.0
    c[64, C_SEL2:C_SEL2 + 64] = 1.0
    c[65, C_SEL2 + 64:C_SEL2 + 128] = 1.0
    return c.astype(np.float32), tb


def _pmajor(w, jc):
    ind, outd = w.shape
    return np.ascontiguousarray(
        w.reshape(ind // 128, 128, outd // jc, jc).transpose(2, 1, 0, 3))


def host_prep(inputs, n_cores=8):
    f32 = np.float32
    x = np.asarray(inputs["L0_attn_a_x"], f32)

    WqT = _pmajor(np.asarray(inputs["L0_attn_a_Wq"], f32).T, 128)
    WkT = _pmajor(np.asarray(inputs["L0_attn_a_Wk"], f32).T, 128)
    WvT = _pmajor(np.asarray(inputs["L0_attn_a_Wv"], f32).T, 256)
    WoT = _pmajor(np.asarray(inputs["L0_attn_a_Wo"], f32).T, 128)
    b_a = np.zeros((128, 4 * KT), f32)
    for i, nm in enumerate(["bq", "bk", "bv", "bo"]):
        b_a[:, i * KT:(i + 1) * KT] = (
            np.asarray(inputs[f"L0_attn_a_{nm}"], f32).reshape(KT, 128).T)
    bv_row = np.asarray(inputs["L0_attn_a_bv"], f32).reshape(1, D)

    wqkvT = _pmajor(np.asarray(inputs["L0_attn_b_w_qkv"], f32).T, 512)
    woutT = _pmajor(np.asarray(inputs["L0_attn_b_w_out"], f32).T, 128)
    g_b = np.asarray(inputs["L0_attn_b_g"], f32).reshape(KT, 128).T
    pm = np.asarray(inputs["L0_attn_b_pm"], f32)
    pmk = np.zeros((128, KT, PMEM), f32)
    pmv = np.zeros((16, H, HD + 1), f32)
    for h in range(H):
        pmk[(h % 2) * 64:(h % 2) * 64 + 64, h // 2, :] = pm[0, h].T
        pmv[:, h, 0:HD] = pm[1, h]
    pmv[:, :, HD] = 1.0

    Wqkv = np.stack([_pmajor(np.asarray(inputs["L0_ffn_Wqkv"], f32)[l], 512)
                     for l in range(DEPTH)])
    Wo_f = np.stack([
        _pmajor(np.asarray(inputs["L0_ffn_Wo"], f32)[l], 128)
        .transpose(1, 2, 0, 3).reshape(128, KT, D)
        for l in range(DEPTH)])
    W1 = np.asarray(inputs["L0_ffn_W1"], f32)
    W1p = np.zeros((DEPTH, FT1, 128, 2, KT, 128), f32)
    for l in range(DEPTH):
        for hi in range(2):
            w = np.zeros((D, FT1 * 128), f32)
            w[:, :DI] = W1[l][:, hi * DI:(hi + 1) * DI]
            W1p[l, :, :, hi] = (
                w.reshape(KT, 128, FT1, 128).transpose(2, 1, 0, 3))
    W2 = np.asarray(inputs["L0_ffn_W2"], f32)
    W2p = np.zeros((DEPTH, KT, 128, FT1, 128), f32)
    for l in range(DEPTH):
        w2 = np.zeros((FT1 * 128, D), f32)
        w2[:DI] = W2[l]
        W2p[l] = w2.reshape(FT1, 128, KT, 128).transpose(2, 1, 0, 3)

    small = np.zeros((DEPTH, 128, 70), f32)
    qs = np.asarray(inputs["L0_ffn_q_scale"], f32)
    ks = np.asarray(inputs["L0_ffn_k_scale"], f32)
    b1 = np.asarray(inputs["L0_ffn_b1"], f32)
    for l in range(DEPTH):
        small[l, :, 0] = np.tile(qs[l], 2)
        small[l, :, 1] = np.tile(ks[l], 2)
        small[l, :, 2:10] = np.asarray(
            inputs["L0_ffn_ln_g"], f32)[l].reshape(KT, 128).T
        small[l, :, 10:18] = np.asarray(
            inputs["L0_ffn_ln_b"], f32)[l].reshape(KT, 128).T
        b1a = np.zeros(FT1 * 128, f32)
        b1g = np.zeros(FT1 * 128, f32)
        b1a[:DI] = b1[l][:DI]
        b1g[:DI] = b1[l][DI:]
        small[l, :, 18:40] = b1a.reshape(FT1, 128).T
        small[l, :, 40:62] = b1g.reshape(FT1, 128).T
        small[l, :, 62:70] = np.asarray(
            inputs["L0_ffn_b2"], f32)[l].reshape(KT, 128).T

    consts, tb = _consts_array()
    xpos_t = tb["xpos"].astype(f32)
    ropeb_t = tb["ropeb"].astype(f32)

    Wqkv = Wqkv.reshape(DEPTH * 6, 128, KT * 512)
    W1p = W1p.reshape(DEPTH * FT1, 128, 2 * KT * 128)
    W2p = W2p.reshape(DEPTH * KT, 128, FT1 * 128)
    in_maps = []
    for c in range(n_cores):
        xc = x[NB * c:NB * c + NB]
        xfm = np.ascontiguousarray(
            xc.transpose(2, 0, 1).reshape(KT, 128, NB, N).transpose(1, 0, 2, 3))
        in_maps.append(dict(
            x_in=xfm, WqT=WqT, WkT=WkT, WvT=WvT, WoT=WoT, b_a=b_a,
            bv_row=bv_row, wqkvT=wqkvT, woutT=woutT, g_b=g_b, pmk=pmk,
            pmv=pmv, Wqkv=Wqkv, Wo_f=Wo_f, W1p=W1p, W2p=W2p, small=small,
            consts=consts, xpos=xpos_t, ropeb=ropeb_t))
    return in_maps



class Sem:
    def __init__(self, stack, nc, name):
        self.h = stack.enter_context(nc.semaphore(name))
        self.n = 0

    def inc(self, inst, k=1):
        inst.then_inc(self.h, k)
        self.n += k
        return inst

    def wait(self, eng, val=None):
        eng.wait_ge(self.h, self.n if val is None else val)


class B:
    pass


def build(upto=UPTO):
    import contextlib
    st = contextlib.ExitStack()
    nc = bass.Bass("TRN2", target_bir_lowering=False)
    b = B()
    b.nc = nc
    b.TE, b.VE, b.SE, b.GE, b.SY = (
        nc.tensor, nc.vector, nc.scalar, nc.gpsimd, nc.sync)

    dt = nc.dram_tensor
    dr = {}
    def di(name, shape):
        dr[name] = dt(name, shape, F32, kind="ExternalInput").ap()
    di("x_in", [128, KT, NB, N])
    di("WqT", [KT, 128, KT, 128])
    di("WkT", [KT, 128, KT, 128])
    di("WvT", [4, 128, KT, 256])
    di("WoT", [KT, 128, KT, 128])
    di("b_a", [128, 4 * KT])
    di("bv_row", [1, D])
    di("wqkvT", [6, 128, KT, 512])
    di("woutT", [KT, 128, KT, 128])
    di("g_b", [128, KT])
    di("pmk", [128, KT, PMEM])
    di("pmv", [16, H, HD + 1])
    di("Wqkv", [DEPTH * 6, 128, KT * 512])
    di("Wo_f", [DEPTH, 128, KT, D])
    di("W1p", [DEPTH * FT1, 128, 2 * KT * 128])
    di("W2p", [DEPTH * KT, 128, FT1 * 128])
    di("small", [DEPTH, 128, 70])
    di("consts", [128, CC_TOT])
    di("xpos", [4, 128, N])
    di("ropeb", [2, 128, N])
    dr["x_out"] = dt("x_out", [128, KT, NB, N], F32, kind="ExternalOutput").ap()
    if DEBUG2:
        for nm, shp in (("e_y", [128, KT * 1024]), ("e_q", [128, KT * 1024]),
                        ("e_k", [128, KT * 1280]), ("e_v", [128, 10 * H * (HD + 1)]),
                        ("e_o", [128, KT * 1024])):
            dr[nm] = dt(nm, shp, BF16, kind="ExternalOutput").ap()
    if DEBUG:
        for nm, shp in (("d_u", [128, KT, N]), ("d_q", [128, KT, N]),
                        ("d_k", [128, KT, N]), ("d_v", [128, 16, H, HD + 1]),
                        ("d_o", [128, KT, N])):
            dr[nm] = dt(nm, shp, BF16, kind="ExternalOutput").ap()
    dr["x_save"] = dt("x_save", [NB, 128, KT, N], F32, kind="Internal").ap()
    dr["h_sc"] = nc.dram_tensor("h_sc", [128, FT1, N], BF16, kind="Internal").ap()
    b.dr = dr

    xs = st.enter_context(nc.sbuf_tensor("xs", [128, NB, KT, NP], F32))
    WB = 74240
    b.WB = WB
    arena = st.enter_context(nc.sbuf_tensor("arena", [128, WB], mybir.dt.uint8))
    onesf = st.enter_context(nc.sbuf_tensor("onesf", [128, 128], F32))
    cst_b = st.enter_context(nc.sbuf_tensor("cst_b", [128, CC_TOT], BF16))
    smalls = st.enter_context(nc.sbuf_tensor("smalls", [128, 70], F32))
    ba_sb = st.enter_context(nc.sbuf_tensor("ba_sb", [128, 4 * KT], F32))
    gb_sb = st.enter_context(nc.sbuf_tensor("gb_sb", [128, KT], F32))
    bvb = st.enter_context(nc.sbuf_tensor("bvb", [1, D], BF16))
    pmk_b = st.enter_context(nc.sbuf_tensor("pmk_b", [128, KT, PMEM], BF16))
    pmv_b = st.enter_context(nc.sbuf_tensor("pmv_b", [16, H, HD + 1], BF16))
    cbias = st.enter_context(nc.sbuf_tensor("cbias", [128, 8], F32))
    ps = st.enter_context(nc.psum_tensor("ps", [128, 8, 512], F32))
    b.xs, b.arena, b.onesf, b.cst_b, b.smalls, b.ps = (
        xs, arena, onesf, cst_b, smalls, ps)
    b.ba_sb, b.gb_sb, b.bvb, b.cbias = ba_sb, gb_sb, bvb, cbias
    b.pmk_b, b.pmv_b = pmk_b, pmv_b

    def view(off, dtype, *shape, p0=0, np_=128):
        esz = 4 if dtype == F32 else 2
        nby = int(np.prod(shape)) * esz
        assert off + nby <= WB, (off, nby, WB)
        ap = arena[p0:p0 + np_, off:off + nby].bitcast(dtype)
        if len(shape) > 1:
            names = " ".join(f"s{i}" for i in range(len(shape)))
            ap = ap.rearrange(f"p ({names}) -> p {names}",
                              **{f"s{i}": v for i, v in enumerate(shape)})
        return ap
    b.view = view

    b.dW = Sem(st, nc, "dW")
    b.dS = Sem(st, nc, "dS")
    b.mm = Sem(st, nc, "mm")
    b.ev = Sem(st, nc, "ev")
    b.es = Sem(st, nc, "es")
    b.ep = Sem(st, nc, "ep")
    b.bar = Sem(st, nc, "bar")

    b.c_PH = cst_b[:, C_PH:C_PH + 128]
    b.c_PI = cst_b[:, C_PI:C_PI + 128]
    b.c_SHIFT = cst_b[0:64, C_SHIFT:C_SHIFT + 128]
    b.c_tric = cst_b[:, C_TRIC:C_TRIC + 128]
    b.c_tria = cst_b[:, C_TRIA:C_TRIA + 128]
    b.c_ones = lambda p0, np_, w: cst_b[p0:p0 + np_, C_ONES:C_ONES + w]
    b.c_ones2 = cst_b[:, C_ONES2:C_ONES2 + 2]
    b.c_sel2 = cst_b[64:66, C_SEL2:C_SEL2 + 128]
    b.c_onesf = lambda p0, np_, w: onesf[p0:p0 + np_, 0:w]

    b.bg = {k: [] for k in range(8)}

    emit_prologue(b)
    if upto >= 1:
        emit_attn_a(b)
    if upto >= 2:
        emit_ffn6(b, 0)
    if upto >= 3:
        emit_attn_b(b)
    if upto >= 4:
        emit_ffn6(b, 1)
    emit_epilogue(b)
    st.close()
    return nc


def join_all(b, engines):
    for eng in engines:
        for sem in (b.mm, b.ev, b.es, b.ep, b.dW, b.dS):
            if sem.n:
                sem.wait(eng)


def bank_wait(b, bank):
    for sem, val in b.bg[bank]:
        sem.wait(b.TE, val)
    b.bg[bank] = []


def emit_prologue(b):
    SY, VE, GE = b.SY, b.VE, b.GE
    b.dW.inc(GE.dma_start(out=b.cst_b[:, :], in_=b.dr["consts"]), 16)
    b.dW.inc(GE.dma_start(out=b.bvb[:, :], in_=b.dr["bv_row"]), 16)
    b.dS.inc(SY.dma_start(out=b.ba_sb[:, :], in_=b.dr["b_a"]), 16)
    b.dS.inc(SY.dma_start(out=b.gb_sb[:, :], in_=b.dr["g_b"]), 16)
    for bb in range(NB):
        b.dS.inc(SY.dma_start(out=b.xs[:, bb, :, 1:NP],
                              in_=b.dr["x_in"][:, :, bb, :]), 16)
    b.dS.wait(VE)
    b.dW.wait(VE)
    VE.memset(b.onesf[:, :], 1.0)
    for i, val in enumerate([LN_EPS, RMS_EPS, -B_ATTA, -B_LOC, -B_ATTB,
                             1e-18, 0.0, 0.0]):
        VE.memset(b.cbias[:, i:i + 1], float(val))
    b.ev.inc(VE.memset(b.xs[:, :, :, 0:1], 0.0))


def emit_epilogue(b):
    SY = b.SY
    b.ev.wait(SY)
    b.es.wait(SY)
    b.mm.wait(SY)
    b.dW.wait(SY)
    b.ep.wait(SY)
    for bb in range(NB):
        b.dS.inc(SY.dma_start(out=b.dr["x_out"][:, :, bb, :],
                              in_=b.xs[:, bb, :, 1:NP]), 16)
    b.dS.wait(SY)



def ln_chunk(b, srcs, dsts, width, eps_col, lnb, gamma=None, beta=None,
             want_mean=True):
    TE, VE, SE = b.TE, b.VE, b.SE
    b.mm.wait(SE)
    for k in range(KT):
        i = SE.activation(lnb["sq"][:, k, 0:width], srcs[k], AF.Square)
    b.es.inc(i)
    bank_wait(b, 7)
    bank_wait(b, 6)
    if want_mean:
        for k in range(KT):
            TE.matmul(b.ps[0:1, 7, 0:width], b.c_onesf(0, 128, 1), srcs[k],
                      start=(k == 0), stop=(k == KT - 1))
    b.es.wait(TE)
    for k in range(KT):
        i = TE.matmul(b.ps[0:1, 6, 0:width], b.c_ones(0, 128, 1),
                      lnb["sq"][:, k, 0:width],
                      start=(k == 0), stop=(k == KT - 1))
    b.mm.inc(i)
    b.mm.wait(VE)
    ra, rb, rc, rsd, rrec = (lnb[n][0:1, 0:width]
                             for n in ("ra", "rb", "rc", "rsd", "rrec"))
    rstd, mrs = lnb["rstd"][0:1, 0:width], lnb["mrs"][0:1, 0:width]
    if want_mean:
        VE.tensor_scalar(ra, b.ps[0:1, 7, 0:width], 1.0 / D, None, ALU.mult)
        VE.tensor_tensor(rb, ra, ra, ALU.mult)
        VE.tensor_scalar(rc, b.ps[0:1, 6, 0:width], 1.0 / D, None, ALU.mult)
        i = VE.tensor_tensor(rc, rc, rb, ALU.subtract)
    else:
        i = VE.tensor_scalar(rc, b.ps[0:1, 6, 0:width], 1.0 / D, None,
                             ALU.mult)
    b.ev.inc(i)
    b.ev.wait(SE)
    SE.activation(rsd, rc, AF.Ln, bias=b.cbias[0:1, eps_col:eps_col + 1])
    b.es.inc(SE.activation(rrec, rsd, AF.Exp, scale=-0.5))
    b.es.wait(VE)
    if want_mean:
        i = VE.tensor_tensor(rb, ra, rrec, ALU.mult)
        b.ev.inc(i)
    b.bg[7] = [(b.es, b.es.n)]
    b.bg[6] = [(b.es, b.es.n)]
    bank_wait(b, 5)
    bank_wait(b, 4)
    b.ev.wait(TE)
    b.es.wait(TE)
    i = TE.matmul(b.ps[:, 5, 0:width], b.c_onesf(0, 1, 128), rrec,
                  start=True, stop=True)
    if want_mean:
        i = TE.matmul(b.ps[:, 4, 0:width], b.c_onesf(0, 1, 128), rb,
                      start=True, stop=True)
    b.mm.inc(i)
    b.mm.wait(VE)
    for k in range(KT):
        t1 = lnb["t1"][:, 0:width]
        VE.tensor_tensor(t1, srcs[k], b.ps[:, 5, 0:width], ALU.mult)
        if want_mean:
            if gamma is None:
                i = VE.tensor_tensor(dsts[k], t1, b.ps[:, 4, 0:width],
                                     ALU.subtract)
            else:
                t2 = lnb["t2"][:, 0:width]
                VE.tensor_tensor(t2, t1, b.ps[:, 4, 0:width], ALU.subtract)
                i = VE.tensor_scalar(dsts[k], t2, gamma[:, k:k + 1],
                                     beta[:, k:k + 1], ALU.mult, ALU.add)
        else:
            if gamma is None:
                i = VE.tensor_copy(dsts[k], t1)
            else:
                i = VE.tensor_scalar(dsts[k], t1, gamma[:, k:k + 1], None,
                                     ALU.mult)
    b.ev.inc(i)
    b.bg[5] = [(b.ev, b.ev.n)]
    b.bg[4] = [(b.ev, b.ev.n)]


def mk_lnb(b, sq_off, rows_off, t_off, width=512, viewfn=None):
    v = viewfn if viewfn is not None else b.view
    return dict(
        sq=v(sq_off, BF16, KT, width),
        ra=v(rows_off, F32, width, p0=0, np_=1),
        rb=v(rows_off + 2048, F32, width, p0=0, np_=1),
        rc=v(rows_off + 4096, F32, width, p0=0, np_=1),
        rsd=v(rows_off + 6144, F32, width, p0=0, np_=1),
        rrec=v(rows_off + 8192, F32, width, p0=0, np_=1),
        rstd=v(rows_off + 10240, BF16, width, p0=0, np_=1),
        mrs=v(rows_off + 10240 + width * 2, BF16, width, p0=0, np_=1),
        t1=v(t_off, F32, width),
        t2=v(t_off + width * 4, F32, width),
    )



def emit_attn_a(b):
    TE, VE, SE, GE, SY = b.TE, b.VE, b.SE, b.GE, b.SY
    ps, v = b.ps, b.view
    dr = b.dr

    u = v(0, BF16, KT, N)
    ocat = v(0, BF16, KT, N)
    v65 = v(32768, BF16, 16, H, HD + 1)
    lnb = mk_lnb(b, 32768, 45056, 40960)
    qkpan = v(66048, BF16, 2, KT, 128)
    vpan = v(70144, BF16, KT, 256)
    abuf = v(66048, BF16, 2, 1024)
    otmp_e = v(70144, BF16, 512)
    otmp_o = v(70144 + 1024, BF16, 512)
    zinv_e = v(72192, BF16, 512, p0=64, np_=1)
    zinv_o = v(72192 + 1024, BF16, 512, p0=64, np_=1)
    zinv_eo = v(72192, BF16, 1024, p0=64, np_=1)
    wopan = v(66048, BF16, 2, KT, 128)

    for eng in (TE, VE, SE, GE):
        b.dS.wait(eng)
        b.dW.wait(eng)
        if eng is not VE:
            b.ev.wait(eng)

    for bb in range(NB):
        join_all(b, (TE, VE, SE, GE, SY))
        xb_bf = b.xs[:, bb].rearrange("p k t -> p (k t)").bitcast(
            BF16).rearrange("p (k t) -> p k t", k=KT, t=2 * NP)
        QC0, KC0 = 0, 2050

        b.dS.inc(SY.dma_start(out=dr["x_save"][bb],
                              in_=b.xs[:, bb, :, 1:NP]), 16)
        spill_done = b.dS.n

        b.ev.wait(SE)
        for tc in range(4):
            srcs = [b.xs[:, bb, k, 1 + tc * 512:1 + (tc + 1) * 512]
                    for k in range(KT)]
            dsts = [u[:, k, tc * 512:(tc + 1) * 512] for k in range(KT)]
            ln_chunk(b, srcs, dsts, 512, 0, lnb)
        u_done = b.ev.n

        b.mm.wait(VE)
        b.ev.inc(VE.memset(v65[:, :, :, HD:HD + 1], 1.0))
        ones_done = b.ev.n

        b.ev.wait(SE, u_done)
        pan_free = [(None, 0), (None, 0)]
        for w01, wnm in ((0, "WqT"), (1, "WkT")):
            for jt in range(KT):
                slot = (w01 * KT + jt) % 2
                g = pan_free[slot]
                if g[0] is not None:
                    g[0].wait(GE, g[1])
                b.dW.inc(GE.dma_start(out=qkpan[:, slot], in_=dr[wnm][jt]), 16)
                for tc in range(4):
                    bank = tc % 4
                    bank_wait(b, bank)
                    if tc == 0:
                        b.dW.wait(TE)
                        b.ev.wait(TE, u_done)
                        if w01 == 0 and jt == 0:
                            b.dS.wait(TE, spill_done)
                    for k in range(KT):
                        i = TE.matmul(ps[:, bank, :], qkpan[:, slot, k, :],
                                      u[:, k, tc * 512:(tc + 1) * 512],
                                      start=(k == 0), stop=(k == KT - 1))
                    b.mm.inc(i)
                pan_free[slot] = (b.mm, b.mm.n)
                b.mm.wait(SE)
                b.dS.wait(SE, spill_done)
                base = QC0 if w01 == 0 else KC0
                for tc in range(4):
                    i = SE.activation(
                        xb_bf[:, jt, base + tc * 512:base + (tc + 1) * 512],
                        ps[:, tc % 4, :], AF.Identity,
                        bias=b.ba_sb[:, w01 * KT + jt:w01 * KT + jt + 1])
                    b.es.inc(i)
                for bank in range(4):
                    b.bg[bank] = [(b.es, b.es.n)]
        vp_guard = (None, 0)
        for jc in range(4):
            if vp_guard[0] is not None:
                vp_guard[0].wait(GE, vp_guard[1])
            b.dW.inc(GE.dma_start(out=vpan[:, :], in_=dr["WvT"][jc]), 16)
            b.dW.wait(TE)
            for tt in range(16):
                bank = tt % 4
                bank_wait(b, bank)
                for k in range(KT):
                    TE.matmul(ps[:, bank, 0:256], u[:, k, tt * 128:(tt + 1) * 128],
                              vpan[:, k, :], start=(k == 0), stop=False)
                i = TE.matmul(ps[:, bank, 0:256], b.c_ones(0, 1, 128),
                              b.bvb[0:1, jc * 256:(jc + 1) * 256],
                              start=False, stop=True)
                b.mm.inc(i)
                b.mm.wait(SE)
                i = SE.copy(v65[:, tt, 4 * jc:4 * jc + 4, 0:HD],
                            ps[:, bank, 0:256])
                b.es.inc(i)
                b.bg[bank] = [(b.es, b.es.n)]
            vp_guard = (b.mm, b.mm.n)

        if DEBUG and bb == 0:
            b.ev.wait(SY, u_done)
            b.es.wait(SY)
            b.dS.inc(SY.dma_start(out=dr["d_u"], in_=u[:, :, :]), 16)
            b.dS.inc(SY.dma_start(
                out=dr["d_q"],
                in_=xb_bf[:, :, QC0:QC0 + N]), 16)
            b.dS.inc(SY.dma_start(
                out=dr["d_k"],
                in_=xb_bf[:, :, KC0:KC0 + N]), 16)
            b.dS.inc(SY.dma_start(out=dr["d_v"], in_=v65[:, :, :, :]), 16)
            dump_done = b.dS.n
            b.dS.wait(TE, dump_done)

        b.ev.wait(TE, ones_done)
        for hp in range(KT):
            for ncx in range(4):
                ncs = slice(ncx * 512, (ncx + 1) * 512)
                expc = {}
                def scores(mt):
                    pb = 2 * (mt % 2)
                    bank_wait(b, pb)
                    bank_wait(b, pb + 1)
                    TE.matmul(ps[:, pb, :],
                              xb_bf[0:64, hp, QC0 + mt * 128:QC0 + (mt + 1) * 128],
                              xb_bf[0:64, hp, KC0 + ncx * 512:KC0 + (ncx + 1) * 512],
                              start=True, stop=True)
                    i = TE.matmul(ps[:, pb + 1, :],
                                  xb_bf[64:128, hp, QC0 + mt * 128:QC0 + (mt + 1) * 128],
                                  xb_bf[64:128, hp, KC0 + ncx * 512:KC0 + (ncx + 1) * 512],
                                  start=True, stop=True)
                    b.mm.inc(i)
                    b.mm.wait(SE)
                    g = aslot_guard[mt % 2]
                    if g[0] is not None:
                        g[0].wait(SE, g[1])
                    i = SE.activation(abuf[:, mt % 2], ps[:, pb:pb + 2, :],
                                      AF.Exp, scale=1.0 / 32.0,
                                      bias=b.cbias[:, 2:3])
                    b.es.inc(i)
                    expc[mt] = b.es.n
                    b.bg[pb] = [(b.es, b.es.n)]
                    b.bg[pb + 1] = [(b.es, b.es.n)]

                def av(mt):
                    if mt == 0:
                        bank_wait(b, 4)
                        bank_wait(b, 5)
                    b.es.wait(TE, expc[mt])
                    TE.matmul(ps[0:65, 4, :], v65[:, mt, 2 * hp, :],
                              abuf[:, mt % 2, 0:512],
                              start=(mt == 0), stop=(mt == 15))
                    i = TE.matmul(ps[0:65, 5, :], v65[:, mt, 2 * hp + 1, :],
                                  abuf[:, mt % 2, 512:1024],
                                  start=(mt == 0), stop=(mt == 15))
                    b.mm.inc(i)
                    aslot_guard[mt % 2] = (b.mm, b.mm.n)

                aslot_guard = [(None, 0), (None, 0)]
                scores(0)
                for mt in range(1, 16):
                    scores(mt)
                    av(mt - 1)
                av(15)
                b.mm.wait(SE)
                for bk in (0, 1):
                    for sem, val in b.bg[bk]:
                        sem.wait(SE, val)
                    b.bg[bk] = []
                SE.activation(ps[64:65, 0, :], ps[64:65, 4, :], AF.Ln)
                SE.activation(ps[64:65, 1, :], ps[64:65, 5, :], AF.Ln)
                i = SE.activation(zinv_eo[0:1, :], ps[64:65, 0:2, :], AF.Exp,
                                  scale=-1.0)
                b.es.inc(i)
                b.bg[0] = [(b.es, b.es.n)]
                b.bg[1] = [(b.es, b.es.n)]
                bank_wait(b, 6)
                bank_wait(b, 7)
                b.es.wait(TE)
                TE.matmul(ps[0:64, 6, :], b.c_ones(64, 1, 64), zinv_e[0:1, :],
                          start=True, stop=True)
                i = TE.matmul(ps[0:64, 7, :], b.c_ones(64, 1, 64),
                              zinv_o[0:1, :], start=True, stop=True)
                b.mm.inc(i)
                b.mm.wait(SE)
                SE.copy(otmp_e[0:64, :], ps[0:64, 4, :])
                i = SE.copy(otmp_o[0:64, :], ps[0:64, 5, :])
                b.es.inc(i)
                b.bg[5] = [(b.es, b.es.n)]
                b.es.wait(VE)
                VE.tensor_tensor(ocat[0:64, hp, ncs], otmp_e[0:64, :],
                                 ps[0:64, 6, :], ALU.mult)
                i = VE.tensor_tensor(abuf[0:64, 0, 0:512], otmp_o[0:64, :],
                                     ps[0:64, 7, :], ALU.mult)
                b.ev.inc(i)
                b.bg[6] = [(b.ev, b.ev.n)]
                b.bg[7] = [(b.ev, b.ev.n)]
                bank_wait(b, 4)
                b.ev.wait(TE)
                i = TE.matmul(ps[:, 4, :], b.c_SHIFT, abuf[0:64, 0, 0:512],
                              start=True, stop=True)
                b.mm.inc(i)
                b.mm.wait(SE)
                i = SE.copy(ocat[64:128, hp, ncs], ps[64:128, 4, :])
                b.es.inc(i)
                b.bg[4] = [(b.es, b.es.n)]

        ocat_done = b.es.n
        b.es.wait(GE)
        b.ev.wait(GE)
        wo_guard = [(None, 0), (None, 0)]
        for ft in range(KT):
            slot = ft % 2
            g = wo_guard[slot]
            if g[0] is not None:
                g[0].wait(GE, g[1])
            b.dW.inc(GE.dma_start(out=wopan[:, slot], in_=dr["WoT"][ft]), 16)
            b.dW.wait(TE)
            if ft == 0:
                b.es.wait(TE, ocat_done)
            for tc in range(4):
                bank = tc % 4
                bank_wait(b, bank)
                for jt in range(KT):
                    i = TE.matmul(ps[:, bank, :], wopan[:, slot, jt, :],
                                  ocat[:, jt, tc * 512:(tc + 1) * 512],
                                  start=(jt == 0), stop=(jt == KT - 1))
                b.mm.inc(i)
            wo_guard[slot] = (b.mm, b.mm.n)
            b.mm.wait(SE)
            for tc in range(4):
                i = SE.activation(
                    b.xs[:, bb, ft, 1 + tc * 512:1 + (tc + 1) * 512],
                    ps[:, tc % 4, :], AF.Identity,
                    bias=b.ba_sb[:, 3 * KT + ft:3 * KT + ft + 1])
                b.es.inc(i)
                b.bg[tc % 4] = [(b.es, b.es.n)]
        b.es.wait(GE)
        b.dW.inc(GE.dma_start(out=b.xs[:, bb, :, 1:NP], in_=dr["x_save"][bb],
                              accum_op=ALU.add), 16)
        b.dW.wait(GE)


def emit_ffn6(b, inv):
    TE, VE, SE, GE, SY = b.TE, b.VE, b.SE, b.GE, b.SY
    ps, v = b.ps, b.view
    dr = b.dr
    sm = b.smalls

    join_all(b, (TE, VE, SE, GE, SY))
    wq_fl, w1_fl, w2_fl = dr["Wqkv"], dr["W1p"], dr["W2p"]
    for bb in range(NB):
        b.dS.inc(SY.dma_start(out=dr["x_save"][bb],
                              in_=b.xs[:, bb, :, 1:NP]), 16)

    for bb in range(NB):
        join_all(b, (TE, VE, SE, GE, SY))

        def rv(off, dtype, *shape, p0=0, np_=128):
            esz = 4 if dtype == F32 else 2
            nby = int(np.prod(shape)) * esz
            assert off + nby <= KT * NP * 4, (off, nby)
            ap = b.xs[p0:p0 + np_, 1 - bb].rearrange(
                "p k t -> p (k t)").bitcast(mybir.dt.uint8)
            ap = ap[:, off:off + nby].bitcast(dtype)
            if len(shape) > 1:
                names = " ".join(f"s{i}" for i in range(len(shape)))
                ap = ap.rearrange(f"p ({names}) -> p {names}",
                                  **{f"s{i}": s for i, s in enumerate(shape)})
            return ap

        zc = lambda k, lo, n: b.xs[:, bb, k, lo:lo + n]

        if bb == 1:
            b.dS.inc(SY.dma_start(out=b.xs[:, 1, :, 1:NP],
                                  in_=dr["x_save"][1]), 16)
        b.dS.wait(SE)
        b.dS.wait(TE)
        b.dS.wait(VE)
        VE.memset(b.xs[:, bb, :, 0:1], 0.0)
        lnb0 = mk_lnb(b, 0, 12288, 8192)
        for tc in range(4):
            srcs = [zc(k, 1 + tc * 512, 512) for k in range(KT)]
            ln_chunk(b, srcs, srcs, 512, 0, lnb0)

        y_bf = v(0, BF16, KT, 1024)
        ocat = v(0, BF16, KT, 1024)
        qf = v(16384, BF16, KT, 1024)
        kf = v(32768, BF16, KT, 1280)
        abufs = [[v(53248, BF16, 2, 384), v(54784, BF16, 2, 384)],
                 [v(56320, BF16, 2, 384), v(57856, BF16, 2, 384)]]
        otmps = [v(59392, BF16, 512), v(60416, BF16, 512)]
        zinv2s = [v(61440, BF16, 2, 256, p0=64, np_=1),
                  v(62464, BF16, 2, 256, p0=64, np_=1)]
        oddts = [v(63488, BF16, 512), v(64512, BF16, 512)]
        rope_bufs = (
            [v(65536, BF16, 512), v(66560, BF16, 512)],
            [v(67584, F32, 512, p0=64, np_=2),
             v(69632, F32, 512, p0=64, np_=2)],
            [v(71680, BF16, 512, p0=64, np_=2),
             v(72704, BF16, 512, p0=64, np_=2)],
            [v(53248, BF16, 512), v(54272, BF16, 512)],
            v(55296, BF16, 512),
            v(56320, BF16, 512),
        )
        b.la_ztmp = v(67584, F32, 512, p0=64, np_=1)
        m1t = v(55296, BF16, 512)
        m2t = v(56320, BF16, 512)
        at1 = v(53248, BF16, 512)
        v65 = rv(0, BF16, 10, H, HD + 1)
        qkvpan = rv(20800, BF16, 2, KT, 512)
        wo_sb = rv(37184, BF16, KT, 1024)
        tabs = rv(53568, BF16, 2, 4, 512)
        xn = v(0, BF16, KT, 2048)
        w1pan = v(32768, BF16, 2, 2, KT, 128)
        hout = v(40960, BF16, 2, 2048)
        ff_tmp = [v(49152 + i * 2048, F32, 512) for i in range(4)]
        lnb1 = mk_lnb(b, 45056, 53248, 57344, viewfn=rv)
        lnb1["t1"] = v(57344, F32, 512)
        lnb1["t2"] = v(59392, F32, 512)
        at2 = v(64512, BF16, 512)
        hin = [v(0, BF16, FT1, 512), v(22528, BF16, FT1, 512)]
        w2all = rv(0, BF16, KT, FT1, 128)

        b.g1tmp_guard = [(None, 0), (None, 0)]
        b.fftmp_guard = [(None, 0), (None, 0)]
        for lyr in range(FFN_DEPTH):
            b.es.wait(SY)
            b.ev.wait(SY)
            b.dS.inc(SY.dma_start(out=sm[:, :], in_=dr["small"][lyr]), 16)
            smalls_ld = b.dS.n

            for c in range(2):
                T0 = c * 1024
                b.ev.wait(GE)
                b.mm.wait(GE)
                for k in range(KT):
                    srcz = zc(k, 1 + T0, 1024) if k < 4 else zc(k, T0, 1024)
                    i = GE.tensor_copy(y_bf[:, k, :], srcz)
                b.ep.inc(i)
                for hc in range(2):
                    b.dW.inc(GE.dma_start(
                        out=tabs[:, hc],
                        in_=dr["xpos"][:, :, T0 + hc * 512:T0 + (hc + 1) * 512]
                        .rearrange("a p t -> p a t")), 16)
                rp = RopePipe(b, qf, kf, tabs, rope_bufs, qkvpan,
                              [wq_fl[lyr * 6 + jc] for jc in range(4)],
                              smalls_ld)
                rp.run(y_bf)
                for jc in (4, 5):
                    slot = jc % 2
                    b.mm.wait(GE)
                    b.dW.inc(GE.dma_start(out=qkvpan[:, slot],
                                          in_=wq_fl[lyr * 6 + jc],
                                          max_dma_last_dim=8192), 16)
                    b.dW.wait(TE)
                    for tt in range(8):
                        bank = tt % 2
                        bank_wait(b, bank)
                        for k in range(KT):
                            i = TE.matmul(
                                ps[:, bank, :],
                                y_bf[:, k, tt * 128:(tt + 1) * 128],
                                qkvpan[:, slot, k, :],
                                start=(k == 0), stop=(k == KT - 1))
                        b.mm.inc(i)
                        b.mm.wait(SE)
                        i = SE.copy(
                            v65[:, 2 + tt, (jc - 4) * 8:(jc - 4) * 8 + 8, 0:HD],
                            ps[:, bank, :])
                        b.es.inc(i)
                        b.bg[bank] = [(b.es, b.es.n)]
                b.es.wait(VE)
                b.ev.inc(VE.memset(v65[:, :, :, HD:HD + 1], 1.0))
                ones_done = b.ev.n

                b.ev.wait(TE, ones_done)
                units = [(hp, w) for hp in range(KT) for w in range(4)]
                prev = None
                for idx, (hp, w) in enumerate(units):
                    first = (c == 0 and w == 0)
                    mts = (2, 3) if first else (2, 0, 1, 3)
                    par = idx % 2
                    g = la_stage1(b, hp, w, mts, qf, kf, abufs[par])
                    if prev is not None:
                        la_stage2(b, *prev)
                    prev = (units[idx][0], units[idx][1], mts, par,
                            abufs[par], g, v65, zinv2s[par], otmps[par],
                            oddts[par], ocat)
                la_stage2(b, *prev)
                if c == 0:
                    b.dW.inc(GE.dma_start(out=wo_sb[:, :],
                                          in_=dr["Wo_f"][lyr],
                                          max_dma_last_dim=8192), 16)
                b.dW.wait(TE)
                for hc in range(2):
                    for ft in range(KT):
                        bank = (hc * KT + ft) % 4
                        bank_wait(b, bank)
                        for jt in range(KT):
                            i = TE.matmul(
                                ps[:, bank, :],
                                wo_sb[:, jt, ft * 128:(ft + 1) * 128],
                                ocat[:, jt, hc * 512:(hc + 1) * 512],
                                start=(jt == 0), stop=(jt == KT - 1))
                        b.mm.inc(i)
                        b.mm.wait(VE)
                        i = VE.tensor_tensor(
                            zc(ft, 1 + T0 + hc * 512, 512),
                            zc(ft, 1 + T0 + hc * 512, 512),
                            ps[:, bank, :], ALU.add)
                        b.ev.inc(i)
                        b.bg[bank] = [(b.ev, b.ev.n)]
                b.ev.wait(GE)
                b.mm.wait(GE)
                GE.tensor_copy(kf[:, :, 0:256], kf[:, :, 1024:1280])
                i = GE.tensor_copy(v65[:, 0:2, :, :], v65[:, 8:10, :, :])
                b.ep.inc(i)
                b.ep.wait(TE)
                b.ep.wait(SE)

            if FFN_ATTN_ONLY:
                continue
            b.mm.wait(VE)
            b.mm.wait(SE)
            for tc in range(4):
                srcs = [zc(k, 1 + tc * 512, 512) if k < 4
                        else zc(k, tc * 512, 512) for k in range(KT)]
                dsts = [xn[:, k, tc * 512:(tc + 1) * 512] for k in range(KT)]
                ln_chunk(b, srcs, dsts, 512, 0, lnb1,
                         gamma=sm[:, 2:10], beta=sm[:, 10:18])
            xn_done = b.ev.n
            b.mm.wait(GE)
            b.ev.wait(GE)
            w1_guard = [(None, 0), (None, 0)]
            ho_guard = [(None, 0), (None, 0)]
            for fp in range(FT1):
                slot = fp % 2
                g = w1_guard[slot]
                if g[0] is not None:
                    g[0].wait(GE, g[1])
                b.dW.inc(GE.dma_start(out=w1pan[:, slot],
                                      in_=w1_fl[lyr * FT1 + fp]), 16)
                b.dW.wait(TE)
                if fp == 0:
                    b.ev.wait(TE, xn_done)
                g = ho_guard[slot]
                if g[0] is not None:
                    g[0].wait(VE, g[1])
                for ts in range(4):
                    ba = (ts % 2) * 2
                    bank_wait(b, ba)
                    bank_wait(b, ba + 1)
                    for k in range(KT):
                        TE.matmul(ps[:, ba, :], w1pan[:, slot, 0, k, :],
                                  xn[:, k, ts * 512:(ts + 1) * 512],
                                  start=(k == 0), stop=(k == KT - 1))
                    for k in range(KT):
                        i = TE.matmul(ps[:, ba + 1, :], w1pan[:, slot, 1, k, :],
                                      xn[:, k, ts * 512:(ts + 1) * 512],
                                      start=(k == 0), stop=(k == KT - 1))
                    b.mm.inc(i)
                    b.mm.wait(SE)
                    gt = m1t if ts % 2 == 0 else m2t
                    at = at1 if ts % 2 == 0 else at2
                    g = b.g1tmp_guard[ts % 2]
                    if g[0] is not None:
                        g[0].wait(SE, g[1])
                    SE.activation(gt, ps[:, ba + 1, :], AF.Gelu,
                                  bias=sm[:, 40 + fp:41 + fp])
                    i = SE.activation(at, ps[:, ba, :], AF.Identity,
                                      bias=sm[:, 18 + fp:19 + fp])
                    b.es.inc(i)
                    b.es.wait(VE)
                    i = VE.tensor_tensor(
                        hout[:, slot, ts * 512:(ts + 1) * 512], at, gt,
                        ALU.mult)
                    b.ev.inc(i)
                    b.g1tmp_guard[ts % 2] = (b.ev, b.ev.n)
                    b.bg[ba] = [(b.ev, b.ev.n)]
                    b.bg[ba + 1] = [(b.ev, b.ev.n)]
                w1_guard[slot] = (b.mm, b.mm.n)
                b.ev.wait(SY)
                b.dS.inc(SY.dma_start(out=dr["h_sc"][:, fp, :],
                                      in_=hout[:, slot, :]), 16)
                ho_guard[slot] = (b.dS, b.dS.n)
            b.mm.wait(GE)
            b.ev.wait(GE)
            for ft in range(KT):
                b.dW.inc(GE.dma_start(out=w2all[:, ft], in_=w2_fl[lyr * KT + ft],
                                      max_dma_last_dim=8192), 16)
            w2_ld = b.dW.n
            hin_guard = [(None, 0), (None, 0)]
            for ts in range(4):
                slot = ts % 2
                g = hin_guard[slot]
                if g[0] is not None:
                    g[0].wait(SY, g[1])
                b.dS.inc(SY.dma_start(out=hin[slot],
                                      in_=dr["h_sc"][:, :, ts * 512:(ts + 1) * 512]), 16)
                b.dS.wait(TE)
                b.dW.wait(TE, w2_ld)
                for ft in range(KT):
                    bank = 4 + (ft % 2)
                    bank_wait(b, bank)
                    for kt in range(FT1):
                        i = TE.matmul(ps[:, bank, :], w2all[:, ft, kt, :],
                                      hin[slot][:, kt, :],
                                      start=(kt == 0), stop=(kt == FT1 - 1))
                    b.mm.inc(i)
                    b.mm.wait(SE)
                    stt = ff_tmp[ft % 2]
                    g = b.fftmp_guard[ft % 2]
                    if g[0] is not None:
                        g[0].wait(SE, g[1])
                    i = SE.activation(stt, ps[:, bank, :], AF.Identity,
                                      bias=sm[:, 62 + ft:63 + ft])
                    b.es.inc(i)
                    b.es.wait(VE)
                    i = VE.tensor_tensor(zc(ft, 1 + ts * 512, 512),
                                         zc(ft, 1 + ts * 512, 512), stt,
                                         ALU.add)
                    b.ev.inc(i)
                    b.fftmp_guard[ft % 2] = (b.ev, b.ev.n)
                    b.bg[bank] = [(b.ev, b.ev.n)]
                hin_guard[slot] = (b.mm, b.mm.n)

        join_all(b, (GE, SY))
        if bb == 0:
            b.dW.inc(GE.dma_start(out=dr["x_save"][0],
                                  in_=b.xs[:, 0, :, 1:NP],
                                  accum_op=ALU.add), 16)
        else:
            b.dW.inc(GE.dma_start(out=b.xs[:, 1, :, 1:NP],
                                  in_=dr["x_save"][1], accum_op=ALU.add), 16)
            b.dW.wait(GE)
    join_all(b, (SY,))
    b.dS.inc(SY.dma_start(out=b.xs[:, 0, :, 1:NP], in_=dr["x_save"][0]), 16)
    b.dS.wait(SY)


class RopePipe:

    def __init__(self, b, qf, kf, tabs, bufs, qkvpan, wq_src, smalls_ld):
        self.b = b
        self.qf, self.kf, self.tabs = qf, kf, tabs
        self.sqt, self.r_f32, self.r_bf, self.rsb, self.m1, self.m2 = bufs
        self.qkvpan = qkvpan
        self.wq_src = wq_src
        self.smalls_ld = smalls_ld
        self.cnt = {}
        self.pan_guard = {}

    def unit(self, u):
        jc, rest = divmod(u, 8)
        jsub, hc = divmod(rest, 2)
        isq = jc < 2
        jt = (jc % 2) * 4 + jsub
        return jc, jsub, hc, isq, jt

    def dst(self, u):
        jc, jsub, hc, isq, jt = self.unit(u)
        if isq:
            return self.qf[:, jt, hc * 512:(hc + 1) * 512]
        return self.kf[:, jt, 256 + hc * 512:256 + (hc + 1) * 512]

    def s0(self, u):
        b = self.b
        TE, GE = b.TE, b.GE
        jc, jsub, hc, isq, jt = self.unit(u)
        if jsub == 0 and hc == 0:
            slot = jc % 2
            g = self.pan_guard.get(slot)
            if g is not None:
                g[0].wait(GE, g[1])
            b.dW.inc(GE.dma_start(out=self.qkvpan[:, slot],
                                  in_=self.wq_src[jc],
                                  max_dma_last_dim=8192), 16)
            self.cnt[("pan", jc)] = b.dW.n
        bank = u % 4
        bank_wait(b, bank)
        b.dW.wait(TE, self.cnt[("pan", jc)])
        b.ep.wait(TE)
        slot = jc % 2
        for k in range(KT):
            i = TE.matmul(b.ps[:, bank, :],
                          self.qkvpan[:, slot, k, jsub * 128:(jsub + 1) * 128],
                          self.ybf[:, k, hc * 512:(hc + 1) * 512],
                          start=(k == 0), stop=(k == KT - 1))
        b.mm.inc(i)
        self.cnt[("s0", u)] = b.mm.n
        if jsub == 3 and hc == 1:
            self.pan_guard[(jc + 1) % 2] = (b.mm, b.mm.n)
        SE = b.SE
        b.mm.wait(SE)
        b.ev.wait(SE)
        i = SE.activation(self.sqt[u % 2], b.ps[:, bank, :], AF.Square)
        b.es.inc(i)
        self.cnt[("sq", u)] = b.es.n

    def ss(self, u):
        b = self.b
        TE, SE, VE = b.TE, b.SE, b.VE
        bank_wait(b, 6)
        b.es.wait(TE, self.cnt[("sq", u)])
        i = TE.matmul(b.ps[64:66, 6, :], b.c_ones2, self.sqt[u % 2],
                      start=True, stop=True)
        b.mm.inc(i)
        b.mm.wait(SE)
        b.ev.wait(SE)
        SE.activation(self.r_f32[u % 2][0:2, :], b.ps[64:66, 6, :],
                      AF.Ln, bias=b.cbias[64:66, 5:6])
        i = SE.activation(self.r_bf[u % 2][0:2, :],
                          self.r_f32[u % 2][0:2, :], AF.Exp, scale=-0.5)
        b.es.inc(i)
        b.bg[6] = [(b.es, b.es.n)]
        self.cnt[("rc", u)] = b.es.n

    def bc(self, u):
        b = self.b
        TE, VE = b.TE, b.VE
        bank_wait(b, 7)
        b.es.wait(TE, self.cnt[("rc", u)])
        i = TE.matmul(b.ps[:, 7, :], b.c_sel2, self.r_bf[u % 2][0:2, :],
                      start=True, stop=True)
        b.mm.inc(i)
        b.mm.wait(VE)
        b.dS.wait(VE, self.smalls_ld)
        jc, jsub, hc, isq, jt = self.unit(u)
        col = 0 if isq else 1
        VE.tensor_scalar(self.rsb[u % 2], b.ps[:, 7, :],
                         b.smalls[:, col:col + 1], None, ALU.mult)
        i = VE.tensor_tensor(self.dst(u), b.ps[:, u % 4, :], self.rsb[u % 2],
                             ALU.mult)
        b.ev.inc(i)
        b.bg[7] = [(b.ev, b.ev.n)]
        b.bg[u % 4] = [(b.ev, b.ev.n)]
        self.cnt[("ev", u)] = b.ev.n

    def pm(self, u):
        b = self.b
        TE, VE = b.TE, b.VE
        bank_wait(b, 5)
        b.ev.wait(TE, self.cnt[("ev", u)])
        d = self.dst(u)
        i = TE.matmul(b.ps[:, 5, :], b.c_PH, d, start=True, stop=True)
        b.mm.inc(i)
        jc, jsub, hc, isq, jt = self.unit(u)
        ct = self.tabs[:, hc, 0 if isq else 2, :]
        st = self.tabs[:, hc, 1 if isq else 3, :]
        b.dW.wait(VE)
        VE.tensor_tensor(self.m1, d, ct, ALU.mult)
        b.mm.wait(VE)
        VE.tensor_tensor(self.m2, b.ps[:, 5, :], st, ALU.mult)
        i = VE.tensor_tensor(d, self.m1, self.m2, ALU.add)
        b.ev.inc(i)
        b.bg[5] = [(b.ev, b.ev.n)]

    def run(self, ybf):
        self.ybf = ybf
        NU = 32
        for u in range(NU):
            self.s0(u)
            if u >= 1:
                self.ss(u - 1)
            if u >= 2:
                self.bc(u - 2)
            if u >= 3:
                self.pm(u - 3)
        self.ss(NU - 1)
        self.bc(NU - 2)
        self.pm(NU - 3)
        self.bc(NU - 1)
        self.pm(NU - 2)
        self.pm(NU - 1)


def la_stage1(b, hp, w, mts, qf, kf, abuf):
    TE, VE, SE = b.TE, b.VE, b.SE
    ps = b.ps
    def sc_slot(mt):
        if mt == 0:
            return 0, 0, 128, 0, 128
        if mt == 1:
            return 0, 128, 384, 0, 256
        if mt == 2:
            return 1, 0, 256, 0, 256
        return 1, 256, 384, 128, 128
    for bk in range(4):
        bank_wait(b, bk)
    p0 = lambda h01: h01 * 64
    qs = lambda h01, n0, nw: qf[p0(h01):p0(h01) + 64, hp,
                                w * 256 + n0:w * 256 + n0 + nw]
    ks = lambda h01, mt: kf[p0(h01):p0(h01) + 64, hp,
                            w * 256 + mt * 128:w * 256 + (mt + 1) * 128]
    for mt in mts:
        sb, c0, c1, n0, nw = sc_slot(mt)
        TE.matmul(ps[:, sb, c0:c1], ks(0, mt), qs(0, n0, nw),
                  start=True, stop=True)
        i = TE.matmul(ps[:, 2 + sb, c0:c1], ks(1, mt), qs(1, n0, nw),
                      start=True, stop=True)
    b.mm.inc(i)
    b.mm.wait(SE)
    b.ev.wait(SE)
    sbs = (0, 1) if 0 in mts else (1,)
    for h01 in (0, 1):
        for sb in sbs:
            i = SE.activation(abuf[h01][:, sb, 0:384],
                              ps[:, 2 * h01 + sb, 0:384], AF.Exp, scale=8.0,
                              bias=b.cbias[:, 3:4])
    b.es.inc(i)
    for bk in range(4):
        b.bg[bk] = [(b.es, b.es.n)]
    b.es.wait(VE)
    for h01 in (0, 1):
        a = abuf[h01]
        if 0 in mts:
            VE.tensor_tensor(a[:, 0, 0:128], a[:, 0, 0:128], b.c_tria,
                             ALU.mult)
            VE.tensor_tensor(a[:, 0, 256:384], a[:, 0, 256:384], b.c_tria,
                             ALU.mult)
        VE.tensor_tensor(a[:, 1, 0:128], a[:, 1, 0:128], b.c_tric, ALU.mult)
        i = VE.tensor_tensor(a[:, 1, 256:384], a[:, 1, 256:384], b.c_tric,
                             ALU.mult)
    b.ev.inc(i)
    return (b.es.n, b.ev.n)


def la_stage2(b, hp, w, mts, par, abuf, guards, v65, zinv2, otmp, oddt, ocat):
    TE, VE, SE = b.TE, b.VE, b.SE
    ps = b.ps
    es_a, ev_a = guards
    amap = {0: (0, 0, 128, 0, 128), 1: (0, 128, 384, 0, 256),
            2: (1, 0, 256, 0, 256), 3: (1, 256, 384, 128, 128)}
    bank_wait(b, 4)
    bank_wait(b, 5)
    b.es.wait(TE, es_a)
    b.ev.wait(TE, ev_a)
    for h01 in (0, 1):
        for n_mt, mt in enumerate(mts):
            sb, c0, c1, n0, nw = amap[mt]
            i = TE.matmul(ps[0:65, 4 + h01, n0:n0 + nw],
                          v65[:, 2 * w + mt, 2 * hp + h01, :],
                          abuf[h01][:, sb, c0:c1],
                          start=(n_mt == 0), stop=(n_mt == len(mts) - 1),
                          skip_group_check=True)
    b.mm.inc(i)
    b.mm.wait(SE)
    zt = b.la_ztmp
    SE.activation(zt[0:1, 0:256], ps[64:65, 4, 0:256], AF.Ln)
    SE.activation(zt[0:1, 256:512], ps[64:65, 5, 0:256], AF.Ln)
    i = SE.activation(zinv2[0:1, :, :], zt[0:1, 0:512], AF.Exp, scale=-1.0)
    b.es.inc(i)
    bank_wait(b, 6)
    b.es.wait(TE)
    TE.matmul(ps[0:64, 6, 0:256], b.c_ones(64, 1, 64), zinv2[0:1, 0, :],
              start=True, stop=True)
    i = TE.matmul(ps[0:64, 6, 256:512], b.c_ones(64, 1, 64), zinv2[0:1, 1, :],
                  start=True, stop=True, skip_group_check=True)
    b.mm.inc(i)
    b.mm.wait(SE)
    b.ev.wait(SE)
    SE.copy(otmp[0:64, 0:512][:, 0:256], ps[0:64, 4, 0:256])
    i = SE.copy(otmp[0:64, 256:512], ps[0:64, 5, 0:256])
    b.es.inc(i)
    b.bg[4] = [(b.es, b.es.n)]
    b.bg[5] = [(b.es, b.es.n)]
    b.es.wait(VE)
    VE.tensor_tensor(ocat[0:64, hp, w * 256:(w + 1) * 256], otmp[0:64, 0:256],
                     ps[0:64, 6, 0:256], ALU.mult)
    i = VE.tensor_tensor(oddt[0:64, 0:256], otmp[0:64, 256:512],
                         ps[0:64, 6, 256:512], ALU.mult)
    b.ev.inc(i)
    b.bg[6] = [(b.ev, b.ev.n)]
    bank_wait(b, 7)
    b.ev.wait(TE)
    i = TE.matmul(ps[:, 7, 0:256], b.c_SHIFT, oddt[0:64, 0:256],
                  start=True, stop=True)
    b.mm.inc(i)
    b.mm.wait(SE)
    i = SE.copy(ocat[64:128, hp, w * 256:(w + 1) * 256], ps[64:128, 7, 0:256])
    b.es.inc(i)
    b.bg[7] = [(b.es, b.es.n)]


def emit_attn_b(b):
    TE, VE, SE, GE, SY = b.TE, b.VE, b.SE, b.GE, b.SY
    ps, v = b.ps, b.view
    dr = b.dr

    join_all(b, (TE, VE, SE, GE, SY))

    b.dW.inc(GE.dma_start(out=b.pmk_b[:, :, :], in_=dr["pmk"]), 16)
    b.dW.inc(GE.dma_start(out=b.pmv_b[:, :, :], in_=dr["pmv"]), 16)

    u = v(0, BF16, KT, 512)
    qseg = v(8192, BF16, KT, 512)
    kseg = v(16384, BF16, KT, 512)
    v65b = v(24576, BF16, 4, H, HD + 1)
    ocat = v(32896, BF16, KT, 512)
    abuf = [v(41088, BF16, 512), v(42112, BF16, 512)]
    apm_e = v(43136, BF16, 512, p0=0, np_=16)
    apm_o = v(44160, BF16, 512, p0=0, np_=16)
    otmp_e = v(45184, BF16, 512)
    otmp_o = v(46208, BF16, 512)
    zinv_e = v(47232, BF16, 512, p0=64, np_=1)
    zinv_o = v(48256, BF16, 512, p0=64, np_=1)
    zinv_eo = v(47232, BF16, 1024, p0=64, np_=1)
    ztmpB = v(69760, F32, 1024, p0=64, np_=1)
    tabs_b = v(49280, BF16, 2, 512)
    m1t = v(51328, BF16, 512)
    m2t = v(52352, BF16, 512)
    wpan = v(53376, BF16, 2, KT, 512)
    lnb = mk_lnb(b, 24576, 28800, 41088)

    for bb in range(NB):
        for s in range(4):
            S0 = s * 512
            join_all(b, (GE, SY))
            srcs = [b.xs[:, bb, k, 1 + S0:1 + S0 + 512] for k in range(KT)]
            dsts = [u[:, k, :] for k in range(KT)]
            ln_chunk(b, srcs, dsts, 512, 1, lnb, gamma=b.gb_sb,
                     want_mean=False)
            u_done = b.ev.n
            b.ev.wait(GE)
            b.dW.inc(GE.dma_start(
                out=tabs_b,
                in_=dr["ropeb"][:, :, S0:S0 + 512].rearrange(
                    "a p t -> p a t")), 16)
            b.ev.wait(SE, u_done)
            for jc in range(6):
                slot = jc % 2
                if jc >= 2:
                    b.mm.wait(GE)
                b.dW.inc(GE.dma_start(out=wpan[:, slot],
                                      in_=dr["wqkvT"][jc],
                                      max_dma_last_dim=8192), 16)
                b.dW.wait(TE)
                b.ev.wait(TE, u_done)
                if jc < 4:
                    isq = jc < 2
                    dstt = qseg if isq else kseg
                    for jsub in range(4):
                        jt = (jc % 2) * 4 + jsub
                        bank = jsub % 2
                        bank_wait(b, bank)
                        for k in range(KT):
                            i = TE.matmul(
                                ps[:, bank, :],
                                wpan[:, slot, k, jsub * 128:(jsub + 1) * 128],
                                u[:, k, :], start=(k == 0), stop=(k == KT - 1))
                        b.mm.inc(i)
                        dsl = dstt[:, jt, :]
                        b.mm.wait(SE)
                        b.ev.wait(SE)
                        i = SE.copy(dsl, ps[:, bank, :])
                        b.es.inc(i)
                        b.bg[bank] = [(b.es, b.es.n)]
                        bank_wait(b, 7)
                        b.es.wait(TE)
                        i = TE.matmul(ps[:, 7, :], b.c_PI, dsl,
                                      start=True, stop=True)
                        b.mm.inc(i)
                        b.dW.wait(VE)
                        b.es.wait(VE)
                        VE.tensor_tensor(m1t, dsl, tabs_b[:, 0, :], ALU.mult)
                        b.mm.wait(VE)
                        VE.tensor_tensor(m2t, ps[:, 7, :], tabs_b[:, 1, :],
                                         ALU.mult)
                        i = VE.tensor_tensor(dsl, m1t, m2t, ALU.add)
                        b.ev.inc(i)
                        b.bg[7] = [(b.ev, b.ev.n)]
                else:
                    for tt in range(4):
                        bank = tt % 2
                        bank_wait(b, bank)
                        for k in range(KT):
                            i = TE.matmul(
                                ps[:, bank, :],
                                u[:, k, tt * 128:(tt + 1) * 128],
                                wpan[:, slot, k, :],
                                start=(k == 0), stop=(k == KT - 1))
                        b.mm.inc(i)
                        b.mm.wait(SE)
                        i = SE.copy(
                            v65b[:, tt, (jc - 4) * 8:(jc - 4) * 8 + 8, 0:HD],
                            ps[:, bank, :])
                        b.es.inc(i)
                        b.bg[bank] = [(b.es, b.es.n)]
            b.es.wait(VE)
            b.ev.inc(VE.memset(v65b[:, :, :, HD:HD + 1], 1.0))
            ones_done = b.ev.n
            b.ev.wait(TE, ones_done)
            b.dW.wait(TE)

            for hp in range(KT):
                for bk in (0, 1, 2, 3):
                    bank_wait(b, bk)
                qsl = lambda h01, c0, cw: qseg[h01 * 64:h01 * 64 + 64, hp,
                                               c0:c0 + cw]
                i = TE.matmul(ps[0:16, 0, :], b.pmk_b[0:64, hp, :],
                              qsl(0, 0, 512), start=True, stop=True)
                i = TE.matmul(ps[0:16, 1, :], b.pmk_b[64:128, hp, :],
                              qsl(1, 0, 512), start=True, stop=True)
                b.mm.inc(i)
                b.mm.wait(SE)
                b.ev.wait(SE)
                i = SE.activation(apm_e[0:16, :], ps[0:16, 0, :], AF.Exp,
                                  scale=0.125, bias=b.cbias[0:16, 4:5])
                i = SE.activation(apm_o[0:16, :], ps[0:16, 1, :], AF.Exp,
                                  scale=0.125, bias=b.cbias[0:16, 4:5])
                b.es.inc(i)
                b.bg[0] = [(b.es, b.es.n)]
                b.bg[1] = [(b.es, b.es.n)]
                b.es.wait(TE)
                TE.matmul(ps[0:65, 2, :], b.pmv_b[:, 2 * hp, :],
                          apm_e[0:16, :], start=True, stop=False,
                          skip_group_check=True)
                i = TE.matmul(ps[0:65, 3, :], b.pmv_b[:, 2 * hp + 1, :],
                              apm_o[0:16, :], start=True, stop=False,
                              skip_group_check=True)
                b.mm.inc(i)
                for kt in range(4):
                    cw = 512 - kt * 128
                    bank_wait(b, 0)
                    bank_wait(b, 1)
                    ksl = lambda h01: kseg[h01 * 64:h01 * 64 + 64, hp,
                                           kt * 128:(kt + 1) * 128]
                    i = TE.matmul(ps[:, 0, kt * 128:512], ksl(0),
                                  qsl(0, kt * 128, cw), start=True, stop=True)
                    i = TE.matmul(ps[:, 1, kt * 128:512], ksl(1),
                                  qsl(1, kt * 128, cw), start=True, stop=True)
                    b.mm.inc(i)
                    b.mm.wait(SE)
                    b.ev.wait(SE)
                    i = SE.activation(abuf[0][:, kt * 128:512],
                                      ps[:, 0, kt * 128:512], AF.Exp,
                                      scale=0.125, bias=b.cbias[:, 4:5])
                    i = SE.activation(abuf[1][:, kt * 128:512],
                                      ps[:, 1, kt * 128:512], AF.Exp,
                                      scale=0.125, bias=b.cbias[:, 4:5])
                    b.es.inc(i)
                    b.bg[0] = [(b.es, b.es.n)]
                    b.bg[1] = [(b.es, b.es.n)]
                    b.es.wait(VE)
                    VE.tensor_tensor(abuf[0][:, kt * 128:(kt + 1) * 128],
                                     abuf[0][:, kt * 128:(kt + 1) * 128],
                                     b.c_tric, ALU.mult)
                    i = VE.tensor_tensor(abuf[1][:, kt * 128:(kt + 1) * 128],
                                         abuf[1][:, kt * 128:(kt + 1) * 128],
                                         b.c_tric, ALU.mult)
                    b.ev.inc(i)
                    b.ev.wait(TE)
                    TE.matmul(ps[0:65, 2, kt * 128:512],
                              v65b[:, kt, 2 * hp, :], abuf[0][:, kt * 128:512],
                              start=False, stop=(kt == 3),
                              skip_group_check=True)
                    i = TE.matmul(ps[0:65, 3, kt * 128:512],
                                  v65b[:, kt, 2 * hp + 1, :],
                                  abuf[1][:, kt * 128:512],
                                  start=False, stop=(kt == 3),
                                  skip_group_check=True)
                    b.mm.inc(i)
                b.mm.wait(SE)
                SE.activation(ztmpB[0:1, 0:512], ps[64:65, 2, :], AF.Ln)
                SE.activation(ztmpB[0:1, 512:1024], ps[64:65, 3, :], AF.Ln)
                i = SE.activation(zinv_eo[0:1, :], ztmpB[0:1, :], AF.Exp,
                                  scale=-1.0)
                b.es.inc(i)
                bank_wait(b, 4)
                bank_wait(b, 5)
                b.es.wait(TE)
                TE.matmul(ps[0:64, 4, :], b.c_ones(64, 1, 64), zinv_e[0:1, :],
                          start=True, stop=True)
                i = TE.matmul(ps[0:64, 5, :], b.c_ones(64, 1, 64),
                              zinv_o[0:1, :], start=True, stop=True)
                b.mm.inc(i)
                b.mm.wait(SE)
                SE.copy(otmp_e[0:64, :], ps[0:64, 2, :])
                i = SE.copy(otmp_o[0:64, :], ps[0:64, 3, :])
                b.es.inc(i)
                b.bg[2] = [(b.es, b.es.n)]
                b.bg[3] = [(b.es, b.es.n)]
                b.es.wait(VE)
                VE.tensor_tensor(ocat[0:64, hp, :], otmp_e[0:64, :],
                                 ps[0:64, 4, :], ALU.mult)
                i = VE.tensor_tensor(m1t[0:64, :], otmp_o[0:64, :],
                                     ps[0:64, 5, :], ALU.mult)
                b.ev.inc(i)
                b.bg[4] = [(b.ev, b.ev.n)]
                b.bg[5] = [(b.ev, b.ev.n)]
                bank_wait(b, 6)
                b.ev.wait(TE)
                i = TE.matmul(ps[:, 6, :], b.c_SHIFT, m1t[0:64, :],
                              start=True, stop=True)
                b.mm.inc(i)
                b.mm.wait(SE)
                i = SE.copy(ocat[64:128, hp, :], ps[64:128, 6, :])
                b.es.inc(i)
                b.bg[6] = [(b.es, b.es.n)]

            ocat_done = b.es.n
            b.es.wait(GE)
            b.ev.wait(GE)
            wo_guard = [(None, 0), (None, 0)]
            for ft in range(KT):
                slot = ft % 2
                g = wo_guard[slot]
                if g[0] is not None:
                    g[0].wait(GE, g[1])
                b.dW.inc(GE.dma_start(out=wpan[:, slot, :, 0:128],
                                      in_=dr["woutT"][ft]), 16)
                b.dW.wait(TE)
                if ft == 0:
                    b.es.wait(TE, ocat_done)
                bank = ft % 4
                bank_wait(b, bank)
                for jt in range(KT):
                    i = TE.matmul(ps[:, bank, :], wpan[:, slot, jt, 0:128],
                                  ocat[:, jt, :], start=(jt == 0),
                                  stop=(jt == KT - 1))
                b.mm.inc(i)
                wo_guard[slot] = (b.mm, b.mm.n)
                b.mm.wait(VE)
                i = VE.tensor_tensor(b.xs[:, bb, ft, 1 + S0:1 + S0 + 512],
                                     b.xs[:, bb, ft, 1 + S0:1 + S0 + 512],
                                     ps[:, bank, :], ALU.add)
                b.ev.inc(i)
                b.bg[bank] = [(b.ev, b.ev.n)]


def kernel(**inputs):
    in_maps = host_prep(inputs)
    nc = build(UPTO)
    res = run_bass_kernel_spmd(nc, in_maps, core_ids=list(range(8)))
    out = np.zeros((16, N, D), np.float32)
    for c in range(8):
        xo = res.results[c]["x_out"]
        out[NB * c:NB * c + NB] = (
            xo.transpose(1, 0, 2, 3).reshape(D, NB, N).transpose(1, 2, 0))
    return out


# version 04
# speedup vs baseline: 1.7871x; 1.0296x over previous
import numpy as np
import concourse.bass as bass
import concourse.mybir as mybir
from concourse.bass_utils import run_bass_kernel_spmd

F32 = mybir.dt.float32
BF16 = mybir.dt.bfloat16
AF = mybir.ActivationFunctionType
ALU = mybir.AluOpType

D = 1024
KT = 8
NB = 2
N = 2048
NP = N + 1
H = 16
HD = 64
DEPTH = 6
DI = 2730
FT1 = 22
WIN = 256
SEG = 512
PMEM = 16

B_ATTA = 4.0
B_LOC = 24.0
B_ATTB = 8.0
LN_EPS = 1e-5
RMS_EPS = 1.1920929e-07

UPTO = 4
FFN_DEPTH = DEPTH
FFN_ATTN_ONLY = False
DEBUG2 = False
DEBUG = False



def _host_tables():
    t = np.arange(N, dtype=np.float64)
    half = np.arange(0, HD, 2, dtype=np.float64)
    inv_freq = 1.0 / 10000.0 ** (half / HD)
    freqs = np.concatenate([t[:, None] * inv_freq[None, :]] * 2, -1)
    base_scale = (half + 0.4 * HD) / (1.4 * HD)
    power = (t - N // 2) / (WIN // 2)
    scale = base_scale[None, :] ** power[:, None]
    scale = np.concatenate([scale, scale], -1)
    cos, sin = np.cos(freqs), np.sin(freqs)
    xpos = np.stack([np.tile(a.T, (2, 1)) for a in
                     (cos * scale, sin * scale, cos / scale, sin / scale)])

    fb = np.repeat(t[:, None] * inv_freq[None, :], 2, axis=-1)
    ropeb = np.stack([np.tile(np.cos(fb).T, (2, 1)),
                      np.tile(np.sin(fb).T, (2, 1))])

    PH = np.zeros((128, 128))
    PI = np.zeros((128, 128))
    for blk in (0, 64):
        for m in range(32):
            PH[blk + m + 32, blk + m] = -1.0
            PH[blk + m, blk + m + 32] = 1.0
        for i in range(32):
            PI[blk + 2 * i + 1, blk + 2 * i] = -1.0
            PI[blk + 2 * i, blk + 2 * i + 1] = 1.0
    SHIFT64 = np.zeros((128, 128))
    for d_ in range(64):
        SHIFT64[d_, 64 + d_] = 1.0

    p_ = np.arange(128)[:, None]
    f_ = np.arange(128)[None, :]
    tri_causal = (p_ <= f_).astype(np.float64)
    tri_anti = (p_ >= f_).astype(np.float64)
    return dict(xpos=xpos, ropeb=ropeb, PH=PH, PI=PI, SHIFT64=SHIFT64,
                tri_causal=tri_causal, tri_anti=tri_anti)


C_PH = 0
C_PI = 128
C_SHIFT = 256
C_TRIC = 384
C_TRIA = 512
C_ONES = 640
C_ONES2 = 768
C_SEL2 = 772
CC_TOT = 900


def _consts_array():
    tb = _host_tables()
    c = np.zeros((128, CC_TOT))
    c[:, C_PH:C_PH + 128] = tb["PH"]
    c[:, C_PI:C_PI + 128] = tb["PI"]
    c[:, C_SHIFT:C_SHIFT + 128] = tb["SHIFT64"]
    c[:, C_TRIC:C_TRIC + 128] = tb["tri_causal"]
    c[:, C_TRIA:C_TRIA + 128] = tb["tri_anti"]
    c[:, C_ONES:C_ONES + 128] = 1.0
    c[0:64, C_ONES2] = 1.0
    c[64:128, C_ONES2 + 1] = 1.0
    c[64, C_SEL2:C_SEL2 + 64] = 1.0
    c[65, C_SEL2 + 64:C_SEL2 + 128] = 1.0
    return c.astype(np.float32), tb


def _pmajor(w, jc):
    ind, outd = w.shape
    return np.ascontiguousarray(
        w.reshape(ind // 128, 128, outd // jc, jc).transpose(2, 1, 0, 3))


def host_prep(inputs, n_cores=8):
    f32 = np.float32
    x = np.asarray(inputs["L0_attn_a_x"], f32)

    WqT = _pmajor(np.asarray(inputs["L0_attn_a_Wq"], f32).T, 128)
    WkT = _pmajor(np.asarray(inputs["L0_attn_a_Wk"], f32).T, 128)
    WvT = _pmajor(np.asarray(inputs["L0_attn_a_Wv"], f32).T, 256)
    WoT = _pmajor(np.asarray(inputs["L0_attn_a_Wo"], f32).T, 128)
    b_a = np.zeros((128, 4 * KT), f32)
    for i, nm in enumerate(["bq", "bk", "bv", "bo"]):
        b_a[:, i * KT:(i + 1) * KT] = (
            np.asarray(inputs[f"L0_attn_a_{nm}"], f32).reshape(KT, 128).T)
    bv_row = np.asarray(inputs["L0_attn_a_bv"], f32).reshape(1, D)

    wqkvT = _pmajor(np.asarray(inputs["L0_attn_b_w_qkv"], f32).T, 512)
    woutT = _pmajor(np.asarray(inputs["L0_attn_b_w_out"], f32).T, 128)
    g_b = np.asarray(inputs["L0_attn_b_g"], f32).reshape(KT, 128).T
    pm = np.asarray(inputs["L0_attn_b_pm"], f32)
    pmk = np.zeros((128, KT, PMEM), f32)
    pmv = np.zeros((16, H, HD + 1), f32)
    for h in range(H):
        pmk[(h % 2) * 64:(h % 2) * 64 + 64, h // 2, :] = pm[0, h].T
        pmv[:, h, 0:HD] = pm[1, h]
    pmv[:, :, HD] = 1.0

    Wqkv = np.stack([_pmajor(np.asarray(inputs["L0_ffn_Wqkv"], f32)[l], 512)
                     for l in range(DEPTH)])
    Wo_f = np.stack([
        _pmajor(np.asarray(inputs["L0_ffn_Wo"], f32)[l], 128)
        .transpose(1, 2, 0, 3).reshape(128, KT, D)
        for l in range(DEPTH)])
    W1 = np.asarray(inputs["L0_ffn_W1"], f32)
    W1p = np.zeros((DEPTH, FT1, 128, 2, KT, 128), f32)
    for l in range(DEPTH):
        for hi in range(2):
            w = np.zeros((D, FT1 * 128), f32)
            w[:, :DI] = W1[l][:, hi * DI:(hi + 1) * DI]
            W1p[l, :, :, hi] = (
                w.reshape(KT, 128, FT1, 128).transpose(2, 1, 0, 3))
    W2 = np.asarray(inputs["L0_ffn_W2"], f32)
    W2p = np.zeros((DEPTH, KT, 128, FT1, 128), f32)
    for l in range(DEPTH):
        w2 = np.zeros((FT1 * 128, D), f32)
        w2[:DI] = W2[l]
        W2p[l] = w2.reshape(FT1, 128, KT, 128).transpose(2, 1, 0, 3)

    small = np.zeros((DEPTH, 128, 70), f32)
    qs = np.asarray(inputs["L0_ffn_q_scale"], f32)
    ks = np.asarray(inputs["L0_ffn_k_scale"], f32)
    b1 = np.asarray(inputs["L0_ffn_b1"], f32)
    for l in range(DEPTH):
        small[l, :, 0] = np.tile(qs[l], 2)
        small[l, :, 1] = np.tile(ks[l], 2)
        small[l, :, 2:10] = np.asarray(
            inputs["L0_ffn_ln_g"], f32)[l].reshape(KT, 128).T
        small[l, :, 10:18] = np.asarray(
            inputs["L0_ffn_ln_b"], f32)[l].reshape(KT, 128).T
        b1a = np.zeros(FT1 * 128, f32)
        b1g = np.zeros(FT1 * 128, f32)
        b1a[:DI] = b1[l][:DI]
        b1g[:DI] = b1[l][DI:]
        small[l, :, 18:40] = b1a.reshape(FT1, 128).T
        small[l, :, 40:62] = b1g.reshape(FT1, 128).T
        small[l, :, 62:70] = np.asarray(
            inputs["L0_ffn_b2"], f32)[l].reshape(KT, 128).T

    consts, tb = _consts_array()
    xpos_t = tb["xpos"].astype(f32)
    ropeb_t = tb["ropeb"].astype(f32)

    Wqkv = Wqkv.reshape(DEPTH * 6, 128, KT * 512)
    W1p = W1p.reshape(DEPTH * FT1, 128, 2 * KT * 128)
    W2p = W2p.reshape(DEPTH * KT, 128, FT1 * 128)
    in_maps = []
    for c in range(n_cores):
        xc = x[NB * c:NB * c + NB]
        xfm = np.ascontiguousarray(
            xc.transpose(2, 0, 1).reshape(KT, 128, NB, N).transpose(1, 0, 2, 3))
        in_maps.append(dict(
            x_in=xfm, WqT=WqT, WkT=WkT, WvT=WvT, WoT=WoT, b_a=b_a,
            bv_row=bv_row, wqkvT=wqkvT, woutT=woutT, g_b=g_b, pmk=pmk,
            pmv=pmv, Wqkv=Wqkv, Wo_f=Wo_f, W1p=W1p, W2p=W2p, small=small,
            consts=consts, xpos=xpos_t, ropeb=ropeb_t))
    return in_maps



class Sem:
    def __init__(self, stack, nc, name):
        self.h = stack.enter_context(nc.semaphore(name))
        self.n = 0

    def inc(self, inst, k=1):
        inst.then_inc(self.h, k)
        self.n += k
        return inst

    def wait(self, eng, val=None):
        eng.wait_ge(self.h, self.n if val is None else val)


class B:
    pass


def build(upto=UPTO):
    import contextlib
    st = contextlib.ExitStack()
    nc = bass.Bass("TRN2", target_bir_lowering=False)
    b = B()
    b.nc = nc
    b.TE, b.VE, b.SE, b.GE, b.SY = (
        nc.tensor, nc.vector, nc.scalar, nc.gpsimd, nc.sync)

    dt = nc.dram_tensor
    dr = {}
    def di(name, shape):
        dr[name] = dt(name, shape, F32, kind="ExternalInput").ap()
    di("x_in", [128, KT, NB, N])
    di("WqT", [KT, 128, KT, 128])
    di("WkT", [KT, 128, KT, 128])
    di("WvT", [4, 128, KT, 256])
    di("WoT", [KT, 128, KT, 128])
    di("b_a", [128, 4 * KT])
    di("bv_row", [1, D])
    di("wqkvT", [6, 128, KT, 512])
    di("woutT", [KT, 128, KT, 128])
    di("g_b", [128, KT])
    di("pmk", [128, KT, PMEM])
    di("pmv", [16, H, HD + 1])
    di("Wqkv", [DEPTH * 6, 128, KT * 512])
    di("Wo_f", [DEPTH, 128, KT, D])
    di("W1p", [DEPTH * FT1, 128, 2 * KT * 128])
    di("W2p", [DEPTH * KT, 128, FT1 * 128])
    di("small", [DEPTH, 128, 70])
    di("consts", [128, CC_TOT])
    di("xpos", [4, 128, N])
    di("ropeb", [2, 128, N])
    dr["x_out"] = dt("x_out", [128, KT, NB, N], F32, kind="ExternalOutput").ap()
    if DEBUG2:
        for nm, shp in (("e_y", [128, KT * 1024]), ("e_q", [128, KT * 1024]),
                        ("e_k", [128, KT * 1280]), ("e_v", [128, 10 * H * (HD + 1)]),
                        ("e_o", [128, KT * 1024])):
            dr[nm] = dt(nm, shp, BF16, kind="ExternalOutput").ap()
    if DEBUG:
        for nm, shp in (("d_u", [128, KT, N]), ("d_q", [128, KT, N]),
                        ("d_k", [128, KT, N]), ("d_v", [128, 16, H, HD + 1]),
                        ("d_o", [128, KT, N])):
            dr[nm] = dt(nm, shp, BF16, kind="ExternalOutput").ap()
    dr["x_save"] = dt("x_save", [NB, 128, KT, N], F32, kind="Internal").ap()
    dr["h_sc"] = nc.dram_tensor("h_sc", [128, FT1, N], BF16, kind="Internal").ap()
    b.dr = dr

    xs = st.enter_context(nc.sbuf_tensor("xs", [128, NB, KT, NP], F32))
    WB = 74240
    b.WB = WB
    arena = st.enter_context(nc.sbuf_tensor("arena", [128, WB], mybir.dt.uint8))
    onesf = st.enter_context(nc.sbuf_tensor("onesf", [128, 128], F32))
    cst_b = st.enter_context(nc.sbuf_tensor("cst_b", [128, CC_TOT], BF16))
    smalls = st.enter_context(nc.sbuf_tensor("smalls", [128, 70], F32))
    ba_sb = st.enter_context(nc.sbuf_tensor("ba_sb", [128, 4 * KT], F32))
    gb_sb = st.enter_context(nc.sbuf_tensor("gb_sb", [128, KT], F32))
    bvb = st.enter_context(nc.sbuf_tensor("bvb", [1, D], BF16))
    pmk_b = st.enter_context(nc.sbuf_tensor("pmk_b", [128, KT, PMEM], BF16))
    pmv_b = st.enter_context(nc.sbuf_tensor("pmv_b", [16, H, HD + 1], BF16))
    cbias = st.enter_context(nc.sbuf_tensor("cbias", [128, 8], F32))
    ps = st.enter_context(nc.psum_tensor("ps", [128, 8, 512], F32))
    b.xs, b.arena, b.onesf, b.cst_b, b.smalls, b.ps = (
        xs, arena, onesf, cst_b, smalls, ps)
    b.ba_sb, b.gb_sb, b.bvb, b.cbias = ba_sb, gb_sb, bvb, cbias
    b.pmk_b, b.pmv_b = pmk_b, pmv_b

    def view(off, dtype, *shape, p0=0, np_=128):
        esz = 4 if dtype == F32 else 2
        nby = int(np.prod(shape)) * esz
        assert off + nby <= WB, (off, nby, WB)
        ap = arena[p0:p0 + np_, off:off + nby].bitcast(dtype)
        if len(shape) > 1:
            names = " ".join(f"s{i}" for i in range(len(shape)))
            ap = ap.rearrange(f"p ({names}) -> p {names}",
                              **{f"s{i}": v for i, v in enumerate(shape)})
        return ap
    b.view = view

    b.dW = Sem(st, nc, "dW")
    b.dS = Sem(st, nc, "dS")
    b.mm = Sem(st, nc, "mm")
    b.ev = Sem(st, nc, "ev")
    b.es = Sem(st, nc, "es")
    b.ep = Sem(st, nc, "ep")
    b.bar = Sem(st, nc, "bar")

    b.c_PH = cst_b[:, C_PH:C_PH + 128]
    b.c_PI = cst_b[:, C_PI:C_PI + 128]
    b.c_SHIFT = cst_b[0:64, C_SHIFT:C_SHIFT + 128]
    b.c_tric = cst_b[:, C_TRIC:C_TRIC + 128]
    b.c_tria = cst_b[:, C_TRIA:C_TRIA + 128]
    b.c_ones = lambda p0, np_, w: cst_b[p0:p0 + np_, C_ONES:C_ONES + w]
    b.c_ones2 = cst_b[:, C_ONES2:C_ONES2 + 2]
    b.c_sel2 = cst_b[64:66, C_SEL2:C_SEL2 + 128]
    b.c_onesf = lambda p0, np_, w: onesf[p0:p0 + np_, 0:w]

    b.bg = {k: [] for k in range(8)}

    emit_prologue(b)
    if upto >= 1:
        emit_attn_a(b)
    if upto >= 2:
        emit_ffn6(b, 0)
    if upto >= 3:
        emit_attn_b(b)
    if upto >= 4:
        emit_ffn6(b, 1)
    emit_epilogue(b)
    st.close()
    return nc


def join_all(b, engines):
    for eng in engines:
        for sem in (b.mm, b.ev, b.es, b.ep, b.dW, b.dS):
            if sem.n:
                sem.wait(eng)


def bank_wait(b, bank):
    for sem, val in b.bg[bank]:
        sem.wait(b.TE, val)
    b.bg[bank] = []


def emit_prologue(b):
    SY, VE, GE = b.SY, b.VE, b.GE
    b.dW.inc(GE.dma_start(out=b.cst_b[:, :], in_=b.dr["consts"]), 16)
    b.dW.inc(GE.dma_start(out=b.bvb[:, :], in_=b.dr["bv_row"]), 16)
    b.dS.inc(SY.dma_start(out=b.ba_sb[:, :], in_=b.dr["b_a"]), 16)
    b.dS.inc(SY.dma_start(out=b.gb_sb[:, :], in_=b.dr["g_b"]), 16)
    for bb in range(NB):
        b.dS.inc(SY.dma_start(out=b.xs[:, bb, :, 1:NP],
                              in_=b.dr["x_in"][:, :, bb, :]), 16)
    b.dS.wait(VE)
    b.dW.wait(VE)
    VE.memset(b.onesf[:, :], 1.0)
    for i, val in enumerate([LN_EPS, RMS_EPS, -B_ATTA, -B_LOC, -B_ATTB,
                             1e-18, 0.0, 0.0]):
        VE.memset(b.cbias[:, i:i + 1], float(val))
    b.ev.inc(VE.memset(b.xs[:, :, :, 0:1], 0.0))


def emit_epilogue(b):
    SY = b.SY
    b.ev.wait(SY)
    b.es.wait(SY)
    b.mm.wait(SY)
    b.dW.wait(SY)
    b.ep.wait(SY)
    for bb in range(NB):
        b.dS.inc(SY.dma_start(out=b.dr["x_out"][:, :, bb, :],
                              in_=b.xs[:, bb, :, 1:NP]), 16)
    b.dS.wait(SY)



def ln_chunk(b, srcs, dsts, width, eps_col, lnb, gamma=None, beta=None,
             want_mean=True):
    TE, VE, SE = b.TE, b.VE, b.SE
    b.mm.wait(SE)
    for k in range(KT):
        i = SE.activation(lnb["sq"][:, k, 0:width], srcs[k], AF.Square)
    b.es.inc(i)
    bank_wait(b, 7)
    bank_wait(b, 6)
    if want_mean:
        for k in range(KT):
            TE.matmul(b.ps[0:1, 7, 0:width], b.c_onesf(0, 128, 1), srcs[k],
                      start=(k == 0), stop=(k == KT - 1))
    b.es.wait(TE)
    for k in range(KT):
        i = TE.matmul(b.ps[0:1, 6, 0:width], b.c_ones(0, 128, 1),
                      lnb["sq"][:, k, 0:width],
                      start=(k == 0), stop=(k == KT - 1))
    b.mm.inc(i)
    b.mm.wait(VE)
    ra, rb, rc, rsd, rrec = (lnb[n][0:1, 0:width]
                             for n in ("ra", "rb", "rc", "rsd", "rrec"))
    rstd, mrs = lnb["rstd"][0:1, 0:width], lnb["mrs"][0:1, 0:width]
    if want_mean:
        VE.tensor_scalar(ra, b.ps[0:1, 7, 0:width], 1.0 / D, None, ALU.mult)
        VE.tensor_tensor(rb, ra, ra, ALU.mult)
        VE.tensor_scalar(rc, b.ps[0:1, 6, 0:width], 1.0 / D, None, ALU.mult)
        i = VE.tensor_tensor(rc, rc, rb, ALU.subtract)
    else:
        i = VE.tensor_scalar(rc, b.ps[0:1, 6, 0:width], 1.0 / D, None,
                             ALU.mult)
    b.ev.inc(i)
    b.ev.wait(SE)
    SE.activation(rsd, rc, AF.Ln, bias=b.cbias[0:1, eps_col:eps_col + 1])
    b.es.inc(SE.activation(rrec, rsd, AF.Exp, scale=-0.5))
    b.es.wait(VE)
    if want_mean:
        i = VE.tensor_tensor(rb, ra, rrec, ALU.mult)
        b.ev.inc(i)
    b.bg[7] = [(b.es, b.es.n)]
    b.bg[6] = [(b.es, b.es.n)]
    bank_wait(b, 5)
    bank_wait(b, 4)
    b.ev.wait(TE)
    b.es.wait(TE)
    i = TE.matmul(b.ps[:, 5, 0:width], b.c_onesf(0, 1, 128), rrec,
                  start=True, stop=True)
    if want_mean:
        i = TE.matmul(b.ps[:, 4, 0:width], b.c_onesf(0, 1, 128), rb,
                      start=True, stop=True)
    b.mm.inc(i)
    b.mm.wait(VE)
    for k in range(KT):
        t1 = lnb["t1"][:, 0:width]
        VE.tensor_tensor(t1, srcs[k], b.ps[:, 5, 0:width], ALU.mult)
        if want_mean:
            if gamma is None:
                i = VE.tensor_tensor(dsts[k], t1, b.ps[:, 4, 0:width],
                                     ALU.subtract)
            else:
                t2 = lnb["t2"][:, 0:width]
                VE.tensor_tensor(t2, t1, b.ps[:, 4, 0:width], ALU.subtract)
                i = VE.tensor_scalar(dsts[k], t2, gamma[:, k:k + 1],
                                     beta[:, k:k + 1], ALU.mult, ALU.add)
        else:
            if gamma is None:
                i = VE.tensor_copy(dsts[k], t1)
            else:
                i = VE.tensor_scalar(dsts[k], t1, gamma[:, k:k + 1], None,
                                     ALU.mult)
    b.ev.inc(i)
    b.bg[5] = [(b.ev, b.ev.n)]
    b.bg[4] = [(b.ev, b.ev.n)]


def mk_lnb(b, sq_off, rows_off, t_off, width=512, viewfn=None):
    v = viewfn if viewfn is not None else b.view
    return dict(
        sq=v(sq_off, BF16, KT, width),
        ra=v(rows_off, F32, width, p0=0, np_=1),
        rb=v(rows_off + 2048, F32, width, p0=0, np_=1),
        rc=v(rows_off + 4096, F32, width, p0=0, np_=1),
        rsd=v(rows_off + 6144, F32, width, p0=0, np_=1),
        rrec=v(rows_off + 8192, F32, width, p0=0, np_=1),
        rstd=v(rows_off + 10240, BF16, width, p0=0, np_=1),
        mrs=v(rows_off + 10240 + width * 2, BF16, width, p0=0, np_=1),
        t1=v(t_off, F32, width),
        t2=v(t_off + width * 4, F32, width),
    )



def emit_attn_a(b):
    TE, VE, SE, GE, SY = b.TE, b.VE, b.SE, b.GE, b.SY
    ps, v = b.ps, b.view
    dr = b.dr

    u = v(0, BF16, KT, N)
    ocat = v(0, BF16, KT, N)
    v65 = v(32768, BF16, 16, H, HD + 1)
    lnb = mk_lnb(b, 32768, 45056, 40960)
    qkpan = v(66048, BF16, 2, KT, 128)
    vpan = v(70144, BF16, KT, 256)
    abuf = v(66048, BF16, 2, 1024)
    otmp_e = v(70144, BF16, 512)
    otmp_o = v(70144 + 1024, BF16, 512)
    zinv_e = v(72192, BF16, 512, p0=64, np_=1)
    zinv_o = v(72192 + 1024, BF16, 512, p0=64, np_=1)
    zinv_eo = v(72192, BF16, 1024, p0=64, np_=1)
    wopan = v(66048, BF16, 2, KT, 128)

    for eng in (TE, VE, SE, GE):
        b.dS.wait(eng)
        b.dW.wait(eng)
        if eng is not VE:
            b.ev.wait(eng)

    for bb in range(NB):
        join_all(b, (TE, VE, SE, GE, SY))
        xb_bf = b.xs[:, bb].rearrange("p k t -> p (k t)").bitcast(
            BF16).rearrange("p (k t) -> p k t", k=KT, t=2 * NP)
        QC0, KC0 = 0, 2050

        b.dS.inc(SY.dma_start(out=dr["x_save"][bb],
                              in_=b.xs[:, bb, :, 1:NP]), 16)
        spill_done = b.dS.n

        b.ev.wait(SE)
        for tc in range(4):
            srcs = [b.xs[:, bb, k, 1 + tc * 512:1 + (tc + 1) * 512]
                    for k in range(KT)]
            dsts = [u[:, k, tc * 512:(tc + 1) * 512] for k in range(KT)]
            ln_chunk(b, srcs, dsts, 512, 0, lnb)
        u_done = b.ev.n

        b.mm.wait(VE)
        b.ev.inc(VE.memset(v65[:, :, :, HD:HD + 1], 1.0))
        ones_done = b.ev.n

        b.ev.wait(SE, u_done)
        pan_free = [(None, 0), (None, 0)]
        for w01, wnm in ((0, "WqT"), (1, "WkT")):
            for jt in range(KT):
                slot = (w01 * KT + jt) % 2
                g = pan_free[slot]
                if g[0] is not None:
                    g[0].wait(GE, g[1])
                b.dW.inc(GE.dma_start(out=qkpan[:, slot], in_=dr[wnm][jt]), 16)
                for tc in range(4):
                    bank = tc % 4
                    bank_wait(b, bank)
                    if tc == 0:
                        b.dW.wait(TE)
                        b.ev.wait(TE, u_done)
                        if w01 == 0 and jt == 0:
                            b.dS.wait(TE, spill_done)
                    for k in range(KT):
                        i = TE.matmul(ps[:, bank, :], qkpan[:, slot, k, :],
                                      u[:, k, tc * 512:(tc + 1) * 512],
                                      start=(k == 0), stop=(k == KT - 1))
                    b.mm.inc(i)
                pan_free[slot] = (b.mm, b.mm.n)
                b.mm.wait(SE)
                b.dS.wait(SE, spill_done)
                base = QC0 if w01 == 0 else KC0
                for tc in range(4):
                    i = SE.activation(
                        xb_bf[:, jt, base + tc * 512:base + (tc + 1) * 512],
                        ps[:, tc % 4, :], AF.Identity,
                        bias=b.ba_sb[:, w01 * KT + jt:w01 * KT + jt + 1])
                    b.es.inc(i)
                for bank in range(4):
                    b.bg[bank] = [(b.es, b.es.n)]
        vp_guard = (None, 0)
        for jc in range(4):
            if vp_guard[0] is not None:
                vp_guard[0].wait(GE, vp_guard[1])
            b.dW.inc(GE.dma_start(out=vpan[:, :], in_=dr["WvT"][jc]), 16)
            b.dW.wait(TE)
            for tt in range(16):
                bank = tt % 4
                bank_wait(b, bank)
                for k in range(KT):
                    TE.matmul(ps[:, bank, 0:256], u[:, k, tt * 128:(tt + 1) * 128],
                              vpan[:, k, :], start=(k == 0), stop=False)
                i = TE.matmul(ps[:, bank, 0:256], b.c_ones(0, 1, 128),
                              b.bvb[0:1, jc * 256:(jc + 1) * 256],
                              start=False, stop=True)
                b.mm.inc(i)
                b.mm.wait(SE)
                i = SE.copy(v65[:, tt, 4 * jc:4 * jc + 4, 0:HD],
                            ps[:, bank, 0:256])
                b.es.inc(i)
                b.bg[bank] = [(b.es, b.es.n)]
            vp_guard = (b.mm, b.mm.n)

        if DEBUG and bb == 0:
            b.ev.wait(SY, u_done)
            b.es.wait(SY)
            b.dS.inc(SY.dma_start(out=dr["d_u"], in_=u[:, :, :]), 16)
            b.dS.inc(SY.dma_start(
                out=dr["d_q"],
                in_=xb_bf[:, :, QC0:QC0 + N]), 16)
            b.dS.inc(SY.dma_start(
                out=dr["d_k"],
                in_=xb_bf[:, :, KC0:KC0 + N]), 16)
            b.dS.inc(SY.dma_start(out=dr["d_v"], in_=v65[:, :, :, :]), 16)
            dump_done = b.dS.n
            b.dS.wait(TE, dump_done)

        b.ev.wait(TE, ones_done)
        for hp in range(KT):
            for ncx in range(4):
                ncs = slice(ncx * 512, (ncx + 1) * 512)
                expc = {}
                def scores(mt):
                    pb = 2 * (mt % 2)
                    bank_wait(b, pb)
                    bank_wait(b, pb + 1)
                    TE.matmul(ps[:, pb, :],
                              xb_bf[0:64, hp, QC0 + mt * 128:QC0 + (mt + 1) * 128],
                              xb_bf[0:64, hp, KC0 + ncx * 512:KC0 + (ncx + 1) * 512],
                              start=True, stop=True)
                    i = TE.matmul(ps[:, pb + 1, :],
                                  xb_bf[64:128, hp, QC0 + mt * 128:QC0 + (mt + 1) * 128],
                                  xb_bf[64:128, hp, KC0 + ncx * 512:KC0 + (ncx + 1) * 512],
                                  start=True, stop=True)
                    b.mm.inc(i)
                    b.mm.wait(SE)
                    g = aslot_guard[mt % 2]
                    if g[0] is not None:
                        g[0].wait(SE, g[1])
                    i = SE.activation(abuf[:, mt % 2], ps[:, pb:pb + 2, :],
                                      AF.Exp, scale=1.0 / 32.0,
                                      bias=b.cbias[:, 2:3])
                    b.es.inc(i)
                    expc[mt] = b.es.n
                    b.bg[pb] = [(b.es, b.es.n)]
                    b.bg[pb + 1] = [(b.es, b.es.n)]

                def av(mt):
                    if mt == 0:
                        bank_wait(b, 4)
                        bank_wait(b, 5)
                    b.es.wait(TE, expc[mt])
                    TE.matmul(ps[0:65, 4, :], v65[:, mt, 2 * hp, :],
                              abuf[:, mt % 2, 0:512],
                              start=(mt == 0), stop=(mt == 15))
                    i = TE.matmul(ps[0:65, 5, :], v65[:, mt, 2 * hp + 1, :],
                                  abuf[:, mt % 2, 512:1024],
                                  start=(mt == 0), stop=(mt == 15))
                    b.mm.inc(i)
                    aslot_guard[mt % 2] = (b.mm, b.mm.n)

                aslot_guard = [(None, 0), (None, 0)]
                scores(0)
                for mt in range(1, 16):
                    scores(mt)
                    av(mt - 1)
                av(15)
                b.mm.wait(SE)
                for bk in (0, 1):
                    for sem, val in b.bg[bk]:
                        sem.wait(SE, val)
                    b.bg[bk] = []
                SE.activation(ps[64:65, 0, :], ps[64:65, 4, :], AF.Ln)
                SE.activation(ps[64:65, 1, :], ps[64:65, 5, :], AF.Ln)
                i = SE.activation(zinv_eo[0:1, :], ps[64:65, 0:2, :], AF.Exp,
                                  scale=-1.0)
                b.es.inc(i)
                b.bg[0] = [(b.es, b.es.n)]
                b.bg[1] = [(b.es, b.es.n)]
                bank_wait(b, 6)
                bank_wait(b, 7)
                b.es.wait(TE)
                TE.matmul(ps[0:64, 6, :], b.c_ones(64, 1, 64), zinv_e[0:1, :],
                          start=True, stop=True)
                i = TE.matmul(ps[0:64, 7, :], b.c_ones(64, 1, 64),
                              zinv_o[0:1, :], start=True, stop=True)
                b.mm.inc(i)
                b.mm.wait(VE)
                b.es.wait(VE)
                VE.tensor_copy(otmp_e[0:64, :], ps[0:64, 4, :])
                i = VE.tensor_copy(otmp_o[0:64, :], ps[0:64, 5, :])
                b.ev.inc(i)
                b.bg[5] = [(b.ev, b.ev.n)]
                VE.tensor_tensor(ocat[0:64, hp, ncs], otmp_e[0:64, :],
                                 ps[0:64, 6, :], ALU.mult)
                i = VE.tensor_tensor(abuf[0:64, 0, 0:512], otmp_o[0:64, :],
                                     ps[0:64, 7, :], ALU.mult)
                b.ev.inc(i)
                b.bg[6] = [(b.ev, b.ev.n)]
                b.bg[7] = [(b.ev, b.ev.n)]
                bank_wait(b, 4)
                b.ev.wait(TE)
                i = TE.matmul(ps[:, 4, :], b.c_SHIFT, abuf[0:64, 0, 0:512],
                              start=True, stop=True)
                b.mm.inc(i)
                b.mm.wait(VE)
                i = VE.tensor_copy(ocat[64:128, hp, ncs], ps[64:128, 4, :])
                b.ev.inc(i)
                b.bg[4] = [(b.ev, b.ev.n)]

        ocat_done = b.es.n
        b.es.wait(GE)
        b.ev.wait(GE)
        wo_guard = [(None, 0), (None, 0)]
        for ft in range(KT):
            slot = ft % 2
            g = wo_guard[slot]
            if g[0] is not None:
                g[0].wait(GE, g[1])
            b.dW.inc(GE.dma_start(out=wopan[:, slot], in_=dr["WoT"][ft]), 16)
            b.dW.wait(TE)
            if ft == 0:
                b.es.wait(TE, ocat_done)
            for tc in range(4):
                bank = tc % 4
                bank_wait(b, bank)
                for jt in range(KT):
                    i = TE.matmul(ps[:, bank, :], wopan[:, slot, jt, :],
                                  ocat[:, jt, tc * 512:(tc + 1) * 512],
                                  start=(jt == 0), stop=(jt == KT - 1))
                b.mm.inc(i)
            wo_guard[slot] = (b.mm, b.mm.n)
            b.mm.wait(SE)
            for tc in range(4):
                i = SE.activation(
                    b.xs[:, bb, ft, 1 + tc * 512:1 + (tc + 1) * 512],
                    ps[:, tc % 4, :], AF.Identity,
                    bias=b.ba_sb[:, 3 * KT + ft:3 * KT + ft + 1])
                b.es.inc(i)
                b.bg[tc % 4] = [(b.es, b.es.n)]
        b.es.wait(GE)
        b.dW.inc(GE.dma_start(out=b.xs[:, bb, :, 1:NP], in_=dr["x_save"][bb],
                              accum_op=ALU.add), 16)
        b.dW.wait(GE)


def emit_ffn6(b, inv):
    TE, VE, SE, GE, SY = b.TE, b.VE, b.SE, b.GE, b.SY
    ps, v = b.ps, b.view
    dr = b.dr
    sm = b.smalls

    join_all(b, (TE, VE, SE, GE, SY))
    wq_fl, w1_fl, w2_fl = dr["Wqkv"], dr["W1p"], dr["W2p"]
    for bb in range(NB):
        b.dS.inc(SY.dma_start(out=dr["x_save"][bb],
                              in_=b.xs[:, bb, :, 1:NP]), 16)

    for bb in range(NB):
        join_all(b, (TE, VE, SE, GE, SY))

        def rv(off, dtype, *shape, p0=0, np_=128):
            esz = 4 if dtype == F32 else 2
            nby = int(np.prod(shape)) * esz
            assert off + nby <= KT * NP * 4, (off, nby)
            ap = b.xs[p0:p0 + np_, 1 - bb].rearrange(
                "p k t -> p (k t)").bitcast(mybir.dt.uint8)
            ap = ap[:, off:off + nby].bitcast(dtype)
            if len(shape) > 1:
                names = " ".join(f"s{i}" for i in range(len(shape)))
                ap = ap.rearrange(f"p ({names}) -> p {names}",
                                  **{f"s{i}": s for i, s in enumerate(shape)})
            return ap

        zc = lambda k, lo, n: b.xs[:, bb, k, lo:lo + n]

        if bb == 1:
            b.dS.inc(SY.dma_start(out=b.xs[:, 1, :, 1:NP],
                                  in_=dr["x_save"][1]), 16)
        b.dS.wait(SE)
        b.dS.wait(TE)
        b.dS.wait(VE)
        VE.memset(b.xs[:, bb, :, 0:1], 0.0)
        lnb0 = mk_lnb(b, 0, 12288, 8192)
        for tc in range(4):
            srcs = [zc(k, 1 + tc * 512, 512) for k in range(KT)]
            ln_chunk(b, srcs, srcs, 512, 0, lnb0)

        y_bf = v(0, BF16, KT, 1024)
        ocat = v(0, BF16, KT, 1024)
        qf = v(16384, BF16, KT, 1024)
        kf = v(32768, BF16, KT, 1280)
        abufs = [[v(53248, BF16, 2, 384), v(54784, BF16, 2, 384)],
                 [v(56320, BF16, 2, 384), v(57856, BF16, 2, 384)]]
        otmps = [v(59392, BF16, 512), v(60416, BF16, 512)]
        zinv2s = [v(61440, BF16, 2, 256, p0=64, np_=1),
                  v(62464, BF16, 2, 256, p0=64, np_=1)]
        oddts = [v(63488, BF16, 512), v(64512, BF16, 512)]
        rope_bufs = (
            [v(65536, BF16, 512), v(66560, BF16, 512)],
            [v(67584, F32, 512, p0=64, np_=2),
             v(69632, F32, 512, p0=64, np_=2)],
            [v(71680, BF16, 512, p0=64, np_=2),
             v(72704, BF16, 512, p0=64, np_=2)],
            [v(53248, BF16, 512), v(54272, BF16, 512)],
            v(55296, BF16, 512),
            v(56320, BF16, 512),
        )
        b.la_ztmp = v(67584, F32, 512, p0=64, np_=1)
        m1t = v(55296, BF16, 512)
        m2t = v(56320, BF16, 512)
        at1 = v(53248, BF16, 512)
        v65 = rv(0, BF16, 10, H, HD + 1)
        qkvpan = rv(20800, BF16, 2, KT, 512)
        wo_sb = rv(37184, BF16, KT, 1024)
        tabs = rv(53568, BF16, 2, 4, 512)
        xn = v(0, BF16, KT, 2048)
        w1pan = v(32768, BF16, 2, 2, KT, 128)
        hout = v(40960, BF16, 2, 2048)
        ff_tmp = [v(49152 + i * 2048, F32, 512) for i in range(4)]
        lnb1 = mk_lnb(b, 45056, 53248, 57344, viewfn=rv)
        lnb1["t1"] = v(57344, F32, 512)
        lnb1["t2"] = v(59392, F32, 512)
        at2 = v(64512, BF16, 512)
        hin = [v(0, BF16, FT1, 512), v(22528, BF16, FT1, 512)]
        w2all = rv(0, BF16, KT, FT1, 128)

        b.g1tmp_guard = [(None, 0), (None, 0)]
        b.fftmp_guard = [(None, 0), (None, 0)]
        for lyr in range(FFN_DEPTH):
            b.es.wait(SY)
            b.ev.wait(SY)
            b.dS.inc(SY.dma_start(out=sm[:, :], in_=dr["small"][lyr]), 16)
            smalls_ld = b.dS.n

            for c in range(2):
                T0 = c * 1024
                b.ev.wait(GE)
                b.mm.wait(GE)
                for k in range(KT):
                    srcz = zc(k, 1 + T0, 1024) if k < 4 else zc(k, T0, 1024)
                    i = GE.tensor_copy(y_bf[:, k, :], srcz)
                b.ep.inc(i)
                for hc in range(2):
                    b.dW.inc(GE.dma_start(
                        out=tabs[:, hc],
                        in_=dr["xpos"][:, :, T0 + hc * 512:T0 + (hc + 1) * 512]
                        .rearrange("a p t -> p a t")), 16)
                rp = RopePipe(b, qf, kf, tabs, rope_bufs, qkvpan,
                              [wq_fl[lyr * 6 + jc] for jc in range(4)],
                              smalls_ld)
                rp.run(y_bf)
                for jc in (4, 5):
                    slot = jc % 2
                    b.mm.wait(GE)
                    b.dW.inc(GE.dma_start(out=qkvpan[:, slot],
                                          in_=wq_fl[lyr * 6 + jc],
                                          max_dma_last_dim=8192), 16)
                    b.dW.wait(TE)
                    for tt in range(8):
                        bank = tt % 2
                        bank_wait(b, bank)
                        for k in range(KT):
                            i = TE.matmul(
                                ps[:, bank, :],
                                y_bf[:, k, tt * 128:(tt + 1) * 128],
                                qkvpan[:, slot, k, :],
                                start=(k == 0), stop=(k == KT - 1))
                        b.mm.inc(i)
                        b.mm.wait(SE)
                        i = SE.copy(
                            v65[:, 2 + tt, (jc - 4) * 8:(jc - 4) * 8 + 8, 0:HD],
                            ps[:, bank, :])
                        b.es.inc(i)
                        b.bg[bank] = [(b.es, b.es.n)]
                b.es.wait(VE)
                b.ev.inc(VE.memset(v65[:, :, :, HD:HD + 1], 1.0))
                ones_done = b.ev.n

                b.ev.wait(TE, ones_done)
                units = [(hp, w) for hp in range(KT) for w in range(4)]
                prev = None
                for idx, (hp, w) in enumerate(units):
                    first = (c == 0 and w == 0)
                    mts = (2, 3) if first else (2, 0, 1, 3)
                    par = idx % 2
                    g = la_stage1(b, hp, w, mts, qf, kf, abufs[par])
                    if prev is not None:
                        la_stage2(b, *prev)
                    prev = (units[idx][0], units[idx][1], mts, par,
                            abufs[par], g, v65, zinv2s[par], otmps[par],
                            oddts[par], ocat)
                la_stage2(b, *prev)
                if c == 0:
                    b.dW.inc(GE.dma_start(out=wo_sb[:, :],
                                          in_=dr["Wo_f"][lyr],
                                          max_dma_last_dim=8192), 16)
                b.dW.wait(TE)
                for hc in range(2):
                    for ft in range(KT):
                        bank = (hc * KT + ft) % 4
                        bank_wait(b, bank)
                        for jt in range(KT):
                            i = TE.matmul(
                                ps[:, bank, :],
                                wo_sb[:, jt, ft * 128:(ft + 1) * 128],
                                ocat[:, jt, hc * 512:(hc + 1) * 512],
                                start=(jt == 0), stop=(jt == KT - 1))
                        b.mm.inc(i)
                        b.mm.wait(VE)
                        i = VE.tensor_tensor(
                            zc(ft, 1 + T0 + hc * 512, 512),
                            zc(ft, 1 + T0 + hc * 512, 512),
                            ps[:, bank, :], ALU.add)
                        b.ev.inc(i)
                        b.bg[bank] = [(b.ev, b.ev.n)]
                b.ev.wait(GE)
                b.mm.wait(GE)
                GE.tensor_copy(kf[:, :, 0:256], kf[:, :, 1024:1280])
                i = GE.tensor_copy(v65[:, 0:2, :, :], v65[:, 8:10, :, :])
                b.ep.inc(i)
                b.ep.wait(TE)
                b.ep.wait(SE)

            if FFN_ATTN_ONLY:
                continue
            b.mm.wait(VE)
            b.mm.wait(SE)
            for tc in range(4):
                srcs = [zc(k, 1 + tc * 512, 512) if k < 4
                        else zc(k, tc * 512, 512) for k in range(KT)]
                dsts = [xn[:, k, tc * 512:(tc + 1) * 512] for k in range(KT)]
                ln_chunk(b, srcs, dsts, 512, 0, lnb1,
                         gamma=sm[:, 2:10], beta=sm[:, 10:18])
            xn_done = b.ev.n
            b.mm.wait(GE)
            b.ev.wait(GE)
            w1_guard = [(None, 0), (None, 0)]
            ho_guard = [(None, 0), (None, 0)]
            for fp in range(FT1):
                slot = fp % 2
                g = w1_guard[slot]
                if g[0] is not None:
                    g[0].wait(GE, g[1])
                b.dW.inc(GE.dma_start(out=w1pan[:, slot],
                                      in_=w1_fl[lyr * FT1 + fp]), 16)
                w1_ld_n = b.dW.n
                if 2 <= fp < 2 + KT:
                    b.dW.inc(GE.dma_start(out=w2all[:, fp - 2],
                                          in_=w2_fl[lyr * KT + fp - 2],
                                          max_dma_last_dim=8192), 16)
                    if fp == 2 + KT - 1:
                        b.w2_ld = b.dW.n
                b.dW.wait(TE, w1_ld_n)
                if fp == 0:
                    b.ev.wait(TE, xn_done)
                g = ho_guard[slot]
                if g[0] is not None:
                    g[0].wait(VE, g[1])
                for ts in range(4):
                    ba = (ts % 2) * 2
                    bank_wait(b, ba)
                    bank_wait(b, ba + 1)
                    for k in range(KT):
                        TE.matmul(ps[:, ba, :], w1pan[:, slot, 0, k, :],
                                  xn[:, k, ts * 512:(ts + 1) * 512],
                                  start=(k == 0), stop=(k == KT - 1))
                    for k in range(KT):
                        i = TE.matmul(ps[:, ba + 1, :], w1pan[:, slot, 1, k, :],
                                      xn[:, k, ts * 512:(ts + 1) * 512],
                                      start=(k == 0), stop=(k == KT - 1))
                    b.mm.inc(i)
                    b.mm.wait(SE)
                    gt = m1t if ts % 2 == 0 else m2t
                    at = at1 if ts % 2 == 0 else at2
                    g = b.g1tmp_guard[ts % 2]
                    if g[0] is not None:
                        g[0].wait(SE, g[1])
                    SE.activation(gt, ps[:, ba + 1, :], AF.Gelu,
                                  bias=sm[:, 40 + fp:41 + fp])
                    i = SE.activation(at, ps[:, ba, :], AF.Identity,
                                      bias=sm[:, 18 + fp:19 + fp])
                    b.es.inc(i)
                    b.es.wait(VE)
                    i = VE.tensor_tensor(
                        hout[:, slot, ts * 512:(ts + 1) * 512], at, gt,
                        ALU.mult)
                    b.ev.inc(i)
                    b.g1tmp_guard[ts % 2] = (b.ev, b.ev.n)
                    b.bg[ba] = [(b.ev, b.ev.n)]
                    b.bg[ba + 1] = [(b.ev, b.ev.n)]
                w1_guard[slot] = (b.mm, b.mm.n)
                b.ev.wait(SY)
                b.dS.inc(SY.dma_start(out=dr["h_sc"][:, fp, :],
                                      in_=hout[:, slot, :]), 16)
                ho_guard[slot] = (b.dS, b.dS.n)
            w2_ld = b.w2_ld
            hin_guard = [(None, 0), (None, 0)]
            for ts in range(4):
                slot = ts % 2
                g = hin_guard[slot]
                if g[0] is not None:
                    g[0].wait(SY, g[1])
                b.dS.inc(SY.dma_start(out=hin[slot],
                                      in_=dr["h_sc"][:, :, ts * 512:(ts + 1) * 512]), 16)
                b.dS.wait(TE)
                b.dW.wait(TE, w2_ld)
                for ft in range(KT):
                    bank = 4 + (ft % 2)
                    bank_wait(b, bank)
                    for kt in range(FT1):
                        i = TE.matmul(ps[:, bank, :], w2all[:, ft, kt, :],
                                      hin[slot][:, kt, :],
                                      start=(kt == 0), stop=(kt == FT1 - 1))
                    b.mm.inc(i)
                    b.mm.wait(SE)
                    stt = ff_tmp[ft % 2]
                    g = b.fftmp_guard[ft % 2]
                    if g[0] is not None:
                        g[0].wait(SE, g[1])
                    i = SE.activation(stt, ps[:, bank, :], AF.Identity,
                                      bias=sm[:, 62 + ft:63 + ft])
                    b.es.inc(i)
                    b.es.wait(VE)
                    i = VE.tensor_tensor(zc(ft, 1 + ts * 512, 512),
                                         zc(ft, 1 + ts * 512, 512), stt,
                                         ALU.add)
                    b.ev.inc(i)
                    b.fftmp_guard[ft % 2] = (b.ev, b.ev.n)
                    b.bg[bank] = [(b.ev, b.ev.n)]
                hin_guard[slot] = (b.mm, b.mm.n)

        join_all(b, (GE, SY))
        if bb == 0:
            b.dW.inc(GE.dma_start(out=dr["x_save"][0],
                                  in_=b.xs[:, 0, :, 1:NP],
                                  accum_op=ALU.add), 16)
        else:
            b.dW.inc(GE.dma_start(out=b.xs[:, 1, :, 1:NP],
                                  in_=dr["x_save"][1], accum_op=ALU.add), 16)
            b.dW.wait(GE)
    join_all(b, (SY,))
    b.dS.inc(SY.dma_start(out=b.xs[:, 0, :, 1:NP], in_=dr["x_save"][0]), 16)
    b.dS.wait(SY)


class RopePipe:

    def __init__(self, b, qf, kf, tabs, bufs, qkvpan, wq_src, smalls_ld):
        self.b = b
        self.qf, self.kf, self.tabs = qf, kf, tabs
        self.sqt, self.r_f32, self.r_bf, self.rsb, self.m1, self.m2 = bufs
        self.qkvpan = qkvpan
        self.wq_src = wq_src
        self.smalls_ld = smalls_ld
        self.cnt = {}
        self.pan_guard = {}

    def unit(self, u):
        jc, rest = divmod(u, 8)
        jsub, hc = divmod(rest, 2)
        isq = jc < 2
        jt = (jc % 2) * 4 + jsub
        return jc, jsub, hc, isq, jt

    def dst(self, u):
        jc, jsub, hc, isq, jt = self.unit(u)
        if isq:
            return self.qf[:, jt, hc * 512:(hc + 1) * 512]
        return self.kf[:, jt, 256 + hc * 512:256 + (hc + 1) * 512]

    def s0(self, u):
        b = self.b
        TE, GE = b.TE, b.GE
        jc, jsub, hc, isq, jt = self.unit(u)
        if jsub == 0 and hc == 0:
            slot = jc % 2
            g = self.pan_guard.get(slot)
            if g is not None:
                g[0].wait(GE, g[1])
            b.dW.inc(GE.dma_start(out=self.qkvpan[:, slot],
                                  in_=self.wq_src[jc],
                                  max_dma_last_dim=8192), 16)
            self.cnt[("pan", jc)] = b.dW.n
        bank = u % 4
        bank_wait(b, bank)
        b.dW.wait(TE, self.cnt[("pan", jc)])
        b.ep.wait(TE)
        slot = jc % 2
        for k in range(KT):
            i = TE.matmul(b.ps[:, bank, :],
                          self.qkvpan[:, slot, k, jsub * 128:(jsub + 1) * 128],
                          self.ybf[:, k, hc * 512:(hc + 1) * 512],
                          start=(k == 0), stop=(k == KT - 1))
        b.mm.inc(i)
        self.cnt[("s0", u)] = b.mm.n
        if jsub == 3 and hc == 1:
            self.pan_guard[(jc + 1) % 2] = (b.mm, b.mm.n)
        SE = b.SE
        b.mm.wait(SE)
        b.ev.wait(SE)
        i = SE.activation(self.sqt[u % 2], b.ps[:, bank, :], AF.Square)
        b.es.inc(i)
        self.cnt[("sq", u)] = b.es.n

    def ss(self, u):
        b = self.b
        TE, SE, VE = b.TE, b.SE, b.VE
        bank_wait(b, 6)
        b.es.wait(TE, self.cnt[("sq", u)])
        i = TE.matmul(b.ps[64:66, 6, :], b.c_ones2, self.sqt[u % 2],
                      start=True, stop=True)
        b.mm.inc(i)
        b.mm.wait(SE)
        b.ev.wait(SE)
        SE.activation(self.r_f32[u % 2][0:2, :], b.ps[64:66, 6, :],
                      AF.Ln, bias=b.cbias[64:66, 5:6])
        i = SE.activation(self.r_bf[u % 2][0:2, :],
                          self.r_f32[u % 2][0:2, :], AF.Exp, scale=-0.5)
        b.es.inc(i)
        b.bg[6] = [(b.es, b.es.n)]
        self.cnt[("rc", u)] = b.es.n

    def bc(self, u):
        b = self.b
        TE, VE = b.TE, b.VE
        bank_wait(b, 7)
        b.es.wait(TE, self.cnt[("rc", u)])
        i = TE.matmul(b.ps[:, 7, :], b.c_sel2, self.r_bf[u % 2][0:2, :],
                      start=True, stop=True)
        b.mm.inc(i)
        b.mm.wait(VE)
        b.dS.wait(VE, self.smalls_ld)
        jc, jsub, hc, isq, jt = self.unit(u)
        col = 0 if isq else 1
        VE.tensor_scalar(self.rsb[u % 2], b.ps[:, 7, :],
                         b.smalls[:, col:col + 1], None, ALU.mult)
        i = VE.tensor_tensor(self.dst(u), b.ps[:, u % 4, :], self.rsb[u % 2],
                             ALU.mult)
        b.ev.inc(i)
        b.bg[7] = [(b.ev, b.ev.n)]
        b.bg[u % 4] = [(b.ev, b.ev.n)]
        self.cnt[("ev", u)] = b.ev.n

    def pm(self, u):
        b = self.b
        TE, VE = b.TE, b.VE
        bank_wait(b, 5)
        b.ev.wait(TE, self.cnt[("ev", u)])
        d = self.dst(u)
        i = TE.matmul(b.ps[:, 5, :], b.c_PH, d, start=True, stop=True)
        b.mm.inc(i)
        jc, jsub, hc, isq, jt = self.unit(u)
        ct = self.tabs[:, hc, 0 if isq else 2, :]
        st = self.tabs[:, hc, 1 if isq else 3, :]
        b.dW.wait(VE)
        VE.tensor_tensor(self.m1, d, ct, ALU.mult)
        b.mm.wait(VE)
        VE.tensor_tensor(self.m2, b.ps[:, 5, :], st, ALU.mult)
        i = VE.tensor_tensor(d, self.m1, self.m2, ALU.add)
        b.ev.inc(i)
        b.bg[5] = [(b.ev, b.ev.n)]

    def run(self, ybf):
        self.ybf = ybf
        NU = 32
        for u in range(NU):
            self.s0(u)
            if u >= 1:
                self.ss(u - 1)
            if u >= 2:
                self.bc(u - 2)
            if u >= 3:
                self.pm(u - 3)
        self.ss(NU - 1)
        self.bc(NU - 2)
        self.pm(NU - 3)
        self.bc(NU - 1)
        self.pm(NU - 2)
        self.pm(NU - 1)


def la_stage1(b, hp, w, mts, qf, kf, abuf):
    TE, VE, SE = b.TE, b.VE, b.SE
    ps = b.ps
    def sc_slot(mt):
        if mt == 0:
            return 0, 0, 128, 0, 128
        if mt == 1:
            return 0, 128, 384, 0, 256
        if mt == 2:
            return 1, 0, 256, 0, 256
        return 1, 256, 384, 128, 128
    for bk in range(4):
        bank_wait(b, bk)
    p0 = lambda h01: h01 * 64
    qs = lambda h01, n0, nw: qf[p0(h01):p0(h01) + 64, hp,
                                w * 256 + n0:w * 256 + n0 + nw]
    ks = lambda h01, mt: kf[p0(h01):p0(h01) + 64, hp,
                            w * 256 + mt * 128:w * 256 + (mt + 1) * 128]
    for mt in mts:
        sb, c0, c1, n0, nw = sc_slot(mt)
        TE.matmul(ps[:, sb, c0:c1], ks(0, mt), qs(0, n0, nw),
                  start=True, stop=True)
        i = TE.matmul(ps[:, 2 + sb, c0:c1], ks(1, mt), qs(1, n0, nw),
                      start=True, stop=True)
    b.mm.inc(i)
    b.mm.wait(SE)
    b.ev.wait(SE)
    sbs = (0, 1) if 0 in mts else (1,)
    for h01 in (0, 1):
        for sb in sbs:
            i = SE.activation(abuf[h01][:, sb, 0:384],
                              ps[:, 2 * h01 + sb, 0:384], AF.Exp, scale=8.0,
                              bias=b.cbias[:, 3:4])
    b.es.inc(i)
    for bk in range(4):
        b.bg[bk] = [(b.es, b.es.n)]
    b.es.wait(VE)
    for h01 in (0, 1):
        a = abuf[h01]
        if 0 in mts:
            VE.tensor_tensor(a[:, 0, 0:128], a[:, 0, 0:128], b.c_tria,
                             ALU.mult)
            VE.tensor_tensor(a[:, 0, 256:384], a[:, 0, 256:384], b.c_tria,
                             ALU.mult)
        VE.tensor_tensor(a[:, 1, 0:128], a[:, 1, 0:128], b.c_tric, ALU.mult)
        i = VE.tensor_tensor(a[:, 1, 256:384], a[:, 1, 256:384], b.c_tric,
                             ALU.mult)
    b.ev.inc(i)
    return (b.es.n, b.ev.n)


def la_stage2(b, hp, w, mts, par, abuf, guards, v65, zinv2, otmp, oddt, ocat):
    TE, VE, SE = b.TE, b.VE, b.SE
    ps = b.ps
    es_a, ev_a = guards
    amap = {0: (0, 0, 128, 0, 128), 1: (0, 128, 384, 0, 256),
            2: (1, 0, 256, 0, 256), 3: (1, 256, 384, 128, 128)}
    bank_wait(b, 4)
    bank_wait(b, 5)
    b.es.wait(TE, es_a)
    b.ev.wait(TE, ev_a)
    for h01 in (0, 1):
        for n_mt, mt in enumerate(mts):
            sb, c0, c1, n0, nw = amap[mt]
            i = TE.matmul(ps[0:65, 4 + h01, n0:n0 + nw],
                          v65[:, 2 * w + mt, 2 * hp + h01, :],
                          abuf[h01][:, sb, c0:c1],
                          start=(n_mt == 0), stop=(n_mt == len(mts) - 1),
                          skip_group_check=True)
    b.mm.inc(i)
    b.mm.wait(SE)
    zt = b.la_ztmp
    SE.activation(zt[0:1, 0:256], ps[64:65, 4, 0:256], AF.Ln)
    SE.activation(zt[0:1, 256:512], ps[64:65, 5, 0:256], AF.Ln)
    i = SE.activation(zinv2[0:1, :, :], zt[0:1, 0:512], AF.Exp, scale=-1.0)
    b.es.inc(i)
    bank_wait(b, 6)
    b.es.wait(TE)
    TE.matmul(ps[0:64, 6, 0:256], b.c_ones(64, 1, 64), zinv2[0:1, 0, :],
              start=True, stop=True)
    i = TE.matmul(ps[0:64, 6, 256:512], b.c_ones(64, 1, 64), zinv2[0:1, 1, :],
                  start=True, stop=True, skip_group_check=True)
    b.mm.inc(i)
    b.mm.wait(VE)
    b.es.wait(VE)
    VE.tensor_copy(otmp[0:64, 0:256], ps[0:64, 4, 0:256])
    i = VE.tensor_copy(otmp[0:64, 256:512], ps[0:64, 5, 0:256])
    b.ev.inc(i)
    b.bg[4] = [(b.ev, b.ev.n)]
    b.bg[5] = [(b.ev, b.ev.n)]
    VE.tensor_tensor(ocat[0:64, hp, w * 256:(w + 1) * 256], otmp[0:64, 0:256],
                     ps[0:64, 6, 0:256], ALU.mult)
    i = VE.tensor_tensor(oddt[0:64, 0:256], otmp[0:64, 256:512],
                         ps[0:64, 6, 256:512], ALU.mult)
    b.ev.inc(i)
    b.bg[6] = [(b.ev, b.ev.n)]
    bank_wait(b, 7)
    b.ev.wait(TE)
    i = TE.matmul(ps[:, 7, 0:256], b.c_SHIFT, oddt[0:64, 0:256],
                  start=True, stop=True)
    b.mm.inc(i)
    b.mm.wait(VE)
    i = VE.tensor_copy(ocat[64:128, hp, w * 256:(w + 1) * 256],
                       ps[64:128, 7, 0:256])
    b.ev.inc(i)
    b.bg[7] = [(b.ev, b.ev.n)]


def emit_attn_b(b):
    TE, VE, SE, GE, SY = b.TE, b.VE, b.SE, b.GE, b.SY
    ps, v = b.ps, b.view
    dr = b.dr

    join_all(b, (TE, VE, SE, GE, SY))

    b.dW.inc(GE.dma_start(out=b.pmk_b[:, :, :], in_=dr["pmk"]), 16)
    b.dW.inc(GE.dma_start(out=b.pmv_b[:, :, :], in_=dr["pmv"]), 16)

    u = v(0, BF16, KT, 512)
    qseg = v(8192, BF16, KT, 512)
    kseg = v(16384, BF16, KT, 512)
    v65b = v(24576, BF16, 4, H, HD + 1)
    ocat = v(32896, BF16, KT, 512)
    abuf = [v(41088, BF16, 512), v(42112, BF16, 512)]
    apm_e = v(43136, BF16, 512, p0=0, np_=16)
    apm_o = v(44160, BF16, 512, p0=0, np_=16)
    otmp_e = v(45184, BF16, 512)
    otmp_o = v(46208, BF16, 512)
    zinv_e = v(47232, BF16, 512, p0=64, np_=1)
    zinv_o = v(48256, BF16, 512, p0=64, np_=1)
    zinv_eo = v(47232, BF16, 1024, p0=64, np_=1)
    ztmpB = v(69760, F32, 1024, p0=64, np_=1)
    tabs_b = v(49280, BF16, 2, 512)
    m1t = v(51328, BF16, 512)
    m2t = v(52352, BF16, 512)
    wpan = v(53376, BF16, 2, KT, 512)
    lnb = mk_lnb(b, 24576, 28800, 41088)

    for bb in range(NB):
        for s in range(4):
            S0 = s * 512
            join_all(b, (GE, SY))
            srcs = [b.xs[:, bb, k, 1 + S0:1 + S0 + 512] for k in range(KT)]
            dsts = [u[:, k, :] for k in range(KT)]
            ln_chunk(b, srcs, dsts, 512, 1, lnb, gamma=b.gb_sb,
                     want_mean=False)
            u_done = b.ev.n
            b.ev.wait(GE)
            b.dW.inc(GE.dma_start(
                out=tabs_b,
                in_=dr["ropeb"][:, :, S0:S0 + 512].rearrange(
                    "a p t -> p a t")), 16)
            b.ev.wait(SE, u_done)
            for jc in range(6):
                slot = jc % 2
                if jc >= 2:
                    b.mm.wait(GE)
                b.dW.inc(GE.dma_start(out=wpan[:, slot],
                                      in_=dr["wqkvT"][jc],
                                      max_dma_last_dim=8192), 16)
                b.dW.wait(TE)
                b.ev.wait(TE, u_done)
                if jc < 4:
                    isq = jc < 2
                    dstt = qseg if isq else kseg
                    for jsub in range(4):
                        jt = (jc % 2) * 4 + jsub
                        bank = jsub % 2
                        bank_wait(b, bank)
                        for k in range(KT):
                            i = TE.matmul(
                                ps[:, bank, :],
                                wpan[:, slot, k, jsub * 128:(jsub + 1) * 128],
                                u[:, k, :], start=(k == 0), stop=(k == KT - 1))
                        b.mm.inc(i)
                        dsl = dstt[:, jt, :]
                        b.mm.wait(SE)
                        b.ev.wait(SE)
                        i = SE.copy(dsl, ps[:, bank, :])
                        b.es.inc(i)
                        b.bg[bank] = [(b.es, b.es.n)]
                        bank_wait(b, 7)
                        b.es.wait(TE)
                        i = TE.matmul(ps[:, 7, :], b.c_PI, dsl,
                                      start=True, stop=True)
                        b.mm.inc(i)
                        b.dW.wait(VE)
                        b.es.wait(VE)
                        VE.tensor_tensor(m1t, dsl, tabs_b[:, 0, :], ALU.mult)
                        b.mm.wait(VE)
                        VE.tensor_tensor(m2t, ps[:, 7, :], tabs_b[:, 1, :],
                                         ALU.mult)
                        i = VE.tensor_tensor(dsl, m1t, m2t, ALU.add)
                        b.ev.inc(i)
                        b.bg[7] = [(b.ev, b.ev.n)]
                else:
                    for tt in range(4):
                        bank = tt % 2
                        bank_wait(b, bank)
                        for k in range(KT):
                            i = TE.matmul(
                                ps[:, bank, :],
                                u[:, k, tt * 128:(tt + 1) * 128],
                                wpan[:, slot, k, :],
                                start=(k == 0), stop=(k == KT - 1))
                        b.mm.inc(i)
                        b.mm.wait(SE)
                        i = SE.copy(
                            v65b[:, tt, (jc - 4) * 8:(jc - 4) * 8 + 8, 0:HD],
                            ps[:, bank, :])
                        b.es.inc(i)
                        b.bg[bank] = [(b.es, b.es.n)]
            b.es.wait(VE)
            b.ev.inc(VE.memset(v65b[:, :, :, HD:HD + 1], 1.0))
            ones_done = b.ev.n
            b.ev.wait(TE, ones_done)
            b.dW.wait(TE)

            for hp in range(KT):
                for bk in (0, 1, 2, 3):
                    bank_wait(b, bk)
                qsl = lambda h01, c0, cw: qseg[h01 * 64:h01 * 64 + 64, hp,
                                               c0:c0 + cw]
                i = TE.matmul(ps[0:16, 0, :], b.pmk_b[0:64, hp, :],
                              qsl(0, 0, 512), start=True, stop=True)
                i = TE.matmul(ps[0:16, 1, :], b.pmk_b[64:128, hp, :],
                              qsl(1, 0, 512), start=True, stop=True)
                b.mm.inc(i)
                b.mm.wait(SE)
                b.ev.wait(SE)
                i = SE.activation(apm_e[0:16, :], ps[0:16, 0, :], AF.Exp,
                                  scale=0.125, bias=b.cbias[0:16, 4:5])
                i = SE.activation(apm_o[0:16, :], ps[0:16, 1, :], AF.Exp,
                                  scale=0.125, bias=b.cbias[0:16, 4:5])
                b.es.inc(i)
                b.bg[0] = [(b.es, b.es.n)]
                b.bg[1] = [(b.es, b.es.n)]
                b.es.wait(TE)
                TE.matmul(ps[0:65, 2, :], b.pmv_b[:, 2 * hp, :],
                          apm_e[0:16, :], start=True, stop=False,
                          skip_group_check=True)
                i = TE.matmul(ps[0:65, 3, :], b.pmv_b[:, 2 * hp + 1, :],
                              apm_o[0:16, :], start=True, stop=False,
                              skip_group_check=True)
                b.mm.inc(i)
                for kt in range(4):
                    cw = 512 - kt * 128
                    bank_wait(b, 0)
                    bank_wait(b, 1)
                    ksl = lambda h01: kseg[h01 * 64:h01 * 64 + 64, hp,
                                           kt * 128:(kt + 1) * 128]
                    i = TE.matmul(ps[:, 0, kt * 128:512], ksl(0),
                                  qsl(0, kt * 128, cw), start=True, stop=True)
                    i = TE.matmul(ps[:, 1, kt * 128:512], ksl(1),
                                  qsl(1, kt * 128, cw), start=True, stop=True)
                    b.mm.inc(i)
                    b.mm.wait(SE)
                    b.ev.wait(SE)
                    i = SE.activation(abuf[0][:, kt * 128:512],
                                      ps[:, 0, kt * 128:512], AF.Exp,
                                      scale=0.125, bias=b.cbias[:, 4:5])
                    i = SE.activation(abuf[1][:, kt * 128:512],
                                      ps[:, 1, kt * 128:512], AF.Exp,
                                      scale=0.125, bias=b.cbias[:, 4:5])
                    b.es.inc(i)
                    b.bg[0] = [(b.es, b.es.n)]
                    b.bg[1] = [(b.es, b.es.n)]
                    b.es.wait(VE)
                    VE.tensor_tensor(abuf[0][:, kt * 128:(kt + 1) * 128],
                                     abuf[0][:, kt * 128:(kt + 1) * 128],
                                     b.c_tric, ALU.mult)
                    i = VE.tensor_tensor(abuf[1][:, kt * 128:(kt + 1) * 128],
                                         abuf[1][:, kt * 128:(kt + 1) * 128],
                                         b.c_tric, ALU.mult)
                    b.ev.inc(i)
                    b.ev.wait(TE)
                    TE.matmul(ps[0:65, 2, kt * 128:512],
                              v65b[:, kt, 2 * hp, :], abuf[0][:, kt * 128:512],
                              start=False, stop=(kt == 3),
                              skip_group_check=True)
                    i = TE.matmul(ps[0:65, 3, kt * 128:512],
                                  v65b[:, kt, 2 * hp + 1, :],
                                  abuf[1][:, kt * 128:512],
                                  start=False, stop=(kt == 3),
                                  skip_group_check=True)
                    b.mm.inc(i)
                b.mm.wait(SE)
                SE.activation(ztmpB[0:1, 0:512], ps[64:65, 2, :], AF.Ln)
                SE.activation(ztmpB[0:1, 512:1024], ps[64:65, 3, :], AF.Ln)
                i = SE.activation(zinv_eo[0:1, :], ztmpB[0:1, :], AF.Exp,
                                  scale=-1.0)
                b.es.inc(i)
                bank_wait(b, 4)
                bank_wait(b, 5)
                b.es.wait(TE)
                TE.matmul(ps[0:64, 4, :], b.c_ones(64, 1, 64), zinv_e[0:1, :],
                          start=True, stop=True)
                i = TE.matmul(ps[0:64, 5, :], b.c_ones(64, 1, 64),
                              zinv_o[0:1, :], start=True, stop=True)
                b.mm.inc(i)
                b.mm.wait(VE)
                b.es.wait(VE)
                VE.tensor_copy(otmp_e[0:64, :], ps[0:64, 2, :])
                i = VE.tensor_copy(otmp_o[0:64, :], ps[0:64, 3, :])
                b.ev.inc(i)
                b.bg[2] = [(b.ev, b.ev.n)]
                b.bg[3] = [(b.ev, b.ev.n)]
                VE.tensor_tensor(ocat[0:64, hp, :], otmp_e[0:64, :],
                                 ps[0:64, 4, :], ALU.mult)
                i = VE.tensor_tensor(m1t[0:64, :], otmp_o[0:64, :],
                                     ps[0:64, 5, :], ALU.mult)
                b.ev.inc(i)
                b.bg[4] = [(b.ev, b.ev.n)]
                b.bg[5] = [(b.ev, b.ev.n)]
                bank_wait(b, 6)
                b.ev.wait(TE)
                i = TE.matmul(ps[:, 6, :], b.c_SHIFT, m1t[0:64, :],
                              start=True, stop=True)
                b.mm.inc(i)
                b.mm.wait(VE)
                i = VE.tensor_copy(ocat[64:128, hp, :], ps[64:128, 6, :])
                b.ev.inc(i)
                b.bg[6] = [(b.ev, b.ev.n)]

            ocat_done = b.es.n
            b.es.wait(GE)
            b.ev.wait(GE)
            wo_guard = [(None, 0), (None, 0)]
            for ft in range(KT):
                slot = ft % 2
                g = wo_guard[slot]
                if g[0] is not None:
                    g[0].wait(GE, g[1])
                b.dW.inc(GE.dma_start(out=wpan[:, slot, :, 0:128],
                                      in_=dr["woutT"][ft]), 16)
                b.dW.wait(TE)
                if ft == 0:
                    b.es.wait(TE, ocat_done)
                bank = ft % 4
                bank_wait(b, bank)
                for jt in range(KT):
                    i = TE.matmul(ps[:, bank, :], wpan[:, slot, jt, 0:128],
                                  ocat[:, jt, :], start=(jt == 0),
                                  stop=(jt == KT - 1))
                b.mm.inc(i)
                wo_guard[slot] = (b.mm, b.mm.n)
                b.mm.wait(VE)
                i = VE.tensor_tensor(b.xs[:, bb, ft, 1 + S0:1 + S0 + 512],
                                     b.xs[:, bb, ft, 1 + S0:1 + S0 + 512],
                                     ps[:, bank, :], ALU.add)
                b.ev.inc(i)
                b.bg[bank] = [(b.ev, b.ev.n)]


def kernel(**inputs):
    in_maps = host_prep(inputs)
    nc = build(UPTO)
    res = run_bass_kernel_spmd(nc, in_maps, core_ids=list(range(8)))
    out = np.zeros((16, N, D), np.float32)
    for c in range(8):
        xo = res.results[c]["x_out"]
        out[NB * c:NB * c + NB] = (
            xo.transpose(1, 0, 2, 3).reshape(D, NB, N).transpose(1, 2, 0))
    return out
